# version 02
# baseline (speedup 1.0000x reference)
import numpy as np
from contextlib import ExitStack

import concourse.bass as bass
import concourse.mybir as mybir
from concourse.bass_utils import run_bass_kernel_spmd

F32 = mybir.dt.float32
BF16 = mybir.dt.bfloat16
I32 = mybir.dt.int32
AF = mybir.ActivationFunctionType
ALU = mybir.AluOpType
AX = mybir.AxisListType

D = 2048
T = 2048
H = 32
N = 64
L = 64
NCH = T // L
NPT = D // 128
NHP = H // 2
E, G, GS, FI = 64, 8, 8, 1408
TL = T + 1

NVEC = 16


class VC:
    (x_r, x_w, x_k, x_v, x_a, x_g, w0, a0, k_k, k_a, r_k, gn_s, gn_o,
     negw0, neghalf, one) = range(NVEC)


NDV = 8


class DV:
    m_r, m_w, m_k, m_v, m_a, m_g, one_m_ka, spare = range(NDV)


MIX = {"r": (VC.x_r, DV.m_r), "w": (VC.x_w, DV.m_w), "k": (VC.x_k, DV.m_k),
       "v": (VC.x_v, DV.m_v), "a": (VC.x_a, DV.m_a), "g": (VC.x_g, DV.m_g)}

NMASK = 7
(MK_QAB_S, MK_QAB_D16, MK_STRICT_T22, MK_INCL_T22, MK_QI, MK_I2, MK_ONES) = range(NMASK)

DEBUG_TAPS = {}


class Sem:
    def __init__(self, ctx, nc, name):
        self.h = ctx.enter_context(nc.semaphore(name))
        self.v = 0


class Phase:
    ENGINES = ("tensor", "vector", "scalar", "sync", "gpsimd")

    def __init__(self, bld):
        self.bld = bld
        self.ops = {e: [] for e in self.ENGINES}

    def on(self, eng, fn):
        self.ops[eng].append(fn)

    def wait(self, eng, sem, val=None):
        v = sem.v if val is None else val
        if v > 0:
            self.on(eng, lambda e, h=sem.h, v=v: e.wait_ge(h, v))

    def inc(self, eng, sem, fn, n=1):
        sem.v += n
        self.on(eng, lambda e, f=fn, h=sem.h, n=n: f(e).then_inc(h, n))

    def dma(self, eng, sem, out, in_, **kw):
        sem.v += 16
        self.on(eng, lambda e, o=out, i=in_, h=sem.h, kw=kw:
                e.dma_start(out=o, in_=i, **kw).then_inc(h, 16))

    def play(self, block):
        for name in self.ENGINES:
            ops = self.ops[name]
            def body(e, ops=ops):
                for f in ops:
                    f(e)
            getattr(block, name)(body)

    def barrier(self):
        s = self.bld.sems["ph"]
        for nm in ["dli", "dlw", "dlo"]:
            self.wait("sync", self.bld.sems[nm])
            self.wait("gpsimd", self.bld.sems[nm])
        for eng in self.ENGINES:
            self.inc(eng, s, lambda e: e.nop())
        tgt = s.v
        for eng in self.ENGINES:
            self.wait(eng, s, tgt)


class Builder:
    def __init__(self, debug_taps=None, phb_level=9, phc_level=9, phd_level=9):
        self.nc = bass.Bass("TRN2", target_bir_lowering=False)
        self.taps_spec = dict(debug_taps or {})
        self.taps = {}
        self.phb_level = phb_level
        self.phc_level = phc_level
        self.phd_level = phd_level
        self._wconv_units = []
        self._wconv_i = 0

    def declare_dram(self):
        nc = self.nc
        dt = nc.dram_tensor
        i = lambda n, s, d=F32: dt(n, s, d, kind="ExternalInput").ap()
        self.xm = i("xm", [TL, D]); self.xp = i("xp", [TL, D])
        self.wr = i("wr", [D, D]); self.wk = i("wk", [D, D])
        self.wv = i("wv", [D, D]); self.wo = i("wo", [D, D])
        self.w1 = i("w1", [D, 64]); self.w2 = i("w2", [64, D])
        self.a1 = i("a1", [D, 64]); self.a2 = i("a2", [64, D])
        self.g1 = i("g1", [D, 128]); self.g2 = i("g2", [128, D])
        self.gwT = i("gwT", [D, E])
        self.muskT = i("muskT", [G, GS, D])
        self.gpw = i("gpw", [G, D, FI]); self.upw = i("upw", [G, D, FI])
        self.dpw = i("dpw", [G, FI, D])
        self.invm = i("invm", [1, 2 * E], I32)
        self.diota = i("iota", [128, 1])
        self.vecs = i("vecs", [D, NVEC])
        self.dident32 = i("ident32", [128, 128])
        self.didentbf = i("identbf", [128, 128], BF16)
        self.dmasks = i("dmasks", [128, 128 * NMASK])
        self.dmasksbf = i("dmasksbf", [128, 128 * NMASK], BF16)
        self.dsegm = i("segm", [128, T], BF16)
        self.dhind = i("hind", [NPT, 128, H], BF16)
        self.deind = i("eind", [NPT, H, 128], BF16)
        self.dgind = i("gind", [E, G])
        self.dgexp = i("gexp", [G, E])
        self.dones = i("ones1", [1, 128])

        self.out = dt("out", [T, D], F32, kind="ExternalOutput").ap()
        for name, (shape, dtp) in self.taps_spec.items():
            self.taps[name] = dt("tap_" + name, shape, dtp, kind="ExternalOutput").ap()

        s = lambda n, sh, d=BF16: dt(n, sh, d).ap()
        self.sc = {}
        for nm in ["r", "lw", "k2", "a", "b", "g", "kraw", "ag"]:
            self.sc[nm] = s("sc_" + nm, [D, T])
        for nm in ["lw", "k2", "a", "b", "kraw", "ag"]:
            self.sc["p" + nm] = s("sc_p" + nm, [D, T])
        self.sc["v"] = s("sc_v", [NHP, T, 128])
        self.sc["pv"] = s("sc_pv", [NHP, T, 128])
        self.sc["x32"] = s("sc_x32", [16, 128, D], F32)
        self.sc["y"] = s("sc_y", [D, T])
        self.sc["beta"] = s("sc_beta", [H, T], F32)
        self.sc["xmid"] = s("sc_xmid", [D, T], F32)
        self.sc["ln2"] = s("sc_ln2", [2, T], F32)
        self.sc["x2"] = s("sc_x2", [D, T])
        self.sc["gpb"] = s("sc_gpb", [G, D, FI])
        self.sc["upb"] = s("sc_upb", [G, D, FI])
        self.sc["dpb"] = s("sc_dpb", [G, FI, D])

    def build(self):
        self.declare_dram()
        self._wconv_units = self.wconv_units()
        nc = self.nc
        with ExitStack() as ctx:
            en = ctx.enter_context
            sb = lambda n, sh, d: en(nc.sbuf_tensor(n, sh, d))
            self.vec = sb("vecS", [128, NPT * NVEC], F32)
            self.dvec = sb("dvecS", [128, NPT * NDV], F32)
            self.id32 = sb("id32S", [128, 128], F32)
            self.idbf = sb("idbfS", [128, 128], BF16)
            self.masks = sb("masksS", [128, 128 * NMASK], F32)
            self.masksbf = self.masks
            self.hind = sb("hindS", [128, NPT * H], BF16)
            self.eind = sb("eindS", [H, NPT * 128], BF16)
            self.gind = sb("gindS", [E, G], F32)
            self.gexp = sb("gexpS", [G, E], F32)
            self.ones1 = sb("ones1S", [1, 128], F32)
            self.onesbf = sb("onesbfS", [1, 128], BF16)
            self.onesc = sb("onescS", [128, 1], BF16)
            self.cst = sb("cstS", [128, 8], F32)
            self.bcol = sb("bcolS", [128, NPT], BF16)
            self.clb = sb("clbS", [64, 2 * NCH], F32)
            self.pmat = sb("pmatS", [E, E], BF16)
            self.gnv = sb("gnvS", [64, 64], F32)
            self.iotac = sb("iotaS", [128, 1], F32)
            self.invs = sb("invsS", [1, 2 * E], I32)
            self.onescb = sb("onescbS", [128, 1], BF16)
            self.state = sb("stateS", [64, NHP * 128], F32)
            self.statebf = sb("statebfS", [64, NHP * 128], BF16)
            self.xln = sb("xlnS", [128, NPT * TL], BF16)
            self.mx = sb("mxS", [128, NPT * T], BF16)
            self.wbig = self.mx
            self.wsm = sb("wsmS", [128, D], BF16)
            self.stg = sb("stgS", [128, D], BF16)
            self.stg2 = sb("stg2S", [128, D], BF16)
            self.stg3 = sb("stg3S", [128, D], BF16)
            self.f32a = sb("f32aS", [128, D], F32)
            self.f32b = sb("f32bS", [128, D], F32)
            self.smallb = sb("smallbS", [64, T], BF16)
            self.segf = sb("segfS", [64, 1024], F32)
            self.lwf = sb("lwfS", [64, 1024], F32)
            self.invnb = self.smallb
            self.wq = sb("wqS", [128, 16 * 256], BF16)
            self.psF = en(nc.psum_tensor("psF", [128, 3584], F32))
            self.psH = en(nc.psum_tensor("psH", [128, 1024], BF16))
            self.sems = {}
            for s_ in ["dli", "dlw", "dlo", "ve", "te", "ac", "gp", "ph"]:
                self.sems[s_] = Sem(ctx, nc, s_)
            self.block = en(nc.Block())
            ph = Phase(self)
            self.emit(ph)
            ph.play(self.block)

    def bank(self, b, rows=128, cols=512):
        return self.psF[0:rows, 512 * b: 512 * b + cols]

    def mask(self, i, bf=False):
        m = self.masksbf if bf else self.masks
        return m[:, i * 128:(i + 1) * 128]

    def vcol(self, pt, col):
        return self.vec[:, pt * NVEC + col: pt * NVEC + col + 1]

    def dcol(self, pt, col):
        return self.dvec[:, pt * NDV + col: pt * NDV + col + 1]

    def xln_t(self, pt, c0, w):
        return self.xln[:].rearrange("p (n c) -> p n c", c=TL)[:, pt, c0:c0 + w]

    def mx_t(self, pt, c0, w):
        return self.mx[:].rearrange("p (n c) -> p n c", c=T)[:, pt, c0:c0 + w]

    def emit(self, ph):
        self.emit_consts(ph)
        ph.barrier()
        self.emit_phaseA(ph, prefix=True)
        ph.barrier()
        self.emit_phaseB(ph, prefix=True)
        ph.barrier()
        self.emit_phaseA(ph, prefix=False)
        ph.barrier()
        self.emit_phaseB(ph, prefix=False)
        ph.barrier()
        if self.phc_level >= 1:
            self.emit_phaseC(ph)
            ph.barrier()
        if self.phd_level >= 1 and self.phc_level >= 1:
            self.emit_phaseD(ph)
            ph.barrier()
        else:
            self.emit_stub_out(ph)
        self.emit_taps(ph)

    def emit_consts(self, ph):
        dli, ve = self.sems["dli"], self.sems["ve"]
        for o, i in [
            (self.id32[:], self.dident32), (self.idbf[:], self.didentbf),
            (self.masks[:], self.dmasks),
            (self.gind[:], self.dgind), (self.gexp[:], self.dgexp),
            (self.ones1[:], self.dones),
            (self.iotac[:], self.diota), (self.invs[:], self.invm),
        ]:
            ph.dma("sync", dli, o, i)
        ph.dma("sync", dli, self.gnv[:, 0:32],
               self.vecs[:, VC.gn_s:VC.gn_s + 1].rearrange("(h n) o -> n (h o)", n=64),
               allow_slow_non_contiguous=True)
        ph.dma("sync", dli, self.gnv[:, 32:64],
               self.vecs[:, VC.gn_o:VC.gn_o + 1].rearrange("(h n) o -> n (h o)", n=64),
               allow_slow_non_contiguous=True)
        for pt in range(NPT):
            ph.dma("sync", dli, self.vec[:, pt * NVEC:(pt + 1) * NVEC],
                   self.vecs[pt * 128:(pt + 1) * 128, :])
            ph.dma("sync", dli, self.hind[:, pt * H:(pt + 1) * H], self.dhind[pt])
            ph.dma("sync", dli, self.eind[:, pt * 128:(pt + 1) * 128], self.deind[pt])
        ph.wait("vector", dli)

        def derive(v):
            for pt in range(NPT):
                for name, (vc_col, dv_col) in MIX.items():
                    v.tensor_scalar(self.dcol(pt, dv_col), self.vcol(pt, vc_col),
                                    -1.0, 1.0, ALU.mult, ALU.add)
                v.tensor_scalar(self.dcol(pt, DV.one_m_ka), self.vcol(pt, VC.k_a),
                                -1.0, 1.0, ALU.mult, ALU.add)
            v.tensor_copy(self.onesbf[:], self.ones1[:])
            v.memset(self.onesc[:], 1.0)
            v.memset(self.onescb[:], 1.0)
            v.memset(self.segf[:, :], 1.0)
            v.memset(self.segf[:, :].rearrange("p (n c) -> p n c", c=L)[:, :, 0], 0.0)
            v.memset(self.cst[:, 0:1], -0.5)
            v.memset(self.cst[:, 1:2], -1.0)
            v.memset(self.cst[:, 2:3], 1.0 / D)
            v.memset(self.cst[:, 3:4], 1e-5)
            v.memset(self.cst[:, 4:5], 1e-20)
            v.memset(self.state[:], 0.0)
            v.memset(self.statebf[:], 0.0)
            return v.tensor_copy(self.f32a[0:1, 0:E],
                                 self.invs[:, :].rearrange("p (n c) -> p n c", c=2)[:, :, 0])
        ph.inc("vector", ve, derive)

        te = self.sems["te"]
        ph.wait("tensor", ve)

        def pmm(t):
            return t.matmul(self.psF[0:E, 0:E], self.ones1[0:1, 0:E],
                            self.f32a[0:1, 0:E], start=True, stop=True)
        ph.inc("tensor", te, pmm)
        ph.wait("vector", te)

        def pbuild(v):
            return v.tensor_scalar(self.pmat[:, :], self.psF[0:E, 0:E],
                                   self.iotac[0:E, 0:1], None, ALU.is_equal)
        ph.inc("vector", ve, pbuild)

    def emit_stub_out(self, ph):
        ve, dlo = self.sems["ve"], self.sems["dlo"]

        def z(v):
            return v.memset(self.f32a[:], 0.0)
        ph.inc("vector", ve, z)
        ph.wait("sync", ve)
        for g in range(16):
            ph.dma("sync", dlo, self.out[g * 128:(g + 1) * 128, :], self.f32a[:])
        ph.wait("sync", dlo)

    def emit_taps(self, ph):
        dlo = self.sems["dlo"]
        for name, ap in self.taps.items():
            if name == "stateF":
                ph.dma("sync", dlo, ap, self.state[:])
                continue
            if name == "softT":
                ph.dma("sync", dlo, ap, self.wsm[0:64, 0:2048])
                continue
            if name == "sgT":
                ph.dma("sync", dlo, ap, self.state[:].bitcast(BF16)[0:G, 0:2048])
                continue
            if name in self.sc:
                ph.dma("sync", dlo, ap, self.sc[name])
        ph.wait("sync", dlo)

    def ctile(self, mt):
        return self.xln[:, mt * TL: mt * TL + T]

    def mxf(self):
        return self.mx[:].bitcast(F32)

    def emit_phaseC(self, ph):
        dli, dlw, dlo = self.sems["dli"], self.sems["dlw"], self.sems["dlo"]
        ve, te, ac = self.sems["ve"], self.sems["te"], self.sems["ac"]
        sc = self.sc

        for kt in range(NPT):
            ph.wait("sync", ve)
            ph.dma("sync", dli, self.stg[:], sc["y"][kt * 128:(kt + 1) * 128, :])
            ph.dma("sync", dli, self.stg2[:], sc["g"][kt * 128:(kt + 1) * 128, :])
            ph.wait("vector", dli)

            def om(v, kt=kt):
                return v.tensor_tensor(self.mx[:, kt * T:(kt + 1) * T],
                                       self.stg[:], self.stg2[:], ALU.mult)
            ph.inc("vector", ve, om)

        for mt in range(16):
            base = (mt % 2) * 16
            ph.wait("gpsimd", te)
            for kt in range(16):
                ph.dma("gpsimd", dlw,
                       self.wq[:, (base + kt) * 128:(base + kt + 1) * 128],
                       self.wo[kt * 128:(kt + 1) * 128, mt * 128:(mt + 1) * 128])
            ph.wait("sync", ve)
            ph.dma("sync", dli,
                   self.f32a[:].rearrange("p (g c) -> p g c", c=128),
                   sc["x32"][:, :, mt * 128:(mt + 1) * 128].rearrange(
                       "g r t -> r g t"))
            ph.wait("tensor", dlw)
            ph.wait("tensor", ve)
            ph.wait("tensor", dlo)

            def mmo(t, base=base):
                ins = None
                for kt in range(16):
                    for fq in range(4):
                        ins = t.matmul(
                            self.psF[:, fq * 512:(fq + 1) * 512],
                            self.wq[:, (base + kt) * 128:(base + kt + 1) * 128],
                            self.mx[:, kt * T + fq * 512:kt * T + (fq + 1) * 512],
                            start=(kt == 0), stop=(kt == 15))
                return ins
            ph.inc("tensor", te, mmo)
            ph.wait("vector", te)
            ph.wait("vector", dli)
            ph.wait("vector", dlo)

            def evc(v, mt=mt):
                v.tensor_tensor(self.f32b[:], self.psF[:, 0:2048], self.f32a[:],
                                ALU.add)
                return v.tensor_copy(self.ctile(mt), self.f32b[:])
            ph.inc("vector", ve, evc)
            ph.wait("sync", ve)
            ph.dma("sync", dlo, sc["xmid"][mt * 128:(mt + 1) * 128, :], self.f32b[:])

        mxf = self.mxf()
        for ci in range(4):
            cs = slice(ci * 512, (ci + 1) * 512)
            ph.wait("tensor", ve)

            def mmx(t, cs=cs):
                ins = None
                for mt in range(16):
                    ins = t.matmul(self.psF[0:1, 2048:2560], self.onescb[:],
                                   self.ctile(mt)[:, cs],
                                   start=(mt == 0), stop=(mt == 15))
                return ins
            ph.inc("tensor", te, mmx)
            for bq in range(4):
                ph.wait("scalar", te)

                def sq(s, bq=bq, cs=cs):
                    ins = None
                    for j in range(4):
                        ins = s.activation(self.stg3[:, j * 512:(j + 1) * 512],
                                           self.ctile(bq * 4 + j)[:, cs], AF.Square)
                    return ins
                ph.inc("scalar", ac, sq)
                ph.wait("tensor", ac)

                def mm2(t, bq=bq):
                    ins = None
                    for j in range(4):
                        ins = t.matmul(self.psF[0:1, 2560:3072], self.onescb[:],
                                       self.stg3[:, j * 512:(j + 1) * 512],
                                       start=(bq == 0 and j == 0),
                                       stop=(bq == 3 and j == 3))
                    return ins
                ph.inc("tensor", te, mm2)
            ph.wait("vector", te)

            def lnrow(v, ci=ci):
                v.tensor_scalar(mxf[0:1, ci * 512:(ci + 1) * 512],
                                self.psF[0:1, 2048:2560], self.cst[0:1, 2:3],
                                None, ALU.mult)
                v.tensor_scalar(mxf[0:1, 6144:6656],
                                self.psF[0:1, 2560:3072], self.cst[0:1, 2:3],
                                None, ALU.mult)
                v.tensor_tensor(mxf[0:1, 6656:7168],
                                mxf[0:1, ci * 512:(ci + 1) * 512],
                                mxf[0:1, ci * 512:(ci + 1) * 512], ALU.mult)
                return v.tensor_sub(mxf[0:1, 7168:7680], mxf[0:1, 6144:6656],
                                    mxf[0:1, 6656:7168])
            ph.inc("vector", ve, lnrow)
            ph.wait("scalar", ve)

            def rstd(s, ci=ci):
                s.activation(mxf[0:1, 6144:6656], mxf[0:1, 7168:7680], AF.Ln,
                             bias=self.cst[0:1, 3:4])
                return s.activation(mxf[0:1, 2048 + ci * 512:2048 + (ci + 1) * 512],
                                    mxf[0:1, 6144:6656], AF.Exp,
                                    scale=self.cst[0:1, 0:1])
            ph.inc("scalar", ac, rstd)

        ph.wait("tensor", ac)

        def mmub(t):
            ins = None
            for fq in range(4):
                ins = t.matmul(self.psF[:, fq * 512:(fq + 1) * 512],
                               self.ones1[0:1, :], mxf[0:1, fq * 512:(fq + 1) * 512],
                               start=True, stop=True)
            return ins
        ph.inc("tensor", te, mmub)
        ph.wait("vector", te)

        def evmu(v):
            return v.tensor_copy(mxf[:, 2048 * 2:2048 * 3], self.psF[:, 0:2048])
        ph.inc("vector", ve, evmu)
        ph.wait("tensor", ve)

        def mmrs(t):
            ins = None
            for fq in range(4):
                ins = t.matmul(self.psF[:, fq * 512:(fq + 1) * 512],
                               self.ones1[0:1, :],
                               mxf[0:1, 2048 + fq * 512:2048 + (fq + 1) * 512],
                               start=True, stop=True)
            return ins
        ph.inc("tensor", te, mmrs)
        ph.wait("vector", te)

        def evrs(v):
            return v.tensor_copy(mxf[:, 2048 * 3:2048 * 4], self.psF[:, 0:2048])
        ph.inc("vector", ve, evrs)

        for mt in range(16):
            ph.wait("vector", dlo)

            def x2t(v, mt=mt):
                v.tensor_sub(self.stg[:], self.ctile(mt), mxf[:, 4096:6144])
                return v.tensor_tensor(self.ctile(mt), self.stg[:],
                                       mxf[:, 6144:8192], ALU.mult)
            ph.inc("vector", ve, x2t)
            ph.wait("sync", ve)
            ph.dma("sync", dlo, sc["x2"][mt * 128:(mt + 1) * 128, :], self.ctile(mt))

        self._gating(ph)

    def dve_fence(self, ph):
        ve, ac = self.sems["ve"], self.sems["ac"]
        ph.wait("scalar", ve)
        ph.inc("scalar", ac, lambda s: s.activation(self.cst[0:1, 7:8],
                                                    self.cst[0:1, 0:1],
                                                    AF.Identity))
        ph.wait("vector", ac)

    def _gating(self, ph):
        dli, dlw = self.sems["dli"], self.sems["dlw"]
        ve, te, ac = self.sems["ve"], self.sems["te"], self.sems["ac"]
        smf = self.smallb[:].bitcast(F32)
        wsmbf = self.wsm[:].bitcast(BF16)
        for kt in range(16):
            ph.dma("gpsimd", dlw, self.wq[:, kt * 64:(kt + 1) * 64],
                   self.gwT[kt * 128:(kt + 1) * 128, :])
        ph.wait("tensor", dlw)
        ph.wait("tensor", ve)

        for fq in range(4):
            ph.wait("tensor", ve)

            def mlgq(t, fq=fq):
                ins = None
                for kt in range(16):
                    ins = t.matmul(self.psF[0:E, 0:512],
                                   self.wq[:, kt * 64:(kt + 1) * 64],
                                   self.ctile(kt)[:, fq * 512:(fq + 1) * 512],
                                   start=(kt == 0), stop=(kt == 15))
                return ins
            ph.inc("tensor", te, mlgq)
            ph.wait("vector", te)

            def evlg(v, fq=fq):
                return v.tensor_copy(self.stg[0:E, fq * 512:(fq + 1) * 512],
                                     self.psF[0:E, 0:512])
            ph.inc("vector", ve, evlg)

        for grp in range(2):
            ph.wait("tensor", ve)

            def trl(t, grp=grp):
                ins = None
                for k in range(16):
                    cc = grp * 16 + k
                    ins = t.transpose(self.psH[0:64, k * 64:(k + 1) * 64],
                                      self.stg[0:E, cc * 64:(cc + 1) * 64],
                                      self.idbf[0:64, 0:64])
                return ins
            ph.inc("tensor", te, trl)
            ph.wait("vector", te)

            def evtl(v, grp=grp):
                return v.tensor_copy(self.stg2[0:64, grp * 1024:(grp + 1) * 1024],
                                     self.psH[0:64, 0:1024])
            ph.inc("vector", ve, evtl)

        def v3t(ap):
            return ap.rearrange("p (n c) -> p n c", c=64)

        ph.wait("vector", ve)

        def smax1(v):
            return v.reduce_max(smf[:, 0:32].unsqueeze(2),
                                v3t(self.stg2[0:64, 0:2048]), axis=AX.X)
        ph.inc("vector", ve, smax1)
        self.dve_fence(ph)

        def smsub(v):
            return v.tensor_tensor(
                v3t(self.stg[0:64, 0:2048]), v3t(self.stg2[0:64, 0:2048]),
                smf[:, 0:32].unsqueeze(2).broadcast_to([64, 32, 64]), ALU.subtract)
        ph.inc("vector", ve, smsub)
        ph.wait("scalar", ve)

        def smexp(s):
            return s.activation(self.stg3[0:64, 0:2048], self.stg[0:64, 0:2048],
                                AF.Exp)
        ph.inc("scalar", ac, smexp)
        ph.wait("vector", ac)

        def smsum(v):
            return v.reduce_sum(smf[:, 32:64].unsqueeze(2),
                                v3t(self.stg3[0:64, 0:2048]), axis=AX.X)
        ph.inc("vector", ve, smsum)
        self.dve_fence(ph)

        def smrcp(v):
            return v.reciprocal(smf[:, 64:96], smf[:, 32:64])
        ph.inc("vector", ve, smrcp)
        self.dve_fence(ph)

        def smmul(v):
            return v.tensor_tensor(
                v3t(self.stg[0:64, 0:2048]), v3t(self.stg3[0:64, 0:2048]),
                smf[:, 64:96].unsqueeze(2).broadcast_to([64, 32, 64]), ALU.mult)
        ph.inc("vector", ve, smmul)
        self.dve_fence(ph)

        def top8(v):
            ins = None
            for cc in range(NCH):
                ins = v.max(out=self.stg2[0:64, cc * 8:(cc + 1) * 8],
                            in_=self.stg[0:64, cc * 64:(cc + 1) * 64])
            return ins
        ph.inc("vector", ve, top8)
        self.dve_fence(ph)

        def thrx(v):
            return v.tensor_copy(smf[:, 96:128],
                                 self.stg2[0:64, 0:256].rearrange(
                                     "p (n c) -> p n c", c=8)[:, :, 7])
        ph.inc("vector", ve, thrx)
        self.dve_fence(ph)

        def wemask(v):
            v.tensor_tensor(v3t(self.stg3[0:64, 0:2048]),
                            v3t(self.stg[0:64, 0:2048]),
                            smf[:, 96:128].unsqueeze(2).broadcast_to([64, 32, 64]),
                            ALU.is_ge)
            return v.tensor_tensor(self.stg2[0:64, 0:2048], self.stg[0:64, 0:2048],
                                   self.stg3[0:64, 0:2048], ALU.mult)
        ph.inc("vector", ve, wemask)

        for grp in range(2):
            ph.wait("tensor", ve)

            def trw(t, grp=grp):
                ins = None
                for k in range(16):
                    cc = grp * 16 + k
                    ins = t.transpose(self.psH[0:64, k * 64:(k + 1) * 64],
                                      self.stg2[0:64, cc * 64:(cc + 1) * 64],
                                      self.idbf[0:64, 0:64])
                return ins
            ph.inc("tensor", te, trw)
            ph.wait("vector", te)

            def evtw(v, grp=grp):
                return v.tensor_copy(self.stg[0:64, grp * 1024:(grp + 1) * 1024],
                                     self.psH[0:64, 0:1024])
            ph.inc("vector", ve, evtw)

        ph.wait("tensor", ve)

        def mfl(t):
            ins = None
            for fq in range(4):
                ins = t.matmul(self.psF[0:E, 512:1024], self.pmat[:, :],
                               self.stg[0:64, fq * 512:(fq + 1) * 512],
                               start=True, stop=True) if fq == 0 else                     t.matmul(self.psF[0:E, 512 + 0:1024][:, 0:512], self.pmat[:, :],
                             self.stg[0:64, fq * 512:(fq + 1) * 512],
                             start=True, stop=True)
                ins = ins
            return ins
        for fq in range(4):
            ph.wait("tensor", ve)

            def mflq(t, fq=fq):
                return t.matmul(self.psF[0:E, 512:1024], self.pmat[:, :],
                                self.stg[0:64, fq * 512:(fq + 1) * 512],
                                start=True, stop=True)
            ph.inc("tensor", te, mflq)
            ph.wait("vector", te)

            def evfl(v, fq=fq):
                return v.tensor_copy(self.stg2[0:64, fq * 512:(fq + 1) * 512],
                                     self.psF[0:E, 512:1024])
            ph.inc("vector", ve, evfl)

        ph.wait("vector", ve)

        def gbf2(v):
            v.tensor_copy(self.wq[0:64, 1024:1032], self.gind[:, :])
            return v.tensor_scalar(self.stg3[0:64, 0:2048],
                                   self.stg2[0:64, 0:2048], 0.0, None, ALU.is_gt)
        ph.inc("vector", ve, gbf2)
        ph.wait("scalar", ve)

        def efx(s):
            return s.activation(self.stg[0:64, 0:2048], self.stg2[0:64, 0:2048],
                                AF.Exp)
        ph.inc("scalar", ac, efx)
        ph.wait("vector", ac)

        def efm(v):
            return v.tensor_tensor(self.stg[0:64, 0:2048], self.stg[0:64, 0:2048],
                                   self.stg3[0:64, 0:2048], ALU.mult) if False else                 v.tensor_tensor(self.smallb[0:64, 0:2048], self.stg[0:64, 0:2048],
                                self.stg3[0:64, 0:2048], ALU.mult)
        ph.inc("vector", ve, efm)

        for fq in range(4):
            ph.wait("tensor", ve)

            def mdn(t, fq=fq):
                t.matmul(self.psF[0:G, 1024:1536], self.wq[0:64, 1024:1032],
                         self.smallb[0:64, fq * 512:(fq + 1) * 512],
                         start=True, stop=True)
                return t.matmul(self.psF[0:G, 1536:2048], self.wq[0:64, 1024:1032],
                                self.stg2[0:64, fq * 512:(fq + 1) * 512],
                                start=True, stop=True)
            ph.inc("tensor", te, mdn)
            ph.wait("vector", te)

            def evdn(v, fq=fq):
                dn = (self.segf if fq < 2 else self.lwf)[0:G,
                                                         (fq % 2) * 512:
                                                         (fq % 2) * 512 + 512]
                v.tensor_scalar(dn, self.psF[0:G, 1024:1536],
                                self.cst[0:G, 4:5], None, ALU.max)
                return v.tensor_copy(
                    self.state[:].bitcast(BF16)[0:G, fq * 512:(fq + 1) * 512],
                    self.psF[0:G, 1536:2048])
            ph.inc("vector", ve, evdn)
        ph.wait("vector", ve)

        def rcpd(v):
            ins = None
            for fq in range(4):
                dn = (self.segf if fq < 2 else self.lwf)[0:G, (fq % 2) * 512:
                                                         (fq % 2) * 512 + 512]
                ins = v.reciprocal(self.mxf()[0:G, fq * 512:(fq + 1) * 512], dn)
            return ins
        ph.inc("vector", ve, rcpd)

        for fq in range(4):
            ph.wait("tensor", ve)

            def mrcp(t, fq=fq):
                return t.matmul(self.psF[0:E, 1024:1536], self.gexp[:, :],
                                self.mxf()[0:G, fq * 512:(fq + 1) * 512],
                                start=True, stop=True)
            ph.inc("tensor", te, mrcp)
            ph.wait("vector", te)

            def evsf(v, fq=fq):
                return v.tensor_tensor(
                    self.wsm[0:64, fq * 512:(fq + 1) * 512],
                    self.smallb[0:64, fq * 512:(fq + 1) * 512],
                    self.psF[0:E, 1024:1536], ALU.mult)
            ph.inc("vector", ve, evsf)

    def pieces(self):
        mxf = self.mxf()
        s2 = self.stg2[:].bitcast(F32)
        s3 = self.stg3[:].bitcast(F32)
        wf = self.wq[:].bitcast(F32)
        ps = [mxf[:, 11008 + i * 512: 11008 + (i + 1) * 512] for i in range(10)]
        ps += [s2[:, 0:512], s2[:, 512:1024], s3[:, 0:512], s3[:, 512:1024],
               wf[:, 1024:1536], wf[:, 1536:2048]]
        return ps

    def wconv_units(self):
        units = []
        for g in range(G):
            for kt in range(16):
                units.append((self.sc["gpb"][g][kt * 128:(kt + 1) * 128, :],
                              self.gpw[g][kt * 128:(kt + 1) * 128, :]))
                units.append((self.sc["upb"][g][kt * 128:(kt + 1) * 128, :],
                              self.upw[g][kt * 128:(kt + 1) * 128, :]))
            for it in range(11):
                units.append((self.sc["dpb"][g][it * 128:(it + 1) * 128, :],
                              self.dpw[g][it * 128:(it + 1) * 128, :]))
        return units

    def emit_wconv(self, ph, n):
        dlw = self.sems["dlw"]
        while self._wconv_i < len(self._wconv_units) and n > 0:
            dst, src = self._wconv_units[self._wconv_i]
            ph.dma("gpsimd", dlw, dst, src)
            self._wconv_i += 1
            n -= 1

    def emit_phaseD(self, ph):
        dli, dlw, dlo = self.sems["dli"], self.sems["dlw"], self.sems["dlo"]
        ve, te, ac = self.sems["ve"], self.sems["te"], self.sems["ac"]
        sgbf = self.state[:].bitcast(BF16)
        P = self.pieces()
        if "sg0" in self.taps:
            ph.wait("sync", ve)
            ph.dma("sync", dlo, self.taps["sg0"], sgbf[0:G, 0:2048])
            ph.dma("sync", dlo, self.taps["soft0"], self.wsm[0:64, 0:2048])
            ph.dma("sync", dlo, self.taps["flat0"], self.stg2[0:64, 0:2048])
            ph.dma("sync", dlo, self.taps["efm0"], self.smallb[0:64, 0:2048])
            ph.wait("sync", dlo)
        self.emit_wconv(ph, 10 ** 9)
        xg = lambda kt: self.mx[:, 8192 + kt * 512:8192 + (kt + 1) * 512]
        xs2 = lambda kt: self.mx[:, kt * 512:(kt + 1) * 512]
        ap_ = lambda it: self.mx[:, 16384 + it * 512:16384 + (it + 1) * 512]
        gu_chunks = [(0, 4), (4, 4), (8, 3)]
        chunk_dlw = {}
        chunk_te = {}
        ccnt = [0]

        def cbuf(c, w):
            return self.xln[:, (c % 2) * 8192:(c % 2) * 8192 + w]

        def sil(i):
            return self.xln[:, 16384 + i * 512:16384 + (i + 1) * 512]

        def issue_gu(mat, g, i0, ni):
            c = ccnt[0]
            if c >= 2:
                ph.wait("gpsimd", te, chunk_te[c - 2])
            for kt in range(16):
                ph.dma("gpsimd", dlw, cbuf(c, 16 * 512)[:, kt * 512:kt * 512 + ni * 128],
                       mat[kt * 128:(kt + 1) * 128, i0 * 128:(i0 + ni) * 128])
            chunk_dlw[c] = self.sems["dlw"].v
            ccnt[0] += 1
            return c

        def issue_dn(mat, g, m0, nm):
            c = ccnt[0]
            if c >= 2:
                ph.wait("gpsimd", te, chunk_te[c - 2])
            for it in range(11):
                ph.dma("gpsimd", dlw, cbuf(c, 11 * 512)[:, it * 512:it * 512 + nm * 128],
                       mat[it * 128:(it + 1) * 128, m0 * 128:(m0 + nm) * 128])
            chunk_dlw[c] = self.sems["dlw"].v
            ccnt[0] += 1
            return c

        for tq in range(4):
            tc = slice(tq * 512, (tq + 1) * 512)
            ph.wait("sync", ve)
            ph.wait("sync", dlo)
            for kt in range(16):
                ph.dma("sync", dli, xs2(kt), self.sc["x2"][kt * 128:(kt + 1) * 128, tc])
            ph.wait("vector", dli)
            ph.wait("vector", dlo)

            def zed(v, P=P):
                ins = None
                for p in P:
                    ins = v.memset(p, 0.0)
                return ins
            ph.inc("vector", ve, zed)

            for g in range(G):
                ph.wait("gpsimd", te)
                ph.dma("gpsimd", dlw, self.wq[0:G, 0:2048], self.muskT[g])
                ph.wait("tensor", dlw)
                ph.wait("tensor", ve)

                def msg1(t, g=g, tc=tc):
                    t.matmul(self.psF[0:1, 2560:3072], self.idbf[0:G, g:g + 1],
                             sgbf[0:G, tc], start=True, stop=True)
                    return t.matmul(self.psF[0:G, 3072:3584],
                                    self.idbf[0:64, g * 8:(g + 1) * 8],
                                    self.wsm[0:64, tc], start=True, stop=True)
                ph.inc("tensor", te, msg1)
                ph.wait("vector", te)

                def evsg(v):
                    v.tensor_copy(self.stg[0:1, 512:1024], self.psF[0:1, 2560:3072])
                    return v.tensor_copy(self.stg[0:G, 0:512],
                                         self.psF[0:G, 3072:3584])
                ph.inc("vector", ve, evsg)
                ph.wait("tensor", ve)

                def msg2(t):
                    return t.matmul(self.psF[:, 2560:3072], self.onesbf[:],
                                    self.stg[0:1, 512:1024], start=True, stop=True)
                ph.inc("tensor", te, msg2)

                for mt in range(16):
                    slot = mt % 2
                    if mt >= 2:
                        ph.wait("tensor", ve)

                    def mdl(t, mt=mt, slot=slot):
                        return t.matmul(self.psF[:, slot * 512:(slot + 1) * 512],
                                        self.wq[0:G, mt * 128:(mt + 1) * 128],
                                        self.stg[0:G, 0:512], start=True, stop=True)
                    ph.inc("tensor", te, mdl)
                    ph.wait("vector", te)

                    def exg(v, mt=mt, slot=slot):
                        return v.tensor_tensor(xg(mt), xs2(mt),
                                               self.psF[:, slot * 512:(slot + 1) * 512],
                                               ALU.add)
                    ph.inc("vector", ve, exg)

                cs = [issue_gu(self.sc["gpb"][g], g, *gu_chunks[k]) for k in range(2)]
                for ci, (i0, ni) in enumerate(gu_chunks):
                    ph.wait("tensor", dlw, chunk_dlw[cs[ci]])
                    ph.wait("tensor", ve)
                    ph.wait("tensor", ac)

                    def mmg(t, c=cs[ci], ni=ni):
                        ins = None
                        for i in range(ni):
                            for kt in range(16):
                                ins = t.matmul(
                                    self.psF[:, i * 512:(i + 1) * 512],
                                    cbuf(c, 16 * 512)[:, kt * 512 + i * 128:
                                                      kt * 512 + (i + 1) * 128],
                                    xg(kt), start=(kt == 0), stop=(kt == 15))
                        return ins
                    ph.inc("tensor", te, mmg)
                    chunk_te[cs[ci]] = self.sems["te"].v
                    if ci + 2 < len(gu_chunks):
                        cs.append(issue_gu(self.sc["gpb"][g], g, *gu_chunks[ci + 2]))
                    ph.wait("scalar", te)

                    def slv(s, i0=i0, ni=ni):
                        return s.activation(
                            self.xln[:, 16384 + i0 * 512:16384 + (i0 + ni) * 512],
                            self.psF[:, 0:ni * 512], AF.Silu)
                    ph.inc("scalar", ac, slv)

                cs = [issue_gu(self.sc["upb"][g], g, *gu_chunks[k]) for k in range(2)]
                for ci, (i0, ni) in enumerate(gu_chunks):
                    ph.wait("tensor", dlw, chunk_dlw[cs[ci]])
                    ph.wait("tensor", ve)
                    ph.wait("tensor", ac)

                    def mmu(t, c=cs[ci], ni=ni):
                        ins = None
                        for i in range(ni):
                            for kt in range(16):
                                ins = t.matmul(
                                    self.psF[:, i * 512:(i + 1) * 512],
                                    cbuf(c, 16 * 512)[:, kt * 512 + i * 128:
                                                      kt * 512 + (i + 1) * 128],
                                    xg(kt), start=(kt == 0), stop=(kt == 15))
                        return ins
                    ph.inc("tensor", te, mmu)
                    chunk_te[cs[ci]] = self.sems["te"].v
                    if ci + 2 < len(gu_chunks):
                        cs.append(issue_gu(self.sc["upb"][g], g, *gu_chunks[ci + 2]))
                    ph.wait("vector", te)

                    def aev(v, i0=i0, ni=ni):
                        v.tensor_tensor(self.stg[:, 0:ni * 512],
                                        self.psF[:, 0:ni * 512],
                                        self.xln[:, 16384 + i0 * 512:
                                                 16384 + (i0 + ni) * 512], ALU.mult)
                        return v.tensor_tensor(
                            self.mx[:, 16384 + i0 * 512:16384 + (i0 + ni) * 512]
                            .rearrange("p (n c) -> p n c", c=512),
                            self.stg[:, 0:ni * 512].rearrange("p (n c) -> p n c", c=512),
                            self.psF[:, 2560:3072].unsqueeze(1)
                            .broadcast_to([128, ni, 512]), ALU.mult)
                    ph.inc("vector", ve, aev)

                dn_chunks = [(0, 4), (4, 4), (8, 4), (12, 4)]
                cs = [issue_dn(self.sc["dpb"][g], g, *dn_chunks[k]) for k in range(2)]
                for ci, (m0, nm) in enumerate(dn_chunks):
                    ph.wait("tensor", dlw, chunk_dlw[cs[ci]])
                    ph.wait("tensor", ve)

                    def mmd(t, c=cs[ci], m0=m0, nm=nm):
                        ins = None
                        for m in range(nm):
                            for it in range(11):
                                ins = t.matmul(
                                    self.psF[:, m * 512:(m + 1) * 512],
                                    cbuf(c, 11 * 512)[:, it * 512 + m * 128:
                                                      it * 512 + (m + 1) * 128],
                                    ap_(it), start=(it == 0), stop=(it == 10))
                        return ins
                    ph.inc("tensor", te, mmd)
                    chunk_te[cs[ci]] = self.sems["te"].v
                    if ci + 2 < len(dn_chunks):
                        cs.append(issue_dn(self.sc["dpb"][g], g, *dn_chunks[ci + 2]))
                    ph.wait("vector", te)

                    def dacc(v, m0=m0, nm=nm, P=P):
                        ins = None
                        for m in range(nm):
                            ins = v.tensor_tensor(P[m0 + m], P[m0 + m],
                                                  self.psF[:, m * 512:(m + 1) * 512],
                                                  ALU.add)
                        return ins
                    ph.inc("vector", ve, dacc)

            for half in range(2):
                for tb in range(2):
                    col0 = tq * 512 + half * 256 + tb * 128
                    ph.wait("sync", ve)
                    ph.wait("sync", dlo)
                    ph.dma("sync", dli,
                           self.f32b[:].rearrange("p (n c) -> p n c", c=128),
                           self.sc["xmid"][:, col0:col0 + 128].rearrange(
                               "(m p) c -> p m c", p=128))
                    ph.wait("vector", dli)

                    def fadd(v, half=half, tb=tb, P=P):
                        ins = None
                        for mt in range(16):
                            ins = v.tensor_tensor(
                                self.f32a[:, mt * 128:(mt + 1) * 128],
                                P[mt][:, half * 256 + tb * 128:
                                      half * 256 + (tb + 1) * 128],
                                self.f32b[:, mt * 128:(mt + 1) * 128], ALU.add)
                        return ins
                    ph.inc("vector", ve, fadd)
                    ph.wait("tensor", ve)

                    def ftr(t):
                        ins = None
                        for mt in range(16):
                            bk = mt // 4
                            ins = t.transpose(
                                self.psF[:, bk * 512 + (mt % 4) * 128:
                                         bk * 512 + (mt % 4 + 1) * 128],
                                self.f32a[:, mt * 128:(mt + 1) * 128],
                                self.id32[:, :])
                        return ins
                    ph.inc("tensor", te, ftr)
                    ph.wait("vector", te)
                    ph.wait("vector", dlo)

                    def fev(v):
                        return v.tensor_copy(self.f32b[:], self.psF[:, 0:2048])
                    ph.inc("vector", ve, fev)
                    ph.wait("sync", ve)
                    ph.dma("sync", dlo, self.out[col0:col0 + 128, :], self.f32b[:])
        ph.wait("sync", dlo)

    def emit_phaseA(self, ph, prefix):
        x_in = self.xp if prefix else self.xm
        p = "p" if prefix else ""
        self._spill_guard = 0
        self._transpose_x(ph, x_in, prefix)
        if (not prefix) and "xlnc0" in self.taps:
            ph.wait("sync", self.sems["ve"])
            for pt in range(NPT):
                ph.dma("sync", self.sems["dlo"], self.taps["xlnc0"][:, pt * 4:(pt + 1) * 4],
                       self.xln[:].rearrange("p (n c) -> p n c", c=TL)[:, pt, 0:4])
            ph.wait("sync", self.sems["dlo"])
        for col0, width in [(1 + 512 * ci, 512) for ci in range(4)]:
            self._ln_job(ph, col0, width)
        if prefix:
            def saveb(v):
                return v.tensor_copy(
                    self.bcol[:].unsqueeze(2),
                    self.xln[:].rearrange("p (n c) -> p n c", c=TL)[:, :, T:T + 1])
            ph.inc("vector", self.sems["ve"], saveb)
        if (not prefix) and "xlnc" in self.taps:
            ph.wait("sync", self.sems["ve"])
            for pt in range(NPT):
                ph.dma("sync", self.sems["dlo"], self.taps["xlnc"][:, pt * 4:(pt + 1) * 4],
                       self.xln[:].rearrange("p (n c) -> p n c", c=TL)[:, pt, 0:4])
            ph.wait("sync", self.sems["dlo"])
        ph.barrier()
        self._chain_w(ph, self.sc[p + "lw"])
        ph.barrier()
        self._chain_a(ph, self.sc[p + "ag"])
        ph.barrier()
        self._proj_plain(ph, self.wk, self.sc[p + "kraw"], "k")
        ph.barrier()
        self._kpost(ph, p)
        ph.barrier()
        self._proj_v(ph, self.sc[p + "v"])
        ph.barrier()
        if not prefix:
            self._proj_plain(ph, self.wr, self.sc["r"], "r")
            ph.barrier()
            self._beta(ph)
            ph.barrier()
            self._chain_g(ph)
            ph.barrier()

    def _transpose_x(self, ph, x_in, prefix):
        dli, dlo = self.sems["dli"], self.sems["dlo"]
        ve, te, ac = self.sems["ve"], self.sems["te"], self.sems["ac"]
        for grp in range(16):
            ph.wait("gpsimd", te)
            ph.dma("gpsimd", dli, self.stg[:], x_in[1 + grp * 128: 129 + grp * 128, :])
            if not prefix:
                ph.wait("sync", ac)
                ph.dma("sync", dli, self.f32a[:], x_in[1 + grp * 128: 129 + grp * 128, :])
            ph.wait("tensor", dli)
            for q in range(4):
                ph.wait("tensor", ve)
                def tq(t, q=q):
                    ins = None
                    for j in range(4):
                        ft = q * 4 + j
                        ins = t.transpose(self.psH[:, j * 128:(j + 1) * 128],
                                          self.stg[:, ft * 128:(ft + 1) * 128],
                                          self.idbf[:])
                    return ins
                ph.inc("tensor", te, tq)
                ph.wait("vector", te)

                def ev(v, grp=grp, q=q):
                    dst = self.xln[:].rearrange("p (n c) -> p n c", c=TL)[
                        :, q * 4:(q + 1) * 4, 1 + grp * 128: 129 + grp * 128]
                    return v.tensor_copy(
                        dst, self.psH[:, 0:512].rearrange("p (n c) -> p n c", c=128))
                ph.inc("vector", ve, ev)
            if not prefix:
                for q in range(4):
                    ph.wait("tensor", ac)
                    def tq32(t, q=q):
                        ins = None
                        for j in range(4):
                            ft = q * 4 + j
                            ins = t.transpose(
                                self.bank(0)[:, j * 128:(j + 1) * 128],
                                self.f32a[:, ft * 128:(ft + 1) * 128], self.id32[:])
                        return ins
                    ph.inc("tensor", te, tq32)
                    ph.wait("scalar", te)

                    def ev32(s, q=q):
                        return s.copy(self.f32b[:, q * 512:(q + 1) * 512], self.bank(0))
                    ph.inc("scalar", ac, ev32)
                ph.wait("sync", ac)
                ph.dma("sync", dlo, self.sc["x32"][grp], self.f32b[:])
        ph.wait("vector", te)

        def setb(v, prefix=prefix):
            dst = self.xln[:].rearrange("p (n c) -> p n c", c=TL)[:, :, 0:1]
            if prefix:
                return v.memset(dst, 0.0)
            return v.tensor_copy(dst, self.bcol[:].unsqueeze(2))
        ph.inc("vector", ve, setb)

    def _ln_job(self, ph, col0, width):
        ve, te, ac = self.sems["ve"], self.sems["te"], self.sems["ac"]
        ph.wait("vector", self.sems["dli"])
        ph.wait("vector", te)

        def sqs(v):
            ins = None
            for pt in range(NPT):
                xv = self.xln_t(pt, col0, width)
                ins = v.tensor_tensor(self.mx[:, pt * width:(pt + 1) * width],
                                      xv, xv, ALU.mult)
            return ins
        ph.inc("vector", ve, sqs)
        ph.wait("tensor", ve)

        def mms(t):
            ins = None
            for pt in range(NPT):
                t.matmul(self.psF[0:1, 512:512 + width], self.onesc[:],
                         self.xln_t(pt, col0, width),
                         start=(pt == 0), stop=(pt == NPT - 1))
            for pt in range(NPT):
                ins = t.matmul(self.psF[0:1, 1024:1024 + width], self.onesc[:],
                               self.mx[:, pt * width:(pt + 1) * width],
                               start=(pt == 0), stop=(pt == NPT - 1))
            return ins
        ph.inc("tensor", te, mms)
        ph.wait("vector", te)

        def fin(v):
            mu = self.f32a[0:1, 0:width]
            msq = self.f32a[0:1, 512:512 + width]
            var = self.f32a[0:1, 1024:1024 + width]
            v.tensor_scalar(mu, self.psF[0:1, 512:512 + width], 1.0 / D, None, ALU.mult)
            v.tensor_scalar(msq, self.psF[0:1, 1024:1024 + width], 1.0 / D, None, ALU.mult)
            v.tensor_tensor(var, mu, mu, ALU.mult)
            v.tensor_sub(var, msq, var)
            return v.tensor_scalar(var, var, 1e-5, None, ALU.add)
        ph.inc("vector", ve, fin)
        ph.wait("scalar", ve)

        def rstd(s):
            lnv = self.f32b[0:1, 0:width]
            s.activation(lnv, self.f32a[0:1, 1024:1024 + width], AF.Ln)
            return s.activation(self.f32b[0:1, 512:512 + width], lnv, AF.Exp,
                                scale=self.cst[0:1, 0:1])
        ph.inc("scalar", ac, rstd)
        ph.wait("tensor", ac)

        def bc(t):
            t.matmul(self.bank(4, cols=width), self.ones1[0:1, :],
                     self.f32a[0:1, 0:width], start=True, stop=True)
            return t.matmul(self.bank(5, cols=width), self.ones1[0:1, :],
                            self.f32b[0:1, 512:512 + width], start=True, stop=True)
        ph.inc("tensor", te, bc)
        ph.wait("vector", te)

        def norm(v):
            ins = None
            for pt in range(NPT):
                xv = self.xln_t(pt, col0, width)
                tmp = self.stg2[:, 0:width]
                v.tensor_sub(tmp, xv, self.bank(4, cols=width))
                ins = v.tensor_tensor(xv, tmp, self.bank(5, cols=width), ALU.mult)
            return ins
        ph.inc("vector", ve, norm)

    def _mix_full(self, ph, which):
        ve = self.sems["ve"]
        vc_col, dv_col = MIX[which]
        ph.wait("vector", self.sems["te"])

        def mixf(v):
            ins = None
            for pt in range(NPT):
                for ci in range(4):
                    x1 = self.xln_t(pt, 1 + ci * 512, 512)
                    x0 = self.xln_t(pt, ci * 512, 512)
                    dst = self.mx_t(pt, ci * 512, 512)
                    tmp = self.stg2[:, 0:512]
                    v.tensor_scalar(tmp, x0, self.vcol(pt, vc_col), None, ALU.mult)
                    ins = v.scalar_tensor_tensor(dst, x1, self.dcol(pt, dv_col), tmp,
                                                 ALU.mult, ALU.add)
            return ins
        ph.inc("vector", ve, mixf)

    def _quarter_mms(self, ph, wdram, evac_fn):
        dlw, te, ve, ac = (self.sems["dlw"], self.sems["te"], self.sems["ve"],
                           self.sems["ac"])
        ring = 16
        slot_busy = [0] * ring
        for q in range(8):
            for kt in range(NPT):
                slot = (q * NPT + kt) % ring
                ph.wait("gpsimd", te, slot_busy[slot])
                ph.dma("gpsimd", dlw, self.wq[:, slot * 256:(slot + 1) * 256],
                       wdram[kt * 128:(kt + 1) * 128, q * 256:(q + 1) * 256])
            ph.wait("tensor", dlw)
            for mt2 in range(2):
                mt = q * 2 + mt2
                ph.wait("tensor", ve)
                ph.wait("tensor", ac)

                def mmq(t, q=q, mt2=mt2):
                    ins = None
                    for kt in range(NPT):
                        slot = (q * NPT + kt) % ring
                        w_ap = self.wq[:, slot * 256 + mt2 * 128:
                                       slot * 256 + (mt2 + 1) * 128]
                        for ci in range(4):
                            ins = t.matmul(self.bank(ci),
                                           w_ap, self.mx_t(kt, ci * 512, 512),
                                           start=(kt == 0), stop=(kt == NPT - 1))
                    return ins
                ph.inc("tensor", te, mmq)
                for kt in range(NPT):
                    slot = (q * NPT + kt) % ring
                    slot_busy[slot] = te.v
                evac_fn(ph, mt)

    def _proj_plain(self, ph, wdram, spill, which):
        ve, dlo = self.sems["ve"], self.sems["dlo"]
        self._mix_full(ph, which)
        tapname = "mx_" + which
        if tapname in self.taps:
            ph.wait("sync", ve)
            for kt in range(NPT):
                ph.dma("sync", dlo, self.taps[tapname][kt * 128:(kt + 1) * 128, :],
                       self.mx[:, kt * T:(kt + 1) * T])
            ph.wait("sync", dlo)

        def evac(ph2, mt):
            ph.wait("vector", self.sems["te"])
            ph.wait("vector", dlo, self._spill_guard)

            def ev(v):
                ins = None
                for ci in range(4):
                    ins = v.tensor_copy(self.stg3[:, ci * 512:(ci + 1) * 512],
                                        self.bank(ci))
                return ins
            ph.inc("vector", ve, ev)
            ph.wait("sync", ve)
            ph.dma("sync", dlo, spill[mt * 128:(mt + 1) * 128, :], self.stg3[:])
            self._spill_guard = dlo.v
        self._spill_guard = 0
        self._quarter_mms(ph, wdram, evac)


    def v3(self, ap, c=128):
        return ap.rearrange("p (n c) -> p n c", c=c)

    def bslot(self, i):
        return self.mx[0:64, i * 4096:(i + 1) * 4096]

    def xslot(self, i):
        return self.xln[0:64, i * 4096:(i + 1) * 4096]

    def chainq(self, q, i8=None):
        buf = self.bslot(7) if q < 8 else self.xslot(6)
        base = (q % 8) * 512
        if i8 is None:
            return buf[:, base:base + 512]
        return buf[:, base + i8 * 128: base + (i8 + 1) * 128]

    def emit_phaseB(self, ph, prefix):
        dli, dlo = self.sems["dli"], self.sems["dlo"]
        ve, te, ac = self.sems["ve"], self.sems["te"], self.sems["ac"]
        sc = self.sc
        p = "p" if prefix else ""
        betat = self.stg3[0:64, 0:2048].bitcast(F32)

        ph.wait("sync", dlo)

        if self.phb_level < 1:
            return

        if not prefix:
            ph.dma("sync", dli, self.f32a[0:32, 0:T], sc["beta"][:, :])
            ph.wait("tensor", dli)
            ph.wait("tensor", ac)
            ph.wait("tensor", ve)
            for half in range(2):
                ph.wait("tensor", ve)

                def btr(t, half=half):
                    ins = None
                    for k in range(16):
                        cc = half * 16 + k
                        ins = t.transpose(self.bank(0)[0:64, k * 32:(k + 1) * 32],
                                          self.f32a[0:32, cc * 64:(cc + 1) * 64],
                                          self.id32[0:32, 0:32])
                    return ins
                ph.inc("tensor", te, btr)
                ph.wait("vector", te)

                def bev(v, half=half):
                    return v.tensor_copy(betat[:, half * 512:(half + 1) * 512],
                                         self.bank(0)[0:64, 0:512])
                ph.inc("vector", ve, bev)

        for hp in range(NHP):
            row0 = hp * 128
            self.emit_wconv(ph, 11)
            ph.wait("sync", ve)
            ph.wait("sync", te)
            ph.wait("sync", ac)
            ph.wait("sync", dlo)
            names = [("lw", 0), ("a", 1), ("b", 2), ("k2", 3)]
            if not prefix:
                names.append(("r", 4))
            for nm, slot in names:
                for h in range(2):
                    ph.dma("sync", dli, self.bslot(slot)[:, h * T:(h + 1) * T],
                           sc[p + nm][row0 + h * 64:row0 + (h + 1) * 64, :])
            ph.dma("sync", dli,
                   self.v3(self.bslot(5)),
                   sc[p + "v"][hp].rearrange("(n p) c -> p n c", p=64))
            ph.wait("vector", dli)

            for h in range(2):
                hc = slice(h * T, (h + 1) * T)

                def lam(v, hc=hc, h=h):
                    ins = None
                    for q in range(2):
                        qs = slice(q * 1024, (q + 1) * 1024)
                        v.tensor_copy(self.lwf[:, :],
                                      self.bslot(0)[:, h * T + q * 1024:
                                                    h * T + (q + 1) * 1024])
                        ins = v.tensor_tensor_scan(
                            self.f32a[0:64, qs], self.segf[:, :],
                            self.lwf[:, :], 0.0, ALU.mult, ALU.add)
                    return ins
                ph.inc("vector", ve, lam)
                ph.wait("vector", ve)

                def sub1(v, hc=hc):
                    return v.tensor_sub(self.f32b[0:64, 0:T],
                                        self.f32a[0:64, 0:T], self.bslot(0)[:, hc])
                ph.inc("vector", ve, sub1)
                ph.wait("scalar", ve)

                def e1(s):
                    return s.activation(self.f32b[0:64, 0:T],
                                        self.f32b[0:64, 0:T], AF.Exp)
                ph.inc("scalar", ac, e1)
                ph.wait("vector", ac)

                def mAt(v, hc=hc):
                    return v.tensor_tensor(self.bslot(0)[:, hc], self.bslot(1)[:, hc],
                                           self.f32b[0:64, 0:T], ALU.mult)
                ph.inc("vector", ve, mAt)
                ph.wait("scalar", ve)

                def e2(s):
                    return s.activation(self.f32b[0:64, 0:T], self.f32a[0:64, 0:T],
                                        AF.Exp, scale=self.cst[0:64, 1:2])
                ph.inc("scalar", ac, e2)
                ph.wait("vector", ac)

                def mBK(v, hc=hc):
                    v.tensor_tensor(self.bslot(2)[:, hc], self.bslot(2)[:, hc],
                                    self.f32b[0:64, 0:T], ALU.mult)
                    return v.tensor_tensor(self.bslot(3)[:, hc], self.bslot(3)[:, hc],
                                           self.f32b[0:64, 0:T], ALU.mult)
                ph.inc("vector", ve, mBK)
                ph.wait("scalar", ve)

                def e3(s):
                    return s.activation(self.f32b[0:64, 0:T], self.f32a[0:64, 0:T],
                                        AF.Exp)
                ph.inc("scalar", ac, e3)
                ph.wait("vector", ac)

                def mRcl(v, hc=hc, h=h, hp=hp, prefix=prefix):
                    ins = v.tensor_copy(
                        self.clb[0:64, h * NCH:(h + 1) * NCH],
                        self.f32b[0:64, 0:T].rearrange(
                            "p (n c) -> p n c", c=L)[:, :, L - 1])
                    if not prefix:
                        ins = v.tensor_tensor(self.bslot(4)[:, hc],
                                              self.bslot(4)[:, hc],
                                              self.f32b[0:64, 0:T], ALU.mult)
                        bview = betat[:].rearrange("p (n h) -> p n h", h=32)[
                            :, :, 2 * hp + h].unsqueeze(2).broadcast_to([64, 32, 64])
                        ins = v.tensor_tensor(
                            self.v3(self.bslot(6))[:, :, h * 64:(h + 1) * 64],
                            self.v3(self.bslot(5))[:, :, h * 64:(h + 1) * 64],
                            bview, ALU.mult)
                    return ins
                ph.inc("vector", ve, mRcl)

            if self.phb_level < 2:
                continue

            for which in range(2):
                srcslot, dstslot = (2, 0) if which == 0 else (3, 1)
                for grp in range(4):
                    ph.wait("tensor", ve)

                    def trx(t, srcslot=srcslot, grp=grp):
                        ins = None
                        for k in range(16):
                            inst = grp * 16 + k
                            cc, h = inst // 2, inst % 2
                            ins = t.transpose(
                                self.psH[0:64, k * 64:(k + 1) * 64],
                                self.bslot(srcslot)[:, h * T + cc * L:
                                                    h * T + (cc + 1) * L],
                                self.idbf[0:64, 0:64])
                        return ins
                    ph.inc("tensor", te, trx)
                    ph.wait("vector", te)

                    def evx(v, dstslot=dstslot, grp=grp):
                        return v.tensor_copy(
                            self.xslot(dstslot)[:, grp * 1024:(grp + 1) * 1024],
                            self.psH[0:64, 0:1024])
                    ph.inc("vector", ve, evx)

            if self.phb_level < 3:
                continue
            for b8 in range(8):
                self._tinv_chain(ph, b8)

            if self.phb_level < 4 and self.phb_level not in (31, 32, 33):
                continue
            self._scan_builds(ph, prefix)

            if hp == 0 and not prefix and "pb_s0" in self.taps:
                ph.wait("sync", ve)
                for i in range(7):
                    ph.dma("sync", dlo, self.taps["pb_s%d" % i], self.bslot(i))
                for i in range(6):
                    ph.dma("sync", dlo, self.taps["pb_x%d" % i], self.xslot(i))
                ph.dma("sync", dlo, self.taps["pb_clb"], self.clb[0:64, :])
                ph.wait("sync", dlo)

            if self.phb_level < 5 or self.phb_level in (31, 32, 33):
                continue
            self._sweep(ph, prefix, hp)

            if not prefix and self.phb_level >= 5:
                ph.wait("sync", ve)
                for h in range(2):
                    ph.dma("sync", dlo, sc["y"][row0 + h * 64:row0 + (h + 1) * 64, :],
                           self.bslot(7)[:, h * T:(h + 1) * T])

    def _slice_hc(self, slot, inst):
        cc, h = inst // 2, inst % 2
        return self.bslot(slot)[:, h * T + cc * L: h * T + (cc + 1) * L]

    def _tinv_chain(self, ph, b8):
        ve, te, ac = self.sems["ve"], self.sems["te"], self.sems["ac"]
        i0 = b8 * 8

        PHYS = {0: 0, 1: 1, 2: 2, 3: 3, 4: 4, 5: 2, 6: 3, 7: 4,
                8: 5, 9: 2, 10: 3, 11: 4, 12: 0, 13: 1}

        def q_ap(q, i8, b=None):
            s = PHYS[q]
            buf = self.bslot(7) if s < 4 else self.xslot(6)
            base = (s % 4) * 1024 + i8 * 128
            if b is None:
                return buf[:, base:base + 128]
            return buf[:, base + b * 64: base + b * 64 + 64]

        def q_all(q):
            s = PHYS[q]
            buf = self.bslot(7) if s < 4 else self.xslot(6)
            return buf[:, (s % 4) * 1024:(s % 4) * 1024 + 1024]

        def mask8(mk):
            return self.mask(mk)[0:64, :].unsqueeze(1).broadcast_to([64, 8, 128])

        def pv(bank0_pair):
            return self.psF[0:64, 0:1024].rearrange("p (n c) -> p n c", c=128)

        ph.wait("tensor", ve)

        def mm0(t, i0=i0):
            ins = None
            for i8 in range(8):
                inst = i0 + i8
                at, bh = self._slice_hc(0, inst), self._slice_hc(2, inst)
                pa = self.psF[0:64, i8 * 128: i8 * 128 + 64]
                pb = self.psF[0:64, i8 * 128 + 64: i8 * 128 + 128]
                t.matmul(pa, at, bh, start=True, stop=True)
                ins = t.matmul(pb, bh, at, start=True, stop=True)
            return ins
        ph.inc("tensor", te, mm0)
        ph.wait("vector", te)

        def ev0(v):
            v.tensor_tensor(self.v3(q_all(0)), pv(0), mask8(MK_QAB_S), ALU.mult)
            v.tensor_tensor(self.v3(q_all(1)), pv(0), mask8(MK_QAB_D16), ALU.mult)
            return v.tensor_tensor(self.v3(q_all(2)), self.v3(q_all(1)),
                                   mask8(MK_QI), ALU.add)
        ph.inc("vector", ve, ev0)

        def prod(qP, qQ, qdst, add_from=None):
            ph.wait("tensor", ve)

            def mm(t, qP=qP, qQ=qQ):
                ins = None
                for i8 in range(8):
                    pa = self.psF[0:64, i8 * 128: i8 * 128 + 64]
                    pb = self.psF[0:64, i8 * 128 + 64: i8 * 128 + 128]
                    t.matmul(pa, q_ap(qP, i8, 1), q_ap(qQ, i8, 0),
                             start=True, stop=True)
                    ins = t.matmul(pb, q_ap(qQ, i8, 0), q_ap(qP, i8, 1),
                                   start=True, stop=True)
                return ins
            ph.inc("tensor", te, mm)
            ph.wait("vector", te)

            def ev(v, qdst=qdst, add_from=add_from):
                if add_from is None:
                    return v.tensor_copy(q_all(qdst), self.psF[0:64, 0:1024])
                return v.tensor_tensor(q_all(qdst), self.psF[0:64, 0:1024],
                                       q_all(add_from), ALU.add)
            ph.inc("vector", ve, ev)

        prod(1, 1, 3)
        prod(2, 3, 4, add_from=2)
        prod(3, 3, 5)
        prod(4, 5, 6, add_from=4)
        prod(5, 5, 7)
        prod(6, 7, 8, add_from=6)
        ph.wait("vector", te)

        def lo1(v):
            return v.tensor_sub(q_all(9), q_all(0), q_all(1))
        ph.inc("vector", ve, lo1)

        ph.wait("tensor", ve)

        def mmB(t):
            ins = None
            for i8 in range(8):
                pa = self.psF[0:64, i8 * 128: i8 * 128 + 64]
                pb = self.psF[0:64, i8 * 128 + 64: i8 * 128 + 128]
                t.matmul(pa, q_ap(8, i8, 1), q_ap(9, i8, 0), start=True, stop=True)
                ins = t.matmul(pb, q_ap(9, i8, 0), q_ap(8, i8, 1),
                               start=True, stop=True)
            return ins
        ph.inc("tensor", te, mmB)
        ph.wait("vector", te)

        def evB(v):
            v.tensor_copy(q_all(10), self.psF[0:64, 0:1024])
            return v.tensor_tensor(self.v3(q_all(11)), self.v3(q_all(10)),
                                   mask8(MK_QI), ALU.add)
        ph.inc("vector", ve, evB)

        prod(10, 10, 12)
        prod(11, 12, 13, add_from=11)

        ph.wait("tensor", ve)

        def mmT(t):
            ins = None
            for i8 in range(8):
                ins = t.matmul(self.psF[0:64, i8 * 64:(i8 + 1) * 64],
                               q_ap(8, i8, 0), q_ap(13, i8, 1),
                               start=True, stop=True)
            return ins
        ph.inc("tensor", te, mmT)
        ph.wait("vector", te)

        def evT(v, i0=i0):
            return v.tensor_copy(self.xslot(2)[:, i0 * 64:(i0 + 8) * 64],
                                 self.psF[0:64, 0:512])
        ph.inc("vector", ve, evT)

    def _scan_builds(self, ph, prefix):
        ve, te = self.sems["ve"], self.sems["te"]
        sub = self.phb_level

        def mask16(mk):
            return self.mask(mk)[0:64, 0:64].unsqueeze(1).broadcast_to([64, 16, 64])

        for b16 in range(4):
            i0 = b16 * 16
            ph.wait("tensor", ve)

            def mmk(t, i0=i0):
                ins = None
                for k in range(16):
                    inst = i0 + k
                    ins = t.matmul(self.psF[0:64, k * 64:(k + 1) * 64],
                                   self._slice_hc(3, inst),
                                   self._slice_hc(0, inst),
                                   start=True, stop=True)
                return ins
            ph.inc("tensor", te, mmk)
            ph.wait("vector", te)

            def evk(v):
                return v.tensor_tensor(
                    self.v3(self.stg2[0:64, 0:1024], c=64),
                    self.psF[0:64, 0:1024].rearrange("p (n c) -> p n c", c=64),
                    mask16(MK_STRICT_T22), ALU.mult)
            ph.inc("vector", ve, evk)
            if sub == 31:
                continue
            ph.wait("tensor", ve)

            def mmu(t, i0=i0):
                ins = None
                for k in range(16):
                    inst = i0 + k
                    cc, h = inst // 2, inst % 2
                    rh = self.v3(self.bslot(5))[:, cc, h * 64:(h + 1) * 64]
                    ins = t.matmul(self.psF[0:64, 1024 + k * 64:1024 + (k + 1) * 64],
                                   self.stg2[0:64, k * 64:(k + 1) * 64], rh,
                                   start=True, stop=True)
                return ins
            ph.inc("tensor", te, mmu)
            ph.wait("vector", te)

            def evu(v, i0=i0):
                return v.tensor_copy(self.xslot(3)[:, i0 * 64:(i0 + 16) * 64],
                                     self.psF[0:64, 1024:2048])
            ph.inc("vector", ve, evu)
            if sub == 32:
                continue

            if not prefix:
                for which in range(2):
                    srcslot = 2 if which == 0 else 3
                    dstx = 4 if which == 0 else 5
                    ph.wait("tensor", ve)

                    def mmn(t, i0=i0, srcslot=srcslot):
                        ins = None
                        for k in range(16):
                            inst = i0 + k
                            ins = t.matmul(self.psF[0:64, k * 64:(k + 1) * 64],
                                           self._slice_hc(srcslot, inst),
                                           self._slice_hc(4, inst),
                                           start=True, stop=True)
                        return ins
                    ph.inc("tensor", te, mmn)
                    ph.wait("vector", te)

                    def evn(v, i0=i0, dstx=dstx):
                        return v.tensor_tensor(
                            self.v3(self.xslot(dstx)[:, i0 * 64:(i0 + 16) * 64], c=64),
                            self.psF[0:64, 0:1024].rearrange("p (n c) -> p n c", c=64),
                            mask16(MK_INCL_T22), ALU.mult)
                    ph.inc("vector", ve, evn)

    def _sweep(self, ph, prefix, hp):
        ve, te, ac = self.sems["ve"], self.sems["te"], self.sems["ac"]
        st = self.state[0:64, hp * 128:(hp + 1) * 128]
        stb = self.statebf[0:64, hp * 128:(hp + 1) * 128]
        yb = self.bslot(7)
        sub = self.phb_level

        for cc in range(NCH):
            ph.wait("tensor", ve)
            ph.wait("tensor", ac)

            def mmp(t, cc=cc):
                ins = None
                for h in range(2):
                    ins = t.matmul(
                        self.psF[0:64, 4 * 512 + h * 64:4 * 512 + (h + 1) * 64],
                        self._slice_hc(0, cc * 2 + h),
                        stb[:, h * 64:(h + 1) * 64],
                        start=True, stop=True)
                return ins
            ph.inc("tensor", te, mmp)
            ph.wait("vector", te)

            def t0add(v, cc=cc):
                return v.tensor_tensor(self.stg[0:64, 0:128],
                                       self.psF[0:64, 4 * 512:4 * 512 + 128],
                                       self.xslot(3)[:, cc * 128:(cc + 1) * 128],
                                       ALU.add)
            ph.inc("vector", ve, t0add)
            if sub == 51:
                continue
            ph.wait("tensor", ve)

            def mmsa(t, cc=cc):
                ins = None
                for h in range(2):
                    ins = t.matmul(
                        self.psF[0:64, 5 * 512 + h * 64:5 * 512 + (h + 1) * 64],
                        self.xslot(2)[:, (cc * 2 + h) * 64:(cc * 2 + h + 1) * 64],
                        self.stg[0:64, h * 64:(h + 1) * 64],
                        start=True, stop=True)
                return ins
            ph.inc("tensor", te, mmsa)
            ph.wait("vector", te)

            def saev(v):
                return v.tensor_copy(self.stg[0:64, 128:256],
                                     self.psF[0:64, 5 * 512:5 * 512 + 128])
            ph.inc("vector", ve, saev)
            if sub == 52:
                continue
            ph.wait("tensor", ve)

            if not prefix and sub != 53:
                def mmy(t, cc=cc):
                    ins = None
                    for h in range(2):
                        inst = cc * 2 + h
                        po = self.psF[0:64, 6 * 512 + h * 64:6 * 512 + (h + 1) * 64]
                        pb = self.psF[0:64, 6 * 512 + 128 + h * 64:
                                      6 * 512 + 128 + (h + 1) * 64]
                        t.matmul(po, stb[:, h * 64:(h + 1) * 64],
                                 self._slice_hc(4, inst), start=True, stop=False)
                        t.matmul(po, self.stg[0:64, 128 + h * 64:128 + (h + 1) * 64],
                                 self.xslot(4)[:, inst * 64:(inst + 1) * 64],
                                 start=False, stop=False)
                        t.matmul(po, self.v3(self.bslot(5))[:, cc, h * 64:(h + 1) * 64],
                                 self.xslot(5)[:, inst * 64:(inst + 1) * 64],
                                 start=False, stop=True)
                        ins = t.matmul(pb,
                                       self.v3(self.bslot(6))[:, cc, h * 64:(h + 1) * 64],
                                       self.idbf[0:64, 0:64],
                                       start=True, stop=True)
                    return ins
                ph.inc("tensor", te, mmy)

            def mmst(t, cc=cc):
                ins = None
                for h in range(2):
                    inst = cc * 2 + h
                    po = self.psF[0:64, h * 64:(h + 1) * 64]
                    t.matmul(po, self.xslot(0)[:, inst * 64:(inst + 1) * 64],
                             self.stg[0:64, 128 + h * 64:128 + (h + 1) * 64],
                             start=True, stop=False)
                    ins = t.matmul(po, self.xslot(1)[:, inst * 64:(inst + 1) * 64],
                                   self.v3(self.bslot(5))[:, cc, h * 64:(h + 1) * 64],
                                   start=False, stop=True)
                return ins
            ph.inc("tensor", te, mmst)
            ph.wait("vector", te)

            def fold1(v, cc=cc):
                return v.tensor_tensor(self.f32a[0:64, 0:128], st[:, :],
                                       self.psF[0:64, 0:128], ALU.add)
            ph.inc("vector", ve, fold1)
            if not prefix and sub != 53:
                ph.wait("scalar", te)

                def gnaff(s, cc=cc, hp=hp):
                    ins = None
                    for h in range(2):
                        hd = 2 * hp + h
                        ins = s.activation(
                            self.stg[0:64, 256 + h * 64:256 + (h + 1) * 64],
                            self.psF[0:64, 6 * 512 + h * 64:6 * 512 + (h + 1) * 64],
                            AF.Identity,
                            scale=self.gnv[0:64, hd:hd + 1],
                            bias=self.gnv[0:64, 32 + hd:33 + hd])
                    return ins
                ph.inc("scalar", ac, gnaff)
                ph.wait("vector", ac)

                def yev(v, cc=cc):
                    return v.tensor_tensor(
                        yb[:].rearrange("p (h c) -> p h c", c=T)[:, :, cc * L:(cc + 1) * L],
                        self.stg[0:64, 256:384].rearrange("p (h c) -> p h c", c=64),
                        self.psF[0:64, 6 * 512 + 128:6 * 512 + 256].rearrange(
                            "p (h c) -> p h c", c=64),
                        ALU.add)
                ph.inc("vector", ve, yev)
            ph.wait("scalar", ve)

            def fold2(s, cc=cc, hp=hp):
                ins = None
                for h in range(2):
                    ins = s.activation(st[:, h * 64:(h + 1) * 64],
                                       self.f32a[0:64, h * 64:(h + 1) * 64],
                                       AF.Identity,
                                       scale=self.clb[0:64, h * NCH + cc:
                                                      h * NCH + cc + 1])
                return ins
            ph.inc("scalar", ac, fold2)
            ph.wait("vector", ac)

            def fold2b(v):
                return v.tensor_copy(stb[:, :], st[:, :])
            ph.inc("vector", ve, fold2b)

    def _chain_w(self, ph, spill):
        dlw, dlo, te, ve, ac = (self.sems["dlw"], self.sems["dlo"], self.sems["te"],
                                self.sems["ve"], self.sems["ac"])
        self._mix_full(ph, "w")
        for kt in range(NPT):
            ph.dma("gpsimd", dlw, self.wq[:, kt * 128: kt * 128 + 64],
                   self.w1[kt * 128:(kt + 1) * 128, :])
        ph.wait("tensor", dlw)
        ph.wait("tensor", ve)

        def mm1(t):
            ins = None
            for kt in range(NPT):
                for ci in range(4):
                    ins = t.matmul(self.psF[0:64, ci * 512:(ci + 1) * 512],
                                   self.wq[:, kt * 128: kt * 128 + 64],
                                   self.mx_t(kt, ci * 512, 512),
                                   start=(kt == 0), stop=(kt == NPT - 1))
            return ins
        ph.inc("tensor", te, mm1)
        ph.wait("scalar", te)

        def tanh(s):
            ins = None
            for ci in range(4):
                ins = s.activation(self.smallb[:, ci * 512:(ci + 1) * 512],
                                   self.psF[0:64, ci * 512:(ci + 1) * 512], AF.Tanh)
            return ins
        ph.inc("scalar", ac, tanh)
        ph.dma("gpsimd", dlw, self.wsm[0:64, 0:D], self.w2[:, :])
        ph.wait("tensor", dlw)
        ph.wait("tensor", ac)
        for mt in range(NPT):
            ph.wait("tensor", ve)
            ph.wait("tensor", ac)

            def mm2(t, mt=mt):
                ins = None
                for ci in range(4):
                    ins = t.matmul(self.bank(ci),
                                   self.wsm[0:64, mt * 128:(mt + 1) * 128],
                                   self.smallb[:, ci * 512:(ci + 1) * 512],
                                   start=True, stop=True)
                return ins
            ph.inc("tensor", te, mm2)
            ph.wait("scalar", te)
            ph.wait("scalar", ve)

            def chain(s, mt=mt):
                au = self.f32a[:, 0:D]
                for ci in range(4):
                    s.activation(au[:, ci * 512:(ci + 1) * 512], self.bank(ci),
                                 AF.Abs, bias=self.vcol(mt, VC.w0))
                eu = self.f32b[:, 0:D]
                s.activation(eu, au, AF.Exp, scale=self.cst[:, 1:2])
                lp = self.f32a[:, 0:D]
                s.activation(lp, eu, AF.Ln, bias=1.0)
                ru = self.f32b[:, 0:D]
                ins = None
                for ci in range(4):
                    ins = s.activation(ru[:, ci * 512:(ci + 1) * 512], self.bank(ci),
                                       AF.Relu, scale=self.cst[:, 1:2],
                                       bias=self.vcol(mt, VC.negw0))
                return ins
            ph.inc("scalar", ac, chain)
            ph.wait("vector", ac)

            def chain2(v, mt=mt):
                sp = self.f32a[:, 0:D]
                return v.tensor_add(sp, self.f32a[:, 0:D], self.f32b[:, 0:D])
            ph.inc("vector", ve, chain2)
            ph.wait("scalar", ve)

            def chain3(s, mt=mt):
                return s.activation(self.f32b[:, 0:D], self.f32a[:, 0:D], AF.Exp,
                                    scale=self.cst[:, 1:2], bias=self.cst[:, 0:1])
            ph.inc("scalar", ac, chain3)
            ph.wait("vector", ac)
            ph.wait("vector", dlo, self._spill_guard)

            def chain4(v, mt=mt):
                return v.tensor_scalar(self.stg3[:], self.f32b[:, 0:D], -1.0, None,
                                       ALU.mult)
            ph.inc("vector", ve, chain4)
            ph.wait("sync", ve)
            ph.dma("sync", dlo, spill[mt * 128:(mt + 1) * 128, :], self.stg3[:])
            self._spill_guard = dlo.v

    def _chain_a(self, ph, spill):
        dlw, dlo, te, ve, ac = (self.sems["dlw"], self.sems["dlo"], self.sems["te"],
                                self.sems["ve"], self.sems["ac"])
        self._mix_full(ph, "a")
        for kt in range(NPT):
            ph.dma("gpsimd", dlw, self.wq[:, kt * 128: kt * 128 + 64],
                   self.a1[kt * 128:(kt + 1) * 128, :])
        ph.wait("tensor", dlw)
        ph.wait("tensor", ve)

        def mm1(t):
            ins = None
            for kt in range(NPT):
                for ci in range(4):
                    ins = t.matmul(self.psF[0:64, ci * 512:(ci + 1) * 512],
                                   self.wq[:, kt * 128: kt * 128 + 64],
                                   self.mx_t(kt, ci * 512, 512),
                                   start=(kt == 0), stop=(kt == NPT - 1))
            return ins
        ph.inc("tensor", te, mm1)
        ph.wait("vector", te)

        def cp(v):
            ins = None
            for ci in range(4):
                ins = v.tensor_copy(self.smallb[:, ci * 512:(ci + 1) * 512],
                                    self.psF[0:64, ci * 512:(ci + 1) * 512])
            return ins
        ph.inc("vector", ve, cp)
        ph.dma("gpsimd", dlw, self.wsm[0:64, 0:D], self.a2[:, :])
        ph.wait("tensor", dlw)
        ph.wait("tensor", ve)
        for mt in range(NPT):
            ph.wait("tensor", ac)

            def mm2(t, mt=mt):
                ins = None
                for ci in range(4):
                    ins = t.matmul(self.bank(ci),
                                   self.wsm[0:64, mt * 128:(mt + 1) * 128],
                                   self.smallb[:, ci * 512:(ci + 1) * 512],
                                   start=True, stop=True)
                return ins
            ph.inc("tensor", te, mm2)
            ph.wait("scalar", te)
            ph.wait("scalar", dlo, self._spill_guard)

            def sig(s, mt=mt):
                ins = None
                for ci in range(4):
                    ins = s.activation(self.stg3[:, ci * 512:(ci + 1) * 512],
                                       self.bank(ci), AF.Sigmoid,
                                       bias=self.vcol(mt, VC.a0))
                return ins
            ph.inc("scalar", ac, sig)
            ph.wait("sync", ac)
            ph.dma("sync", dlo, spill[mt * 128:(mt + 1) * 128, :], self.stg3[:])
            self._spill_guard = dlo.v

    def _kpost(self, ph, p):
        dli, dlo, te, ve, ac = (self.sems["dli"], self.sems["dlo"], self.sems["te"],
                                self.sems["ve"], self.sems["ac"])
        sc = self.sc
        ph.wait("sync", dlo)
        for mt in range(NPT):
            ph.wait("sync", ve)
            ph.dma("sync", dli, self.stg[:], sc[p + "kraw"][mt * 128:(mt + 1) * 128, :])
            ph.wait("vector", dli)
            ph.wait("vector", te)

            def kk_sq(v, mt=mt):
                v.tensor_scalar(self.stg2[:], self.stg[:], self.vcol(mt, VC.k_k),
                                None, ALU.mult)
                return v.tensor_tensor(self.stg3[:], self.stg2[:], self.stg2[:],
                                       ALU.mult)
            ph.inc("vector", ve, kk_sq)
            ph.wait("tensor", ve)

            def summ(t, mt=mt):
                ins = None
                for ci in range(4):
                    ins = t.matmul(self.psF[0:32, ci * 512:(ci + 1) * 512],
                                   self.hind[:, mt * H:(mt + 1) * H],
                                   self.stg3[:, ci * 512:(ci + 1) * 512],
                                   start=(mt == 0), stop=(mt == NPT - 1))
                return ins
            ph.inc("tensor", te, summ)
        ph.wait("scalar", te)

        def inv(s):
            s.activation(self.f32a[0:32, 0:D], self.psF[0:32, 0:D], AF.Ln)
            return s.activation(self.f32b[0:32, 0:D], self.f32a[0:32, 0:D], AF.Exp,
                                scale=self.cst[0:32, 0:1])
        ph.inc("scalar", ac, inv)
        ph.wait("vector", ac)

        def invc(v):
            v.tensor_scalar(self.f32b[0:32, 0:D], self.f32b[0:32, 0:D], 1e12,
                            None, ALU.min)
            return v.tensor_copy(self.invnb[0:32, 0:D], self.f32b[0:32, 0:D])
        ph.inc("vector", ve, invc)
        for mt in range(NPT):
            ph.wait("sync", ve)
            ph.dma("sync", dli, self.stg[:], sc[p + "kraw"][mt * 128:(mt + 1) * 128, :])
            ph.dma("sync", dli, self.stg2[:], sc[p + "ag"][mt * 128:(mt + 1) * 128, :])
            ph.wait("tensor", dli)
            ph.wait("tensor", ve)

            def expd(t, mt=mt):
                ins = None
                for ci in range(4):
                    ins = t.matmul(self.bank(ci),
                                   self.eind[:, mt * 128:(mt + 1) * 128],
                                   self.invnb[0:32, ci * 512:(ci + 1) * 512],
                                   start=True, stop=True)
                return ins
            ph.inc("tensor", te, expd)
            ph.wait("vector", te)
            ph.wait("vector", dlo, self._spill_guard)

            def mkstreams(v, mt=mt):
                for ci in range(4):
                    pb = self.bank(ci)
                    sl = slice(ci * 512, (ci + 1) * 512)
                    kkv = self.f32a[:, 0:512]
                    v.tensor_scalar(kkv, self.stg[:, sl], self.vcol(mt, VC.k_k),
                                    None, ALU.mult)
                    t1 = self.f32b[:, 0:512]
                    v.tensor_tensor(t1, kkv, pb, ALU.mult)
                    v.tensor_scalar(self.stg3[:, sl], t1, -1.0, None, ALU.mult)
                    v.tensor_tensor(self.mx[:, sl], t1, self.stg2[:, sl], ALU.mult)
                    t2 = self.f32a[:, 512:1024]
                    v.tensor_scalar(t2, self.stg2[:, sl], self.vcol(mt, VC.k_a),
                                    self.dcol(mt, DV.one_m_ka), ALU.mult, ALU.add)
                    ins = v.tensor_tensor(self.mx[:, D + ci * 512: D + (ci + 1) * 512],
                                          self.stg[:, sl], t2, ALU.mult)
                return ins
            ph.inc("vector", ve, mkstreams)
            ph.wait("sync", ve)
            ph.dma("sync", dlo, sc[p + "a"][mt * 128:(mt + 1) * 128, :], self.stg3[:])
            ph.dma("sync", dlo, sc[p + "b"][mt * 128:(mt + 1) * 128, :], self.mx[:, 0:D])
            ph.dma("sync", dlo, sc[p + "k2"][mt * 128:(mt + 1) * 128, :],
                   self.mx[:, D:2 * D])
            self._spill_guard = dlo.v

    def _proj_v(self, ph, spill):
        dlw, dlo, te, ve = (self.sems["dlw"], self.sems["dlo"], self.sems["te"],
                            self.sems["ve"])
        for kt in range(NPT):
            ph.dma("gpsimd", dlw, self.wbig[:, kt * D:(kt + 1) * D],
                   self.wv[kt * 128:(kt + 1) * 128, :])
        vc_col, dv_col = MIX["v"]
        for tt in range(16):
            ph.wait("vector", te)

            def mixtt(v, tt=tt):
                for kt in range(NPT):
                    x0 = self.xln_t(kt, tt * 128, 128)
                    v.tensor_scalar(self.stg3[:, kt * 128:(kt + 1) * 128], x0,
                                    self.vcol(kt, vc_col), None, ALU.mult)
                ins = None
                for kt in range(NPT):
                    x1 = self.xln_t(kt, 1 + tt * 128, 128)
                    ins = v.scalar_tensor_tensor(self.stg[:, kt * 128:(kt + 1) * 128],
                                                 x1, self.dcol(kt, dv_col),
                                                 self.stg3[:, kt * 128:(kt + 1) * 128],
                                                 ALU.mult, ALU.add)
                return ins
            ph.inc("vector", ve, mixtt)
            if tt == 0 and "vmix0" in self.taps:
                ph.wait("sync", ve)
                ph.dma("sync", dlo, self.taps["vmix0"], self.stg[:])
                ph.wait("sync", dlo)
            ph.wait("tensor", ve)
            ph.wait("tensor", dlw)

            def mmv(t, tt=tt):
                ins = None
                for fq in range(4):
                    for kt in range(NPT):
                        ins = t.matmul(self.bank(fq),
                                       self.stg[:, kt * 128:(kt + 1) * 128],
                                       self.wbig[:, kt * D + fq * 512:
                                                  kt * D + (fq + 1) * 512],
                                       start=(kt == 0), stop=(kt == NPT - 1))
                return ins
            ph.inc("tensor", te, mmv)
            ph.wait("vector", te)
            ph.wait("vector", dlo, self._spill_guard)

            def evv(v, tt=tt):
                ins = None
                for fq in range(4):
                    ins = v.tensor_copy(self.stg2[:, fq * 512:(fq + 1) * 512],
                                        self.bank(fq))
                return ins
            ph.inc("vector", ve, evv)
            ph.wait("sync", ve)
            for hp in range(NHP):
                ph.dma("sync", dlo, spill[hp, tt * 128:(tt + 1) * 128, :],
                       self.stg2[:, hp * 128:(hp + 1) * 128])
            self._spill_guard = dlo.v

    def _beta(self, ph):
        dli, dlo, te, ve = (self.sems["dli"], self.sems["dlo"], self.sems["te"],
                            self.sems["ve"])
        ph.wait("sync", dlo)
        for mt in range(NPT):
            ph.wait("sync", ve)
            ph.dma("sync", dli, self.stg[:], self.sc["r"][mt * 128:(mt + 1) * 128, :])
            ph.dma("sync", dli, self.stg2[:], self.sc["k2"][mt * 128:(mt + 1) * 128, :])
            ph.wait("vector", dli)
            ph.wait("vector", te)

            def rr(v, mt=mt):
                v.tensor_tensor(self.stg3[:], self.stg[:], self.stg2[:], ALU.mult)
                return v.tensor_scalar(self.stg3[:], self.stg3[:],
                                       self.vcol(mt, VC.r_k), None, ALU.mult)
            ph.inc("vector", ve, rr)
            ph.wait("tensor", ve)

            def summ(t, mt=mt):
                ins = None
                for ci in range(4):
                    ins = t.matmul(self.psF[0:32, ci * 512:(ci + 1) * 512],
                                   self.hind[:, mt * H:(mt + 1) * H],
                                   self.stg3[:, ci * 512:(ci + 1) * 512],
                                   start=(mt == 0), stop=(mt == NPT - 1))
                return ins
            ph.inc("tensor", te, summ)
        ph.wait("vector", te)

        def bev(v):
            return v.tensor_copy(self.f32a[0:32, 0:D], self.psF[0:32, 0:D])
        ph.inc("vector", ve, bev)
        ph.wait("sync", ve)
        ph.dma("sync", dlo, self.sc["beta"][:, :], self.f32a[0:32, 0:D])

    def _chain_g(self, ph):
        dlw, dlo, te, ve, ac = (self.sems["dlw"], self.sems["dlo"], self.sems["te"],
                                self.sems["ve"], self.sems["ac"])
        self._mix_full(ph, "g")
        for kt in range(NPT):
            ph.dma("gpsimd", dlw, self.wq[:, kt * 128:(kt + 1) * 128],
                   self.g1[kt * 128:(kt + 1) * 128, :])
        ph.wait("tensor", dlw)
        ph.wait("tensor", ve)

        def mm1(t):
            ins = None
            for kt in range(NPT):
                for ci in range(4):
                    ins = t.matmul(self.bank(ci),
                                   self.wq[:, kt * 128:(kt + 1) * 128],
                                   self.mx_t(kt, ci * 512, 512),
                                   start=(kt == 0), stop=(kt == NPT - 1))
            return ins
        ph.inc("tensor", te, mm1)
        ph.wait("scalar", te)

        def sig(s):
            ins = None
            for ci in range(4):
                ins = s.activation(self.stg[:, ci * 512:(ci + 1) * 512],
                                   self.bank(ci), AF.Sigmoid)
            return ins
        ph.inc("scalar", ac, sig)
        ph.dma("gpsimd", dlw, self.wsm[:, 0:D], self.g2[:, :])
        ph.wait("tensor", dlw)
        for mt in range(NPT):
            ph.wait("tensor", ac)
            ph.wait("tensor", ve)

            def mm2(t, mt=mt):
                ins = None
                for ci in range(4):
                    ins = t.matmul(self.bank(ci),
                                   self.wsm[:, mt * 128:(mt + 1) * 128],
                                   self.stg[:, ci * 512:(ci + 1) * 512],
                                   start=True, stop=True)
                return ins
            ph.inc("tensor", te, mm2)
            ph.wait("vector", te)
            ph.wait("vector", dlo, self._spill_guard)

            def ev(v, mt=mt):
                ins = None
                for ci in range(4):
                    ins = v.tensor_copy(self.stg3[:, ci * 512:(ci + 1) * 512],
                                        self.bank(ci))
                return ins
            ph.inc("vector", ve, ev)
            ph.wait("sync", ve)
            ph.dma("sync", dlo, self.sc["g"][mt * 128:(mt + 1) * 128, :], self.stg3[:])
            self._spill_guard = dlo.v



def _invm_i32(inv):
    arr = np.ascontiguousarray(np.asarray(inv)).reshape(-1).view(np.int32)
    out = np.zeros((1, 2 * E), np.int32)
    if arr.size == E:
        out[0, 0::2] = arr
    else:
        out[0, :] = arr
    return out


def host_inputs(inputs):
    import ml_dtypes
    x = np.asarray(inputs["L0_attn_x"])
    zrow = np.zeros((1, D), dtype=np.float32)

    def tokslab(b, t0):
        if t0 == 0:
            return np.concatenate([zrow, np.asarray(x[b, 0:T])], axis=0)
        return np.asarray(x[b, t0 - 1:t0 + T])

    common = {
        "wr": np.ascontiguousarray(inputs["L0_attn_W_r"], np.float32),
        "wk": np.ascontiguousarray(inputs["L0_attn_W_k"], np.float32),
        "wv": np.ascontiguousarray(inputs["L0_attn_W_v"], np.float32),
        "wo": np.ascontiguousarray(inputs["L0_attn_W_o"], np.float32),
        "w1": np.ascontiguousarray(inputs["L0_attn_w1"], np.float32),
        "w2": np.ascontiguousarray(inputs["L0_attn_w2"], np.float32),
        "a1": np.ascontiguousarray(inputs["L0_attn_a1"], np.float32),
        "a2": np.ascontiguousarray(inputs["L0_attn_a2"], np.float32),
        "g1": np.ascontiguousarray(inputs["L0_attn_g1"], np.float32),
        "g2": np.ascontiguousarray(inputs["L0_attn_g2"], np.float32),
        "gwT": np.ascontiguousarray(np.asarray(inputs["L0_ffn_gate_w"]).T, np.float32),
        "muskT": np.ascontiguousarray(np.transpose(np.asarray(inputs["L0_ffn_mask_up_w"]), (0, 2, 1)), np.float32),
        "gpw": np.ascontiguousarray(np.transpose(np.asarray(inputs["L0_ffn_gate_proj_w"]), (0, 2, 1)), np.float32),
        "upw": np.ascontiguousarray(np.transpose(np.asarray(inputs["L0_ffn_up_proj_w"]), (0, 2, 1)), np.float32),
        "dpw": np.ascontiguousarray(np.transpose(np.asarray(inputs["L0_ffn_down_proj_w"]), (0, 2, 1)), np.float32),
        "invm": _invm_i32(inputs["L0_ffn_inv_map"]),
    }
    vecs = np.zeros((D, NVEC), dtype=np.float32)
    for col, key in [(VC.x_r, "L0_attn_x_r"), (VC.x_w, "L0_attn_x_w"),
                     (VC.x_k, "L0_attn_x_k"), (VC.x_v, "L0_attn_x_v"),
                     (VC.x_a, "L0_attn_x_a"), (VC.x_g, "L0_attn_x_g"),
                     (VC.w0, "L0_attn_w0"), (VC.a0, "L0_attn_a0"),
                     (VC.k_k, "L0_attn_k_k"), (VC.k_a, "L0_attn_k_a")]:
        vecs[:, col] = np.asarray(inputs[key]).reshape(D)
    vecs[:, VC.r_k] = np.asarray(inputs["L0_attn_r_k"]).reshape(D)
    vecs[:, VC.gn_s] = np.asarray(inputs["L0_attn_gn_scale"]).reshape(D)
    vecs[:, VC.gn_o] = np.asarray(inputs["L0_attn_gn_offset"]).reshape(D)
    vecs[:, VC.negw0] = -vecs[:, VC.w0]
    vecs[:, VC.neghalf] = -0.5
    vecs[:, VC.one] = 1.0
    common["vecs"] = vecs

    eye = np.eye(128, dtype=np.float32)
    common["ident32"] = eye
    common["identbf"] = eye.astype(ml_dtypes.bfloat16)

    def dual(m):
        z = np.zeros((128, 128), np.float32)
        z[:N, :N] = m
        z[N:, N:] = m
        return z
    tril_s = np.tril(np.ones((N, N), np.float32), -1)
    tril_i = np.tril(np.ones((N, N), np.float32))
    d16 = np.zeros((N, N), np.float32)
    for i_ in range(0, N, 16):
        d16[i_:i_ + 16, i_:i_ + 16] = 1.0
    d16s = d16 * tril_s
    eye = np.eye(N, dtype=np.float32)
    qab = lambda m: np.concatenate([np.tile(m, (2, 1)), np.tile(m.T, (2, 1))], axis=1)
    masks = np.concatenate(
        [qab(tril_s), qab(d16s), np.tile(tril_s.T, (2, 2)), np.tile(tril_i.T, (2, 2)),
         np.concatenate([np.tile(eye, (2, 1)), np.tile(eye, (2, 1))], axis=1),
         dual(eye), np.ones((128, 128), np.float32)],
        axis=1)
    common["dmasks"] = masks
    common["dmasksbf"] = masks.astype(ml_dtypes.bfloat16)
    seg = np.ones((128, T), np.float32)
    seg[:, 0::L] = 0.0
    common["segm"] = seg
    hind = np.zeros((NPT, 128, H), np.float32)
    eind = np.zeros((NPT, H, 128), np.float32)
    for pt in range(NPT):
        for j in range(128):
            h = (pt * 128 + j) // N
            hind[pt, j, h] = 1.0
            eind[pt, h, j] = 1.0
    common["hind"] = hind.astype(ml_dtypes.bfloat16)
    common["eind"] = eind.astype(ml_dtypes.bfloat16)
    gi = np.zeros((E, G), np.float32)
    ge = np.zeros((G, E), np.float32)
    for x_ in range(E):
        gi[x_, x_ // GS] = 1.0
        ge[x_ // GS, x_] = 1.0
    common["gind"] = gi
    common["gexp"] = ge
    common["ones1"] = np.ones((1, 128), np.float32)
    common["iota"] = np.arange(128, dtype=np.float32).reshape(128, 1)

    in_maps = []
    for c in range(8):
        b_, p = c // 2, c % 2
        m = dict(common)
        m["xm"] = np.ascontiguousarray(tokslab(b_, p * T), dtype=np.float32)
        m["xp"] = (np.ascontiguousarray(tokslab(b_, 0), dtype=np.float32)
                   if p == 1 else np.zeros((TL, D), np.float32))
        in_maps.append(m)
    return in_maps


def kernel(**inputs):
    in_maps = host_inputs(inputs)
    bld = Builder(DEBUG_TAPS)
    bld.build()
    res = run_bass_kernel_spmd(bld.nc, in_maps, core_ids=list(range(8)))
    outs = [res.results[c]["out"] for c in range(8)]
    full = np.stack([np.concatenate([outs[2 * b], outs[2 * b + 1]], axis=0)
                     for b in range(4)], axis=0)
    return full


# version 03
# speedup vs baseline: 1.3278x; 1.3278x over previous
import numpy as np
from contextlib import ExitStack

import concourse.bass as bass
import concourse.mybir as mybir
from concourse.bass_utils import run_bass_kernel_spmd

F32 = mybir.dt.float32
BF16 = mybir.dt.bfloat16
I32 = mybir.dt.int32
AF = mybir.ActivationFunctionType
ALU = mybir.AluOpType
AX = mybir.AxisListType

D = 2048
T = 2048
H = 32
N = 64
L = 64
NCH = T // L
NPT = D // 128
NHP = H // 2
E, G, GS, FI = 64, 8, 8, 1408
TL = T + 1

NVEC = 16


class VC:
    (x_r, x_w, x_k, x_v, x_a, x_g, w0, a0, k_k, k_a, r_k, gn_s, gn_o,
     negw0, neghalf, one) = range(NVEC)


NDV = 8


class DV:
    m_r, m_w, m_k, m_v, m_a, m_g, one_m_ka, spare = range(NDV)


MIX = {"r": (VC.x_r, DV.m_r), "w": (VC.x_w, DV.m_w), "k": (VC.x_k, DV.m_k),
       "v": (VC.x_v, DV.m_v), "a": (VC.x_a, DV.m_a), "g": (VC.x_g, DV.m_g)}

NMASK = 7
(MK_QAB_S, MK_QAB_D16, MK_STRICT_T22, MK_INCL_T22, MK_QI, MK_I2, MK_ONES) = range(NMASK)

DEBUG_TAPS = {}


class Sem:
    def __init__(self, ctx, nc, name):
        self.h = ctx.enter_context(nc.semaphore(name))
        self.v = 0


class Phase:
    ENGINES = ("tensor", "vector", "scalar", "sync", "gpsimd")

    def __init__(self, bld):
        self.bld = bld
        self.ops = {e: [] for e in self.ENGINES}

    def on(self, eng, fn):
        self.ops[eng].append(fn)

    def wait(self, eng, sem, val=None):
        v = sem.v if val is None else val
        if v > 0:
            self.on(eng, lambda e, h=sem.h, v=v: e.wait_ge(h, v))

    def inc(self, eng, sem, fn, n=1):
        sem.v += n
        self.on(eng, lambda e, f=fn, h=sem.h, n=n: f(e).then_inc(h, n))

    def dma(self, eng, sem, out, in_, **kw):
        sem.v += 16
        self.on(eng, lambda e, o=out, i=in_, h=sem.h, kw=kw:
                e.dma_start(out=o, in_=i, **kw).then_inc(h, 16))

    def play(self, block):
        for name in self.ENGINES:
            ops = self.ops[name]
            def body(e, ops=ops):
                for f in ops:
                    f(e)
            getattr(block, name)(body)

    def barrier(self):
        s = self.bld.sems["ph"]
        for nm in ["dli", "dlw", "dlo"]:
            self.wait("sync", self.bld.sems[nm])
            self.wait("gpsimd", self.bld.sems[nm])
        for eng in self.ENGINES:
            self.inc(eng, s, lambda e: e.nop())
        tgt = s.v
        for eng in self.ENGINES:
            self.wait(eng, s, tgt)


class Builder:
    def __init__(self, debug_taps=None, phb_level=9, phc_level=9, phd_level=9):
        self.nc = bass.Bass("TRN2", target_bir_lowering=False)
        self.taps_spec = dict(debug_taps or {})
        self.taps = {}
        self.phb_level = phb_level
        self.phc_level = phc_level
        self.phd_level = phd_level
        self._wconv_units = []
        self._wconv_i = 0

    def declare_dram(self):
        nc = self.nc
        dt = nc.dram_tensor
        i = lambda n, s, d=F32: dt(n, s, d, kind="ExternalInput").ap()
        self.xm = i("xm", [TL, D]); self.xp = i("xp", [TL, D])
        self.wr = i("wr", [D, D]); self.wk = i("wk", [D, D])
        self.wv = i("wv", [D, D]); self.wo = i("wo", [D, D])
        self.w1 = i("w1", [D, 64]); self.w2 = i("w2", [64, D])
        self.a1 = i("a1", [D, 64]); self.a2 = i("a2", [64, D])
        self.g1 = i("g1", [D, 128]); self.g2 = i("g2", [128, D])
        self.gwT = i("gwT", [D, E])
        self.muskT = i("muskT", [G, GS, D])
        self.gpw = i("gpw", [G, D, FI]); self.upw = i("upw", [G, D, FI])
        self.dpw = i("dpw", [G, FI, D])
        self.invm = i("invm", [1, 2 * E], I32)
        self.diota = i("iota", [128, 1])
        self.vecs = i("vecs", [D, NVEC])
        self.dident32 = i("ident32", [128, 128])
        self.didentbf = i("identbf", [128, 128], BF16)
        self.dmasks = i("dmasks", [128, 128 * NMASK])
        self.dmasksbf = i("dmasksbf", [128, 128 * NMASK], BF16)
        self.dsegm = i("segm", [128, T], BF16)
        self.dhind = i("hind", [NPT, 128, H], BF16)
        self.deind = i("eind", [NPT, H, 128], BF16)
        self.dgind = i("gind", [E, G])
        self.dgexp = i("gexp", [G, E])
        self.dones = i("ones1", [1, 128])

        self.out = dt("out", [T, D], F32, kind="ExternalOutput").ap()
        for name, (shape, dtp) in self.taps_spec.items():
            self.taps[name] = dt("tap_" + name, shape, dtp, kind="ExternalOutput").ap()

        s = lambda n, sh, d=BF16: dt(n, sh, d).ap()
        self.sc = {}
        for nm in ["r", "lw", "k2", "a", "b", "g", "kraw", "ag"]:
            self.sc[nm] = s("sc_" + nm, [D, T])
        for nm in ["lw", "k2", "a", "b", "kraw", "ag"]:
            self.sc["p" + nm] = s("sc_p" + nm, [D, T])
        self.sc["v"] = s("sc_v", [NHP, T, 128])
        self.sc["pv"] = s("sc_pv", [NHP, T, 128])
        self.sc["x32"] = s("sc_x32", [16, 128, D], F32)
        self.sc["y"] = s("sc_y", [D, T])
        self.sc["beta"] = s("sc_beta", [H, T], F32)
        self.sc["xmid"] = s("sc_xmid", [D, T], F32)
        self.sc["ln2"] = s("sc_ln2", [2, T], F32)
        self.sc["x2"] = s("sc_x2", [D, T])
        self.sc["gpb"] = s("sc_gpb", [G, D, FI])
        self.sc["upb"] = s("sc_upb", [G, D, FI])
        self.sc["dpb"] = s("sc_dpb", [G, FI, D])

    def build(self):
        self.declare_dram()
        self._wconv_units = self.wconv_units()
        nc = self.nc
        with ExitStack() as ctx:
            en = ctx.enter_context
            sb = lambda n, sh, d: en(nc.sbuf_tensor(n, sh, d))
            self.vec = sb("vecS", [128, NPT * NVEC], F32)
            self.dvec = sb("dvecS", [128, NPT * NDV], F32)
            self.id32 = sb("id32S", [128, 128], F32)
            self.idbf = sb("idbfS", [128, 128], BF16)
            self.masks = sb("masksS", [128, 128 * NMASK], F32)
            self.masksbf = self.masks
            self.hind = sb("hindS", [128, NPT * H], BF16)
            self.eind = sb("eindS", [H, NPT * 128], BF16)
            self.gind = sb("gindS", [E, G], F32)
            self.gexp = sb("gexpS", [G, E], F32)
            self.ones1 = sb("ones1S", [1, 128], F32)
            self.onesbf = sb("onesbfS", [1, 128], BF16)
            self.onesc = sb("onescS", [128, 1], BF16)
            self.cst = sb("cstS", [128, 8], F32)
            self.bcol = sb("bcolS", [128, NPT], BF16)
            self.clb = sb("clbS", [64, 2 * NCH], F32)
            self.pmat = sb("pmatS", [E, E], BF16)
            self.gnv = sb("gnvS", [64, 64], F32)
            self.iotac = sb("iotaS", [128, 1], F32)
            self.invs = sb("invsS", [1, 2 * E], I32)
            self.onescb = sb("onescbS", [128, 1], BF16)
            self.state = sb("stateS", [64, NHP * 128], F32)
            self.statebf = sb("statebfS", [64, NHP * 128], BF16)
            self.xln = sb("xlnS", [128, NPT * TL], BF16)
            self.mx = sb("mxS", [128, NPT * T], BF16)
            self.wbig = self.mx
            self.wsm = sb("wsmS", [128, D], BF16)
            self.stg = sb("stgS", [128, D], BF16)
            self.stg2 = sb("stg2S", [128, D], BF16)
            self.stg3 = sb("stg3S", [128, D], BF16)
            self.f32a = sb("f32aS", [128, D], F32)
            self.f32b = sb("f32bS", [128, D], F32)
            self.smallb = sb("smallbS", [64, T], BF16)
            self.segf = sb("segfS", [64, 1024], F32)
            self.lwf = sb("lwfS", [64, 1024], F32)
            self.invnb = self.smallb
            self.wq = sb("wqS", [128, 16 * 256], BF16)
            self.psF = en(nc.psum_tensor("psF", [128, 3584], F32))
            self.psH = en(nc.psum_tensor("psH", [128, 1024], BF16))
            self.sems = {}
            for s_ in ["dli", "dlw", "dlo", "ve", "te", "ac", "gp", "ph"]:
                self.sems[s_] = Sem(ctx, nc, s_)
            self.block = en(nc.Block())
            ph = Phase(self)
            self.emit(ph)
            ph.play(self.block)

    def bank(self, b, rows=128, cols=512):
        return self.psF[0:rows, 512 * b: 512 * b + cols]

    def mask(self, i, bf=False):
        m = self.masksbf if bf else self.masks
        return m[:, i * 128:(i + 1) * 128]

    def vcol(self, pt, col):
        return self.vec[:, pt * NVEC + col: pt * NVEC + col + 1]

    def dcol(self, pt, col):
        return self.dvec[:, pt * NDV + col: pt * NDV + col + 1]

    def xln_t(self, pt, c0, w):
        return self.xln[:].rearrange("p (n c) -> p n c", c=TL)[:, pt, c0:c0 + w]

    def mx_t(self, pt, c0, w):
        return self.mx[:].rearrange("p (n c) -> p n c", c=T)[:, pt, c0:c0 + w]

    def emit(self, ph):
        self.emit_consts(ph)
        ph.barrier()
        self.emit_phaseA(ph, prefix=True)
        ph.barrier()
        self.emit_phaseB(ph, prefix=True)
        ph.barrier()
        self.emit_phaseA(ph, prefix=False)
        ph.barrier()
        self.emit_phaseB(ph, prefix=False)
        ph.barrier()
        if self.phc_level >= 1:
            self.emit_phaseC(ph)
            ph.barrier()
        if self.phd_level >= 1 and self.phc_level >= 1:
            self.emit_phaseD(ph)
            ph.barrier()
        else:
            self.emit_stub_out(ph)
        self.emit_taps(ph)

    def emit_consts(self, ph):
        dli, ve = self.sems["dli"], self.sems["ve"]
        for o, i in [
            (self.id32[:], self.dident32), (self.idbf[:], self.didentbf),
            (self.masks[:], self.dmasks),
            (self.gind[:], self.dgind), (self.gexp[:], self.dgexp),
            (self.ones1[:], self.dones),
            (self.iotac[:], self.diota), (self.invs[:], self.invm),
        ]:
            ph.dma("sync", dli, o, i)
        ph.dma("sync", dli, self.gnv[:, 0:32],
               self.vecs[:, VC.gn_s:VC.gn_s + 1].rearrange("(h n) o -> n (h o)", n=64),
               allow_slow_non_contiguous=True)
        ph.dma("sync", dli, self.gnv[:, 32:64],
               self.vecs[:, VC.gn_o:VC.gn_o + 1].rearrange("(h n) o -> n (h o)", n=64),
               allow_slow_non_contiguous=True)
        for pt in range(NPT):
            ph.dma("sync", dli, self.vec[:, pt * NVEC:(pt + 1) * NVEC],
                   self.vecs[pt * 128:(pt + 1) * 128, :])
            ph.dma("sync", dli, self.hind[:, pt * H:(pt + 1) * H], self.dhind[pt])
            ph.dma("sync", dli, self.eind[:, pt * 128:(pt + 1) * 128], self.deind[pt])
        ph.wait("vector", dli)

        def derive(v):
            for pt in range(NPT):
                for name, (vc_col, dv_col) in MIX.items():
                    v.tensor_scalar(self.dcol(pt, dv_col), self.vcol(pt, vc_col),
                                    -1.0, 1.0, ALU.mult, ALU.add)
                v.tensor_scalar(self.dcol(pt, DV.one_m_ka), self.vcol(pt, VC.k_a),
                                -1.0, 1.0, ALU.mult, ALU.add)
            v.tensor_copy(self.onesbf[:], self.ones1[:])
            v.memset(self.onesc[:], 1.0)
            v.memset(self.onescb[:], 1.0)
            v.memset(self.segf[:, :], 1.0)
            v.memset(self.segf[:, :].rearrange("p (n c) -> p n c", c=L)[:, :, 0], 0.0)
            v.memset(self.cst[:, 0:1], -0.5)
            v.memset(self.cst[:, 1:2], -1.0)
            v.memset(self.cst[:, 2:3], 1.0 / D)
            v.memset(self.cst[:, 3:4], 1e-5)
            v.memset(self.cst[:, 4:5], 1e-20)
            v.memset(self.state[:], 0.0)
            v.memset(self.statebf[:], 0.0)
            return v.tensor_copy(self.f32a[0:1, 0:E],
                                 self.invs[:, :].rearrange("p (n c) -> p n c", c=2)[:, :, 0])
        ph.inc("vector", ve, derive)

        te = self.sems["te"]
        ph.wait("tensor", ve)

        def pmm(t):
            return t.matmul(self.psF[0:E, 0:E], self.ones1[0:1, 0:E],
                            self.f32a[0:1, 0:E], start=True, stop=True)
        ph.inc("tensor", te, pmm)
        ph.wait("vector", te)

        def pbuild(v):
            return v.tensor_scalar(self.pmat[:, :], self.psF[0:E, 0:E],
                                   self.iotac[0:E, 0:1], None, ALU.is_equal)
        ph.inc("vector", ve, pbuild)

    def emit_stub_out(self, ph):
        ve, dlo = self.sems["ve"], self.sems["dlo"]

        def z(v):
            return v.memset(self.f32a[:], 0.0)
        ph.inc("vector", ve, z)
        ph.wait("sync", ve)
        for g in range(16):
            ph.dma("sync", dlo, self.out[g * 128:(g + 1) * 128, :], self.f32a[:])
        ph.wait("sync", dlo)

    def emit_taps(self, ph):
        dlo = self.sems["dlo"]
        for name, ap in self.taps.items():
            if name == "stateF":
                ph.dma("sync", dlo, ap, self.state[:])
                continue
            if name == "softT":
                ph.dma("sync", dlo, ap, self.wsm[0:64, 0:2048])
                continue
            if name == "sgT":
                ph.dma("sync", dlo, ap, self.state[:].bitcast(BF16)[0:G, 0:2048])
                continue
            if name in self.sc:
                ph.dma("sync", dlo, ap, self.sc[name])
        ph.wait("sync", dlo)

    def ctile(self, mt):
        return self.xln[:, mt * TL: mt * TL + T]

    def mxf(self):
        return self.mx[:].bitcast(F32)

    def emit_phaseC(self, ph):
        dli, dlw, dlo = self.sems["dli"], self.sems["dlw"], self.sems["dlo"]
        ve, te, ac = self.sems["ve"], self.sems["te"], self.sems["ac"]
        sc = self.sc

        for kt in range(NPT):
            ph.wait("sync", ve)
            ph.dma("sync", dli, self.stg[:], sc["y"][kt * 128:(kt + 1) * 128, :])
            ph.dma("sync", dli, self.stg2[:], sc["g"][kt * 128:(kt + 1) * 128, :])
            ph.wait("vector", dli)

            def om(v, kt=kt):
                return v.tensor_tensor(self.mx[:, kt * T:(kt + 1) * T],
                                       self.stg[:], self.stg2[:], ALU.mult)
            ph.inc("vector", ve, om)

        for mt in range(16):
            base = (mt % 2) * 16
            ph.wait("gpsimd", te)
            for kt in range(16):
                ph.dma("gpsimd", dlw,
                       self.wq[:, (base + kt) * 128:(base + kt + 1) * 128],
                       self.wo[kt * 128:(kt + 1) * 128, mt * 128:(mt + 1) * 128])
            ph.wait("sync", ve)
            ph.dma("sync", dli,
                   self.f32a[:].rearrange("p (g c) -> p g c", c=128),
                   sc["x32"][:, :, mt * 128:(mt + 1) * 128].rearrange(
                       "g r t -> r g t"))
            ph.wait("tensor", dlw)
            ph.wait("tensor", ve)
            ph.wait("tensor", dlo)

            def mmo(t, base=base):
                ins = None
                for kt in range(16):
                    for fq in range(4):
                        ins = t.matmul(
                            self.psF[:, fq * 512:(fq + 1) * 512],
                            self.wq[:, (base + kt) * 128:(base + kt + 1) * 128],
                            self.mx[:, kt * T + fq * 512:kt * T + (fq + 1) * 512],
                            start=(kt == 0), stop=(kt == 15))
                return ins
            ph.inc("tensor", te, mmo)
            ph.wait("vector", te)
            ph.wait("vector", dli)
            ph.wait("vector", dlo)

            def evc(v, mt=mt):
                v.tensor_tensor(self.f32b[:], self.psF[:, 0:2048], self.f32a[:],
                                ALU.add)
                return v.tensor_copy(self.ctile(mt), self.f32b[:])
            ph.inc("vector", ve, evc)
            ph.wait("sync", ve)
            ph.dma("sync", dlo, sc["xmid"][mt * 128:(mt + 1) * 128, :], self.f32b[:])

        mxf = self.mxf()
        for ci in range(4):
            cs = slice(ci * 512, (ci + 1) * 512)
            ph.wait("tensor", ve)

            def mmx(t, cs=cs):
                ins = None
                for mt in range(16):
                    ins = t.matmul(self.psF[0:1, 2048:2560], self.onescb[:],
                                   self.ctile(mt)[:, cs],
                                   start=(mt == 0), stop=(mt == 15))
                return ins
            ph.inc("tensor", te, mmx)
            for bq in range(4):
                ph.wait("scalar", te)

                def sq(s, bq=bq, cs=cs):
                    ins = None
                    for j in range(4):
                        ins = s.activation(self.stg3[:, j * 512:(j + 1) * 512],
                                           self.ctile(bq * 4 + j)[:, cs], AF.Square)
                    return ins
                ph.inc("scalar", ac, sq)
                ph.wait("tensor", ac)

                def mm2(t, bq=bq):
                    ins = None
                    for j in range(4):
                        ins = t.matmul(self.psF[0:1, 2560:3072], self.onescb[:],
                                       self.stg3[:, j * 512:(j + 1) * 512],
                                       start=(bq == 0 and j == 0),
                                       stop=(bq == 3 and j == 3))
                    return ins
                ph.inc("tensor", te, mm2)
            ph.wait("vector", te)

            def lnrow(v, ci=ci):
                v.tensor_scalar(mxf[0:1, ci * 512:(ci + 1) * 512],
                                self.psF[0:1, 2048:2560], self.cst[0:1, 2:3],
                                None, ALU.mult)
                v.tensor_scalar(mxf[0:1, 6144:6656],
                                self.psF[0:1, 2560:3072], self.cst[0:1, 2:3],
                                None, ALU.mult)
                v.tensor_tensor(mxf[0:1, 6656:7168],
                                mxf[0:1, ci * 512:(ci + 1) * 512],
                                mxf[0:1, ci * 512:(ci + 1) * 512], ALU.mult)
                return v.tensor_sub(mxf[0:1, 7168:7680], mxf[0:1, 6144:6656],
                                    mxf[0:1, 6656:7168])
            ph.inc("vector", ve, lnrow)
            ph.wait("scalar", ve)

            def rstd(s, ci=ci):
                s.activation(mxf[0:1, 6144:6656], mxf[0:1, 7168:7680], AF.Ln,
                             bias=self.cst[0:1, 3:4])
                return s.activation(mxf[0:1, 2048 + ci * 512:2048 + (ci + 1) * 512],
                                    mxf[0:1, 6144:6656], AF.Exp,
                                    scale=self.cst[0:1, 0:1])
            ph.inc("scalar", ac, rstd)

        ph.wait("tensor", ac)

        def mmub(t):
            ins = None
            for fq in range(4):
                ins = t.matmul(self.psF[:, fq * 512:(fq + 1) * 512],
                               self.ones1[0:1, :], mxf[0:1, fq * 512:(fq + 1) * 512],
                               start=True, stop=True)
            return ins
        ph.inc("tensor", te, mmub)
        ph.wait("vector", te)

        def evmu(v):
            return v.tensor_copy(mxf[:, 2048 * 2:2048 * 3], self.psF[:, 0:2048])
        ph.inc("vector", ve, evmu)
        ph.wait("tensor", ve)

        def mmrs(t):
            ins = None
            for fq in range(4):
                ins = t.matmul(self.psF[:, fq * 512:(fq + 1) * 512],
                               self.ones1[0:1, :],
                               mxf[0:1, 2048 + fq * 512:2048 + (fq + 1) * 512],
                               start=True, stop=True)
            return ins
        ph.inc("tensor", te, mmrs)
        ph.wait("vector", te)

        def evrs(v):
            return v.tensor_copy(mxf[:, 2048 * 3:2048 * 4], self.psF[:, 0:2048])
        ph.inc("vector", ve, evrs)

        for mt in range(16):
            ph.wait("vector", dlo)

            def x2t(v, mt=mt):
                v.tensor_sub(self.stg[:], self.ctile(mt), mxf[:, 4096:6144])
                return v.tensor_tensor(self.ctile(mt), self.stg[:],
                                       mxf[:, 6144:8192], ALU.mult)
            ph.inc("vector", ve, x2t)
            ph.wait("sync", ve)
            ph.dma("sync", dlo, sc["x2"][mt * 128:(mt + 1) * 128, :], self.ctile(mt))

        self._gating(ph)

    def dve_fence(self, ph):
        ve, ac = self.sems["ve"], self.sems["ac"]
        ph.wait("scalar", ve)
        ph.inc("scalar", ac, lambda s: s.activation(self.cst[0:1, 7:8],
                                                    self.cst[0:1, 0:1],
                                                    AF.Identity))
        ph.wait("vector", ac)

    def _gating(self, ph):
        dli, dlw = self.sems["dli"], self.sems["dlw"]
        ve, te, ac = self.sems["ve"], self.sems["te"], self.sems["ac"]
        smf = self.smallb[:].bitcast(F32)
        wsmbf = self.wsm[:].bitcast(BF16)
        for kt in range(16):
            ph.dma("gpsimd", dlw, self.wq[:, kt * 64:(kt + 1) * 64],
                   self.gwT[kt * 128:(kt + 1) * 128, :])
        ph.wait("tensor", dlw)
        ph.wait("tensor", ve)

        for fq in range(4):
            ph.wait("tensor", ve)

            def mlgq(t, fq=fq):
                ins = None
                for kt in range(16):
                    ins = t.matmul(self.psF[0:E, 0:512],
                                   self.wq[:, kt * 64:(kt + 1) * 64],
                                   self.ctile(kt)[:, fq * 512:(fq + 1) * 512],
                                   start=(kt == 0), stop=(kt == 15))
                return ins
            ph.inc("tensor", te, mlgq)
            ph.wait("vector", te)

            def evlg(v, fq=fq):
                return v.tensor_copy(self.stg[0:E, fq * 512:(fq + 1) * 512],
                                     self.psF[0:E, 0:512])
            ph.inc("vector", ve, evlg)

        for grp in range(2):
            ph.wait("tensor", ve)

            def trl(t, grp=grp):
                ins = None
                for k in range(16):
                    cc = grp * 16 + k
                    ins = t.transpose(self.psH[0:64, k * 64:(k + 1) * 64],
                                      self.stg[0:E, cc * 64:(cc + 1) * 64],
                                      self.idbf[0:64, 0:64])
                return ins
            ph.inc("tensor", te, trl)
            ph.wait("vector", te)

            def evtl(v, grp=grp):
                return v.tensor_copy(self.stg2[0:64, grp * 1024:(grp + 1) * 1024],
                                     self.psH[0:64, 0:1024])
            ph.inc("vector", ve, evtl)

        def v3t(ap):
            return ap.rearrange("p (n c) -> p n c", c=64)

        ph.wait("vector", ve)

        def smax1(v):
            return v.reduce_max(smf[:, 0:32].unsqueeze(2),
                                v3t(self.stg2[0:64, 0:2048]), axis=AX.X)
        ph.inc("vector", ve, smax1)
        self.dve_fence(ph)

        def smsub(v):
            return v.tensor_tensor(
                v3t(self.stg[0:64, 0:2048]), v3t(self.stg2[0:64, 0:2048]),
                smf[:, 0:32].unsqueeze(2).broadcast_to([64, 32, 64]), ALU.subtract)
        ph.inc("vector", ve, smsub)
        ph.wait("scalar", ve)

        def smexp(s):
            return s.activation(self.stg3[0:64, 0:2048], self.stg[0:64, 0:2048],
                                AF.Exp)
        ph.inc("scalar", ac, smexp)
        ph.wait("vector", ac)

        def smsum(v):
            return v.reduce_sum(smf[:, 32:64].unsqueeze(2),
                                v3t(self.stg3[0:64, 0:2048]), axis=AX.X)
        ph.inc("vector", ve, smsum)
        self.dve_fence(ph)

        def smrcp(v):
            return v.reciprocal(smf[:, 64:96], smf[:, 32:64])
        ph.inc("vector", ve, smrcp)
        self.dve_fence(ph)

        def smmul(v):
            return v.tensor_tensor(
                v3t(self.stg[0:64, 0:2048]), v3t(self.stg3[0:64, 0:2048]),
                smf[:, 64:96].unsqueeze(2).broadcast_to([64, 32, 64]), ALU.mult)
        ph.inc("vector", ve, smmul)
        self.dve_fence(ph)

        def top8(v):
            ins = None
            for cc in range(NCH):
                ins = v.max(out=self.stg2[0:64, cc * 8:(cc + 1) * 8],
                            in_=self.stg[0:64, cc * 64:(cc + 1) * 64])
            return ins
        ph.inc("vector", ve, top8)
        self.dve_fence(ph)

        def thrx(v):
            return v.tensor_copy(smf[:, 96:128],
                                 self.stg2[0:64, 0:256].rearrange(
                                     "p (n c) -> p n c", c=8)[:, :, 7])
        ph.inc("vector", ve, thrx)
        self.dve_fence(ph)

        def wemask(v):
            v.tensor_tensor(v3t(self.stg3[0:64, 0:2048]),
                            v3t(self.stg[0:64, 0:2048]),
                            smf[:, 96:128].unsqueeze(2).broadcast_to([64, 32, 64]),
                            ALU.is_ge)
            return v.tensor_tensor(self.stg2[0:64, 0:2048], self.stg[0:64, 0:2048],
                                   self.stg3[0:64, 0:2048], ALU.mult)
        ph.inc("vector", ve, wemask)

        for grp in range(2):
            ph.wait("tensor", ve)

            def trw(t, grp=grp):
                ins = None
                for k in range(16):
                    cc = grp * 16 + k
                    ins = t.transpose(self.psH[0:64, k * 64:(k + 1) * 64],
                                      self.stg2[0:64, cc * 64:(cc + 1) * 64],
                                      self.idbf[0:64, 0:64])
                return ins
            ph.inc("tensor", te, trw)
            ph.wait("vector", te)

            def evtw(v, grp=grp):
                return v.tensor_copy(self.stg[0:64, grp * 1024:(grp + 1) * 1024],
                                     self.psH[0:64, 0:1024])
            ph.inc("vector", ve, evtw)

        ph.wait("tensor", ve)

        def mfl(t):
            ins = None
            for fq in range(4):
                ins = t.matmul(self.psF[0:E, 512:1024], self.pmat[:, :],
                               self.stg[0:64, fq * 512:(fq + 1) * 512],
                               start=True, stop=True) if fq == 0 else                     t.matmul(self.psF[0:E, 512 + 0:1024][:, 0:512], self.pmat[:, :],
                             self.stg[0:64, fq * 512:(fq + 1) * 512],
                             start=True, stop=True)
                ins = ins
            return ins
        for fq in range(4):
            ph.wait("tensor", ve)

            def mflq(t, fq=fq):
                return t.matmul(self.psF[0:E, 512:1024], self.pmat[:, :],
                                self.stg[0:64, fq * 512:(fq + 1) * 512],
                                start=True, stop=True)
            ph.inc("tensor", te, mflq)
            ph.wait("vector", te)

            def evfl(v, fq=fq):
                return v.tensor_copy(self.stg2[0:64, fq * 512:(fq + 1) * 512],
                                     self.psF[0:E, 512:1024])
            ph.inc("vector", ve, evfl)

        ph.wait("vector", ve)

        def gbf2(v):
            v.tensor_copy(self.wq[0:64, 1024:1032], self.gind[:, :])
            return v.tensor_scalar(self.stg3[0:64, 0:2048],
                                   self.stg2[0:64, 0:2048], 0.0, None, ALU.is_gt)
        ph.inc("vector", ve, gbf2)
        ph.wait("scalar", ve)

        def efx(s):
            return s.activation(self.stg[0:64, 0:2048], self.stg2[0:64, 0:2048],
                                AF.Exp)
        ph.inc("scalar", ac, efx)
        ph.wait("vector", ac)

        def efm(v):
            return v.tensor_tensor(self.stg[0:64, 0:2048], self.stg[0:64, 0:2048],
                                   self.stg3[0:64, 0:2048], ALU.mult) if False else                 v.tensor_tensor(self.smallb[0:64, 0:2048], self.stg[0:64, 0:2048],
                                self.stg3[0:64, 0:2048], ALU.mult)
        ph.inc("vector", ve, efm)

        for fq in range(4):
            ph.wait("tensor", ve)

            def mdn(t, fq=fq):
                t.matmul(self.psF[0:G, 1024:1536], self.wq[0:64, 1024:1032],
                         self.smallb[0:64, fq * 512:(fq + 1) * 512],
                         start=True, stop=True)
                return t.matmul(self.psF[0:G, 1536:2048], self.wq[0:64, 1024:1032],
                                self.stg2[0:64, fq * 512:(fq + 1) * 512],
                                start=True, stop=True)
            ph.inc("tensor", te, mdn)
            ph.wait("vector", te)

            def evdn(v, fq=fq):
                dn = (self.segf if fq < 2 else self.lwf)[0:G,
                                                         (fq % 2) * 512:
                                                         (fq % 2) * 512 + 512]
                v.tensor_scalar(dn, self.psF[0:G, 1024:1536],
                                self.cst[0:G, 4:5], None, ALU.max)
                return v.tensor_copy(
                    self.state[:].bitcast(BF16)[0:G, fq * 512:(fq + 1) * 512],
                    self.psF[0:G, 1536:2048])
            ph.inc("vector", ve, evdn)
        ph.wait("vector", ve)

        def rcpd(v):
            ins = None
            for fq in range(4):
                dn = (self.segf if fq < 2 else self.lwf)[0:G, (fq % 2) * 512:
                                                         (fq % 2) * 512 + 512]
                ins = v.reciprocal(self.mxf()[0:G, fq * 512:(fq + 1) * 512], dn)
            return ins
        ph.inc("vector", ve, rcpd)

        for fq in range(4):
            ph.wait("tensor", ve)

            def mrcp(t, fq=fq):
                return t.matmul(self.psF[0:E, 1024:1536], self.gexp[:, :],
                                self.mxf()[0:G, fq * 512:(fq + 1) * 512],
                                start=True, stop=True)
            ph.inc("tensor", te, mrcp)
            ph.wait("vector", te)

            def evsf(v, fq=fq):
                return v.tensor_tensor(
                    self.wsm[0:64, fq * 512:(fq + 1) * 512],
                    self.smallb[0:64, fq * 512:(fq + 1) * 512],
                    self.psF[0:E, 1024:1536], ALU.mult)
            ph.inc("vector", ve, evsf)

    def pieces(self):
        mxf = self.mxf()
        s2 = self.stg2[:].bitcast(F32)
        s3 = self.stg3[:].bitcast(F32)
        wf = self.wq[:].bitcast(F32)
        ps = [mxf[:, 11008 + i * 512: 11008 + (i + 1) * 512] for i in range(10)]
        ps += [s2[:, 0:512], s2[:, 512:1024], s3[:, 0:512], s3[:, 512:1024],
               wf[:, 1024:1536], wf[:, 1536:2048]]
        return ps

    def wconv_units(self):
        units = []
        for g in range(G):
            for kt in range(16):
                units.append((self.sc["gpb"][g][kt * 128:(kt + 1) * 128, :],
                              self.gpw[g][kt * 128:(kt + 1) * 128, :]))
                units.append((self.sc["upb"][g][kt * 128:(kt + 1) * 128, :],
                              self.upw[g][kt * 128:(kt + 1) * 128, :]))
            for it in range(11):
                units.append((self.sc["dpb"][g][it * 128:(it + 1) * 128, :],
                              self.dpw[g][it * 128:(it + 1) * 128, :]))
        return units

    def emit_wconv(self, ph, n):
        dlw = self.sems["dlw"]
        while self._wconv_i < len(self._wconv_units) and n > 0:
            dst, src = self._wconv_units[self._wconv_i]
            ph.dma("gpsimd", dlw, dst, src)
            self._wconv_i += 1
            n -= 1

    def emit_phaseD(self, ph):
        dli, dlw, dlo = self.sems["dli"], self.sems["dlw"], self.sems["dlo"]
        ve, te, ac = self.sems["ve"], self.sems["te"], self.sems["ac"]
        sgbf = self.state[:].bitcast(BF16)
        P = self.pieces()
        if "sg0" in self.taps:
            ph.wait("sync", ve)
            ph.dma("sync", dlo, self.taps["sg0"], sgbf[0:G, 0:2048])
            ph.dma("sync", dlo, self.taps["soft0"], self.wsm[0:64, 0:2048])
            ph.dma("sync", dlo, self.taps["flat0"], self.stg2[0:64, 0:2048])
            ph.dma("sync", dlo, self.taps["efm0"], self.smallb[0:64, 0:2048])
            ph.wait("sync", dlo)
        self.emit_wconv(ph, 10 ** 9)
        xg = lambda kt: self.mx[:, 8192 + kt * 512:8192 + (kt + 1) * 512]
        xs2 = lambda kt: self.mx[:, kt * 512:(kt + 1) * 512]
        ap_ = lambda it: self.mx[:, 16384 + it * 512:16384 + (it + 1) * 512]
        gu_chunks = [(0, 4), (4, 4), (8, 3)]
        chunk_dlw = {}
        chunk_te = {}
        ccnt = [0]

        def cbuf(c, w):
            return self.xln[:, (c % 2) * 8192:(c % 2) * 8192 + w]

        def sil(i):
            return self.xln[:, 16384 + i * 512:16384 + (i + 1) * 512]

        def issue_gu(mat, g, i0, ni):
            c = ccnt[0]
            if c >= 2:
                ph.wait("gpsimd", te, chunk_te[c - 2])
            for kt in range(16):
                ph.dma("gpsimd", dlw, cbuf(c, 16 * 512)[:, kt * 512:kt * 512 + ni * 128],
                       mat[kt * 128:(kt + 1) * 128, i0 * 128:(i0 + ni) * 128])
            chunk_dlw[c] = self.sems["dlw"].v
            ccnt[0] += 1
            return c

        def issue_dn(mat, g, m0, nm):
            c = ccnt[0]
            if c >= 2:
                ph.wait("gpsimd", te, chunk_te[c - 2])
            for it in range(11):
                ph.dma("gpsimd", dlw, cbuf(c, 11 * 512)[:, it * 512:it * 512 + nm * 128],
                       mat[it * 128:(it + 1) * 128, m0 * 128:(m0 + nm) * 128])
            chunk_dlw[c] = self.sems["dlw"].v
            ccnt[0] += 1
            return c

        for tq in range(4):
            tc = slice(tq * 512, (tq + 1) * 512)
            ph.wait("sync", ve)
            ph.wait("sync", dlo)
            for kt in range(16):
                ph.dma("sync", dli, xs2(kt), self.sc["x2"][kt * 128:(kt + 1) * 128, tc])
            ph.wait("vector", dli)
            ph.wait("vector", dlo)

            def zed(v, P=P):
                ins = None
                for p in P:
                    ins = v.memset(p, 0.0)
                return ins
            ph.inc("vector", ve, zed)

            for g in range(G):
                ph.wait("gpsimd", te)
                ph.dma("gpsimd", dlw, self.wq[0:G, 0:2048], self.muskT[g])
                ph.wait("tensor", dlw)
                ph.wait("tensor", ve)

                def msg1(t, g=g, tc=tc):
                    t.matmul(self.psF[0:1, 2560:3072], self.idbf[0:G, g:g + 1],
                             sgbf[0:G, tc], start=True, stop=True)
                    return t.matmul(self.psF[0:G, 3072:3584],
                                    self.idbf[0:64, g * 8:(g + 1) * 8],
                                    self.wsm[0:64, tc], start=True, stop=True)
                ph.inc("tensor", te, msg1)
                ph.wait("vector", te)

                def evsg(v):
                    v.tensor_copy(self.stg[0:1, 512:1024], self.psF[0:1, 2560:3072])
                    return v.tensor_copy(self.stg[0:G, 0:512],
                                         self.psF[0:G, 3072:3584])
                ph.inc("vector", ve, evsg)
                ph.wait("tensor", ve)

                def msg2(t):
                    return t.matmul(self.psF[:, 2560:3072], self.onesbf[:],
                                    self.stg[0:1, 512:1024], start=True, stop=True)
                ph.inc("tensor", te, msg2)

                for mt in range(16):
                    slot = mt % 2
                    if mt >= 2:
                        ph.wait("tensor", ve)

                    def mdl(t, mt=mt, slot=slot):
                        return t.matmul(self.psF[:, slot * 512:(slot + 1) * 512],
                                        self.wq[0:G, mt * 128:(mt + 1) * 128],
                                        self.stg[0:G, 0:512], start=True, stop=True)
                    ph.inc("tensor", te, mdl)
                    ph.wait("vector", te)

                    def exg(v, mt=mt, slot=slot):
                        return v.tensor_tensor(xg(mt), xs2(mt),
                                               self.psF[:, slot * 512:(slot + 1) * 512],
                                               ALU.add)
                    ph.inc("vector", ve, exg)

                cs = [issue_gu(self.sc["gpb"][g], g, *gu_chunks[k]) for k in range(2)]
                for ci, (i0, ni) in enumerate(gu_chunks):
                    ph.wait("tensor", dlw, chunk_dlw[cs[ci]])
                    ph.wait("tensor", ve)
                    ph.wait("tensor", ac)

                    def mmg(t, c=cs[ci], ni=ni):
                        ins = None
                        for i in range(ni):
                            for kt in range(16):
                                ins = t.matmul(
                                    self.psF[:, i * 512:(i + 1) * 512],
                                    cbuf(c, 16 * 512)[:, kt * 512 + i * 128:
                                                      kt * 512 + (i + 1) * 128],
                                    xg(kt), start=(kt == 0), stop=(kt == 15))
                        return ins
                    ph.inc("tensor", te, mmg)
                    chunk_te[cs[ci]] = self.sems["te"].v
                    if ci + 2 < len(gu_chunks):
                        cs.append(issue_gu(self.sc["gpb"][g], g, *gu_chunks[ci + 2]))
                    ph.wait("scalar", te)

                    def slv(s, i0=i0, ni=ni):
                        return s.activation(
                            self.xln[:, 16384 + i0 * 512:16384 + (i0 + ni) * 512],
                            self.psF[:, 0:ni * 512], AF.Silu)
                    ph.inc("scalar", ac, slv)

                cs = [issue_gu(self.sc["upb"][g], g, *gu_chunks[k]) for k in range(2)]
                for ci, (i0, ni) in enumerate(gu_chunks):
                    ph.wait("tensor", dlw, chunk_dlw[cs[ci]])
                    ph.wait("tensor", ve)
                    ph.wait("tensor", ac)

                    def mmu(t, c=cs[ci], ni=ni):
                        ins = None
                        for i in range(ni):
                            for kt in range(16):
                                ins = t.matmul(
                                    self.psF[:, i * 512:(i + 1) * 512],
                                    cbuf(c, 16 * 512)[:, kt * 512 + i * 128:
                                                      kt * 512 + (i + 1) * 128],
                                    xg(kt), start=(kt == 0), stop=(kt == 15))
                        return ins
                    ph.inc("tensor", te, mmu)
                    chunk_te[cs[ci]] = self.sems["te"].v
                    if ci + 2 < len(gu_chunks):
                        cs.append(issue_gu(self.sc["upb"][g], g, *gu_chunks[ci + 2]))
                    ph.wait("vector", te)

                    def aev(v, i0=i0, ni=ni):
                        v.tensor_tensor(self.stg[:, 0:ni * 512],
                                        self.psF[:, 0:ni * 512],
                                        self.xln[:, 16384 + i0 * 512:
                                                 16384 + (i0 + ni) * 512], ALU.mult)
                        return v.tensor_tensor(
                            self.mx[:, 16384 + i0 * 512:16384 + (i0 + ni) * 512]
                            .rearrange("p (n c) -> p n c", c=512),
                            self.stg[:, 0:ni * 512].rearrange("p (n c) -> p n c", c=512),
                            self.psF[:, 2560:3072].unsqueeze(1)
                            .broadcast_to([128, ni, 512]), ALU.mult)
                    ph.inc("vector", ve, aev)

                dn_chunks = [(0, 4), (4, 4), (8, 4), (12, 4)]
                cs = [issue_dn(self.sc["dpb"][g], g, *dn_chunks[k]) for k in range(2)]
                for ci, (m0, nm) in enumerate(dn_chunks):
                    ph.wait("tensor", dlw, chunk_dlw[cs[ci]])
                    ph.wait("tensor", ve)

                    def mmd(t, c=cs[ci], m0=m0, nm=nm):
                        ins = None
                        for m in range(nm):
                            for it in range(11):
                                ins = t.matmul(
                                    self.psF[:, m * 512:(m + 1) * 512],
                                    cbuf(c, 11 * 512)[:, it * 512 + m * 128:
                                                      it * 512 + (m + 1) * 128],
                                    ap_(it), start=(it == 0), stop=(it == 10))
                        return ins
                    ph.inc("tensor", te, mmd)
                    chunk_te[cs[ci]] = self.sems["te"].v
                    if ci + 2 < len(dn_chunks):
                        cs.append(issue_dn(self.sc["dpb"][g], g, *dn_chunks[ci + 2]))
                    ph.wait("vector", te)

                    def dacc(v, m0=m0, nm=nm, P=P):
                        ins = None
                        for m in range(nm):
                            ins = v.tensor_tensor(P[m0 + m], P[m0 + m],
                                                  self.psF[:, m * 512:(m + 1) * 512],
                                                  ALU.add)
                        return ins
                    ph.inc("vector", ve, dacc)

            for half in range(2):
                for tb in range(2):
                    col0 = tq * 512 + half * 256 + tb * 128
                    ph.wait("sync", ve)
                    ph.wait("sync", dlo)
                    ph.dma("sync", dli,
                           self.f32b[:].rearrange("p (n c) -> p n c", c=128),
                           self.sc["xmid"][:, col0:col0 + 128].rearrange(
                               "(m p) c -> p m c", p=128))
                    ph.wait("vector", dli)

                    def fadd(v, half=half, tb=tb, P=P):
                        ins = None
                        for mt in range(16):
                            ins = v.tensor_tensor(
                                self.f32a[:, mt * 128:(mt + 1) * 128],
                                P[mt][:, half * 256 + tb * 128:
                                      half * 256 + (tb + 1) * 128],
                                self.f32b[:, mt * 128:(mt + 1) * 128], ALU.add)
                        return ins
                    ph.inc("vector", ve, fadd)
                    ph.wait("tensor", ve)

                    def ftr(t):
                        ins = None
                        for mt in range(16):
                            bk = mt // 4
                            ins = t.transpose(
                                self.psF[:, bk * 512 + (mt % 4) * 128:
                                         bk * 512 + (mt % 4 + 1) * 128],
                                self.f32a[:, mt * 128:(mt + 1) * 128],
                                self.id32[:, :])
                        return ins
                    ph.inc("tensor", te, ftr)
                    ph.wait("vector", te)
                    ph.wait("vector", dlo)

                    def fev(v):
                        return v.tensor_copy(self.f32b[:], self.psF[:, 0:2048])
                    ph.inc("vector", ve, fev)
                    ph.wait("sync", ve)
                    ph.dma("sync", dlo, self.out[col0:col0 + 128, :], self.f32b[:])
        ph.wait("sync", dlo)

    def emit_phaseA(self, ph, prefix):
        x_in = self.xp if prefix else self.xm
        p = "p" if prefix else ""
        self._spill_guard = 0
        self._transpose_x(ph, x_in, prefix)
        if (not prefix) and "xlnc0" in self.taps:
            ph.wait("sync", self.sems["ve"])
            for pt in range(NPT):
                ph.dma("sync", self.sems["dlo"], self.taps["xlnc0"][:, pt * 4:(pt + 1) * 4],
                       self.xln[:].rearrange("p (n c) -> p n c", c=TL)[:, pt, 0:4])
            ph.wait("sync", self.sems["dlo"])
        for col0, width in [(1 + 512 * ci, 512) for ci in range(4)]:
            self._ln_job(ph, col0, width)
        if prefix:
            def saveb(v):
                return v.tensor_copy(
                    self.bcol[:].unsqueeze(2),
                    self.xln[:].rearrange("p (n c) -> p n c", c=TL)[:, :, T:T + 1])
            ph.inc("vector", self.sems["ve"], saveb)
        if (not prefix) and "xlnc" in self.taps:
            ph.wait("sync", self.sems["ve"])
            for pt in range(NPT):
                ph.dma("sync", self.sems["dlo"], self.taps["xlnc"][:, pt * 4:(pt + 1) * 4],
                       self.xln[:].rearrange("p (n c) -> p n c", c=TL)[:, pt, 0:4])
            ph.wait("sync", self.sems["dlo"])
        ph.barrier()
        self._chain_w(ph, self.sc[p + "lw"])
        ph.barrier()
        self._chain_a(ph, self.sc[p + "ag"])
        ph.barrier()
        self._proj_plain(ph, self.wk, self.sc[p + "kraw"], "k")
        ph.barrier()
        self._kpost(ph, p)
        ph.barrier()
        self._proj_v(ph, self.sc[p + "v"])
        ph.barrier()
        if not prefix:
            self._proj_plain(ph, self.wr, self.sc["r"], "r")
            ph.barrier()
            self._beta(ph)
            ph.barrier()
            self._chain_g(ph)
            ph.barrier()

    def _transpose_x(self, ph, x_in, prefix):
        dli, dlo = self.sems["dli"], self.sems["dlo"]
        ve, te, ac = self.sems["ve"], self.sems["te"], self.sems["ac"]
        for grp in range(16):
            ph.wait("gpsimd", te)
            ph.dma("gpsimd", dli, self.stg[:], x_in[1 + grp * 128: 129 + grp * 128, :])
            if not prefix:
                ph.wait("sync", ac)
                ph.dma("sync", dli, self.f32a[:], x_in[1 + grp * 128: 129 + grp * 128, :])
            ph.wait("tensor", dli)
            for q in range(4):
                ph.wait("tensor", ve)
                def tq(t, q=q):
                    ins = None
                    for j in range(4):
                        ft = q * 4 + j
                        ins = t.transpose(self.psH[:, j * 128:(j + 1) * 128],
                                          self.stg[:, ft * 128:(ft + 1) * 128],
                                          self.idbf[:])
                    return ins
                ph.inc("tensor", te, tq)
                ph.wait("vector", te)

                def ev(v, grp=grp, q=q):
                    dst = self.xln[:].rearrange("p (n c) -> p n c", c=TL)[
                        :, q * 4:(q + 1) * 4, 1 + grp * 128: 129 + grp * 128]
                    return v.tensor_copy(
                        dst, self.psH[:, 0:512].rearrange("p (n c) -> p n c", c=128))
                ph.inc("vector", ve, ev)
            if not prefix:
                for q in range(4):
                    ph.wait("tensor", ac)
                    def tq32(t, q=q):
                        ins = None
                        for j in range(4):
                            ft = q * 4 + j
                            ins = t.transpose(
                                self.bank(0)[:, j * 128:(j + 1) * 128],
                                self.f32a[:, ft * 128:(ft + 1) * 128], self.id32[:])
                        return ins
                    ph.inc("tensor", te, tq32)
                    ph.wait("scalar", te)

                    def ev32(s, q=q):
                        return s.copy(self.f32b[:, q * 512:(q + 1) * 512], self.bank(0))
                    ph.inc("scalar", ac, ev32)
                ph.wait("sync", ac)
                ph.dma("sync", dlo, self.sc["x32"][grp], self.f32b[:])
        ph.wait("vector", te)

        def setb(v, prefix=prefix):
            dst = self.xln[:].rearrange("p (n c) -> p n c", c=TL)[:, :, 0:1]
            if prefix:
                return v.memset(dst, 0.0)
            return v.tensor_copy(dst, self.bcol[:].unsqueeze(2))
        ph.inc("vector", ve, setb)

    def _ln_job(self, ph, col0, width):
        ve, te, ac = self.sems["ve"], self.sems["te"], self.sems["ac"]
        ph.wait("vector", self.sems["dli"])
        ph.wait("vector", te)

        def sqs(v):
            ins = None
            for pt in range(NPT):
                xv = self.xln_t(pt, col0, width)
                ins = v.tensor_tensor(self.mx[:, pt * width:(pt + 1) * width],
                                      xv, xv, ALU.mult)
            return ins
        ph.inc("vector", ve, sqs)
        ph.wait("tensor", ve)

        def mms(t):
            ins = None
            for pt in range(NPT):
                t.matmul(self.psF[0:1, 512:512 + width], self.onesc[:],
                         self.xln_t(pt, col0, width),
                         start=(pt == 0), stop=(pt == NPT - 1))
            for pt in range(NPT):
                ins = t.matmul(self.psF[0:1, 1024:1024 + width], self.onesc[:],
                               self.mx[:, pt * width:(pt + 1) * width],
                               start=(pt == 0), stop=(pt == NPT - 1))
            return ins
        ph.inc("tensor", te, mms)
        ph.wait("vector", te)

        def fin(v):
            mu = self.f32a[0:1, 0:width]
            msq = self.f32a[0:1, 512:512 + width]
            var = self.f32a[0:1, 1024:1024 + width]
            v.tensor_scalar(mu, self.psF[0:1, 512:512 + width], 1.0 / D, None, ALU.mult)
            v.tensor_scalar(msq, self.psF[0:1, 1024:1024 + width], 1.0 / D, None, ALU.mult)
            v.tensor_tensor(var, mu, mu, ALU.mult)
            v.tensor_sub(var, msq, var)
            return v.tensor_scalar(var, var, 1e-5, None, ALU.add)
        ph.inc("vector", ve, fin)
        ph.wait("scalar", ve)

        def rstd(s):
            lnv = self.f32b[0:1, 0:width]
            s.activation(lnv, self.f32a[0:1, 1024:1024 + width], AF.Ln)
            return s.activation(self.f32b[0:1, 512:512 + width], lnv, AF.Exp,
                                scale=self.cst[0:1, 0:1])
        ph.inc("scalar", ac, rstd)
        ph.wait("tensor", ac)

        def bc(t):
            t.matmul(self.bank(4, cols=width), self.ones1[0:1, :],
                     self.f32a[0:1, 0:width], start=True, stop=True)
            return t.matmul(self.bank(5, cols=width), self.ones1[0:1, :],
                            self.f32b[0:1, 512:512 + width], start=True, stop=True)
        ph.inc("tensor", te, bc)
        ph.wait("vector", te)

        def norm(v):
            ins = None
            for pt in range(NPT):
                xv = self.xln_t(pt, col0, width)
                tmp = self.stg2[:, 0:width]
                v.tensor_sub(tmp, xv, self.bank(4, cols=width))
                ins = v.tensor_tensor(xv, tmp, self.bank(5, cols=width), ALU.mult)
            return ins
        ph.inc("vector", ve, norm)

    def _mix_full(self, ph, which):
        ve = self.sems["ve"]
        vc_col, dv_col = MIX[which]
        ph.wait("vector", self.sems["te"])

        def mixf(v):
            ins = None
            for pt in range(NPT):
                for ci in range(4):
                    x1 = self.xln_t(pt, 1 + ci * 512, 512)
                    x0 = self.xln_t(pt, ci * 512, 512)
                    dst = self.mx_t(pt, ci * 512, 512)
                    tmp = self.stg2[:, 0:512]
                    v.tensor_scalar(tmp, x0, self.vcol(pt, vc_col), None, ALU.mult)
                    ins = v.scalar_tensor_tensor(dst, x1, self.dcol(pt, dv_col), tmp,
                                                 ALU.mult, ALU.add)
            return ins
        ph.inc("vector", ve, mixf)

    def _quarter_mms(self, ph, wdram, evac_fn):
        dlw, te, ve, ac = (self.sems["dlw"], self.sems["te"], self.sems["ve"],
                           self.sems["ac"])
        ring = 16
        slot_busy = [0] * ring
        for q in range(8):
            for kt in range(NPT):
                slot = (q * NPT + kt) % ring
                ph.wait("gpsimd", te, slot_busy[slot])
                ph.dma("gpsimd", dlw, self.wq[:, slot * 256:(slot + 1) * 256],
                       wdram[kt * 128:(kt + 1) * 128, q * 256:(q + 1) * 256])
            ph.wait("tensor", dlw)
            for mt2 in range(2):
                mt = q * 2 + mt2
                ph.wait("tensor", ve)
                ph.wait("tensor", ac)

                def mmq(t, q=q, mt2=mt2):
                    ins = None
                    for kt in range(NPT):
                        slot = (q * NPT + kt) % ring
                        w_ap = self.wq[:, slot * 256 + mt2 * 128:
                                       slot * 256 + (mt2 + 1) * 128]
                        for ci in range(4):
                            ins = t.matmul(self.bank(ci),
                                           w_ap, self.mx_t(kt, ci * 512, 512),
                                           start=(kt == 0), stop=(kt == NPT - 1))
                    return ins
                ph.inc("tensor", te, mmq)
                for kt in range(NPT):
                    slot = (q * NPT + kt) % ring
                    slot_busy[slot] = te.v
                evac_fn(ph, mt)

    def _proj_plain(self, ph, wdram, spill, which):
        ve, dlo = self.sems["ve"], self.sems["dlo"]
        self._mix_full(ph, which)
        tapname = "mx_" + which
        if tapname in self.taps:
            ph.wait("sync", ve)
            for kt in range(NPT):
                ph.dma("sync", dlo, self.taps[tapname][kt * 128:(kt + 1) * 128, :],
                       self.mx[:, kt * T:(kt + 1) * T])
            ph.wait("sync", dlo)

        def evac(ph2, mt):
            ph.wait("vector", self.sems["te"])
            ph.wait("vector", dlo, self._spill_guard)

            def ev(v):
                ins = None
                for ci in range(4):
                    ins = v.tensor_copy(self.stg3[:, ci * 512:(ci + 1) * 512],
                                        self.bank(ci))
                return ins
            ph.inc("vector", ve, ev)
            ph.wait("sync", ve)
            ph.dma("sync", dlo, spill[mt * 128:(mt + 1) * 128, :], self.stg3[:])
            self._spill_guard = dlo.v
        self._spill_guard = 0
        self._quarter_mms(ph, wdram, evac)


    def v3(self, ap, c=128):
        return ap.rearrange("p (n c) -> p n c", c=c)

    def bslot(self, i):
        return self.mx[0:64, i * 4096:(i + 1) * 4096]

    def xslot(self, i):
        return self.xln[0:64, i * 4096:(i + 1) * 4096]

    def chainq(self, q, i8=None):
        buf = self.bslot(7) if q < 8 else self.xslot(6)
        base = (q % 8) * 512
        if i8 is None:
            return buf[:, base:base + 512]
        return buf[:, base + i8 * 128: base + (i8 + 1) * 128]

    def emit_phaseB(self, ph, prefix):
        dli, dlo = self.sems["dli"], self.sems["dlo"]
        ve, te, ac = self.sems["ve"], self.sems["te"], self.sems["ac"]
        sc = self.sc
        p = "p" if prefix else ""
        betat = self.stg3[0:64, 0:2048].bitcast(F32)

        ph.wait("sync", dlo)

        if self.phb_level < 1:
            return

        if not prefix:
            ph.dma("sync", dli, self.f32a[0:32, 0:T], sc["beta"][:, :])
            ph.wait("tensor", dli)
            ph.wait("tensor", ac)
            ph.wait("tensor", ve)
            for half in range(2):
                ph.wait("tensor", ve)

                def btr(t, half=half):
                    ins = None
                    for k in range(16):
                        cc = half * 16 + k
                        ins = t.transpose(self.bank(0)[0:64, k * 32:(k + 1) * 32],
                                          self.f32a[0:32, cc * 64:(cc + 1) * 64],
                                          self.id32[0:32, 0:32])
                    return ins
                ph.inc("tensor", te, btr)
                ph.wait("vector", te)

                def bev(v, half=half):
                    return v.tensor_copy(betat[:, half * 512:(half + 1) * 512],
                                         self.bank(0)[0:64, 0:512])
                ph.inc("vector", ve, bev)

        for hp in range(NHP):
            row0 = hp * 128
            self.emit_wconv(ph, 11)
            ph.wait("sync", ve)
            ph.wait("sync", te)
            ph.wait("sync", ac)
            ph.wait("sync", dlo)
            names = [("lw", 0), ("a", 1), ("b", 2), ("k2", 3)]
            if not prefix:
                names.append(("r", 4))
            for nm, slot in names:
                for h in range(2):
                    ph.dma("sync", dli, self.bslot(slot)[:, h * T:(h + 1) * T],
                           sc[p + nm][row0 + h * 64:row0 + (h + 1) * 64, :])
            ph.dma("sync", dli,
                   self.v3(self.bslot(5)),
                   sc[p + "v"][hp].rearrange("(n p) c -> p n c", p=64))
            ph.wait("vector", dli)

            for h in range(2):
                hc = slice(h * T, (h + 1) * T)

                def lam(v, hc=hc, h=h):
                    ins = None
                    for q in range(2):
                        qs = slice(q * 1024, (q + 1) * 1024)
                        v.tensor_copy(self.lwf[:, :],
                                      self.bslot(0)[:, h * T + q * 1024:
                                                    h * T + (q + 1) * 1024])
                        ins = v.tensor_tensor_scan(
                            self.f32a[0:64, qs], self.segf[:, :],
                            self.lwf[:, :], 0.0, ALU.mult, ALU.add)
                    return ins
                ph.inc("vector", ve, lam)
                ph.wait("vector", ve)

                def sub1(v, hc=hc):
                    return v.tensor_sub(self.f32b[0:64, 0:T],
                                        self.f32a[0:64, 0:T], self.bslot(0)[:, hc])
                ph.inc("vector", ve, sub1)
                ph.wait("scalar", ve)

                def e1(s):
                    return s.activation(self.f32b[0:64, 0:T],
                                        self.f32b[0:64, 0:T], AF.Exp)
                ph.inc("scalar", ac, e1)
                ph.wait("vector", ac)

                def mAt(v, hc=hc):
                    return v.tensor_tensor(self.bslot(0)[:, hc], self.bslot(1)[:, hc],
                                           self.f32b[0:64, 0:T], ALU.mult)
                ph.inc("vector", ve, mAt)
                ph.wait("scalar", ve)

                def e2(s):
                    return s.activation(self.f32b[0:64, 0:T], self.f32a[0:64, 0:T],
                                        AF.Exp, scale=self.cst[0:64, 1:2])
                ph.inc("scalar", ac, e2)
                ph.wait("vector", ac)

                def mBK(v, hc=hc):
                    v.tensor_tensor(self.bslot(2)[:, hc], self.bslot(2)[:, hc],
                                    self.f32b[0:64, 0:T], ALU.mult)
                    return v.tensor_tensor(self.bslot(3)[:, hc], self.bslot(3)[:, hc],
                                           self.f32b[0:64, 0:T], ALU.mult)
                ph.inc("vector", ve, mBK)
                ph.wait("scalar", ve)

                def e3(s):
                    return s.activation(self.f32b[0:64, 0:T], self.f32a[0:64, 0:T],
                                        AF.Exp)
                ph.inc("scalar", ac, e3)
                ph.wait("vector", ac)

                def mRcl(v, hc=hc, h=h, hp=hp, prefix=prefix):
                    ins = v.tensor_copy(
                        self.clb[0:64, h * NCH:(h + 1) * NCH],
                        self.f32b[0:64, 0:T].rearrange(
                            "p (n c) -> p n c", c=L)[:, :, L - 1])
                    if not prefix:
                        ins = v.tensor_tensor(self.bslot(4)[:, hc],
                                              self.bslot(4)[:, hc],
                                              self.f32b[0:64, 0:T], ALU.mult)
                        bview = betat[:].rearrange("p (n h) -> p n h", h=32)[
                            :, :, 2 * hp + h].unsqueeze(2).broadcast_to([64, 32, 64])
                        ins = v.tensor_tensor(
                            self.v3(self.bslot(6))[:, :, h * 64:(h + 1) * 64],
                            self.v3(self.bslot(5))[:, :, h * 64:(h + 1) * 64],
                            bview, ALU.mult)
                    return ins
                ph.inc("vector", ve, mRcl)

            if self.phb_level < 2:
                continue

            for which in range(2):
                srcslot, dstslot = (2, 0) if which == 0 else (3, 1)
                for grp in range(4):
                    ph.wait("tensor", ve)

                    def trx(t, srcslot=srcslot, grp=grp):
                        ins = None
                        for k in range(16):
                            inst = grp * 16 + k
                            cc, h = inst // 2, inst % 2
                            ins = t.transpose(
                                self.psH[0:64, k * 64:(k + 1) * 64],
                                self.bslot(srcslot)[:, h * T + cc * L:
                                                    h * T + (cc + 1) * L],
                                self.idbf[0:64, 0:64])
                        return ins
                    ph.inc("tensor", te, trx)
                    ph.wait("vector", te)

                    def evx(v, dstslot=dstslot, grp=grp):
                        return v.tensor_copy(
                            self.xslot(dstslot)[:, grp * 1024:(grp + 1) * 1024],
                            self.psH[0:64, 0:1024])
                    ph.inc("vector", ve, evx)

            if self.phb_level < 3:
                continue
            for bp in range(4):
                self._tinv_chain2(ph, 2 * bp, 2 * bp + 1)

            if self.phb_level < 4 and self.phb_level not in (31, 32, 33):
                continue
            self._scan_builds(ph, prefix)

            if hp == 0 and not prefix and "pb_s0" in self.taps:
                ph.wait("sync", ve)
                for i in range(7):
                    ph.dma("sync", dlo, self.taps["pb_s%d" % i], self.bslot(i))
                for i in range(6):
                    ph.dma("sync", dlo, self.taps["pb_x%d" % i], self.xslot(i))
                ph.dma("sync", dlo, self.taps["pb_clb"], self.clb[0:64, :])
                ph.wait("sync", dlo)

            if self.phb_level < 5 or self.phb_level in (31, 32, 33):
                continue
            self._sweep(ph, prefix, hp)

            if not prefix and self.phb_level >= 5:
                ph.wait("sync", ve)
                for h in range(2):
                    ph.dma("sync", dlo, sc["y"][row0 + h * 64:row0 + (h + 1) * 64, :],
                           self.bslot(7)[:, h * T:(h + 1) * T])

    def _slice_hc(self, slot, inst):
        cc, h = inst // 2, inst % 2
        return self.bslot(slot)[:, h * T + cc * L: h * T + (cc + 1) * L]

    def chainbuf(self, batch, phys):
        if batch == 0:
            if phys < 4:
                return self.bslot(7)[:, phys * 1024:(phys + 1) * 1024]
            return self.xslot(6)[:, (phys - 4) * 1024:(phys - 3) * 1024]
        if phys < 2:
            return self.xslot(6)[:, 2048 + phys * 1024:2048 + (phys + 1) * 1024]
        return self.xslot(7)[:, (phys - 2) * 1024:(phys - 1) * 1024]

    def _tinv_chain2(self, ph, bA, bB):
        ve, te = self.sems["ve"], self.sems["te"]
        PHYS = {0: 0, 1: 1, 2: 2, 3: 3, 4: 4, 5: 2, 6: 3, 7: 4,
                8: 5, 9: 2, 10: 3, 11: 4, 12: 0, 13: 1}
        i0s = {0: bA * 8, 1: bB * 8}

        def q_ap(bt, q, i8, b=None):
            buf = self.chainbuf(bt, PHYS[q])
            base = i8 * 128
            if b is None:
                return buf[:, base:base + 128]
            return buf[:, base + b * 64: base + b * 64 + 64]

        def q_all(bt, q):
            return self.chainbuf(bt, PHYS[q])

        def ps(bt, i8, half):
            c0 = bt * 1024 + i8 * 128 + half * 64
            return self.psF[0:64, c0:c0 + 64]

        def ps8(bt, w=1024):
            return self.psF[0:64, bt * 1024:bt * 1024 + w]

        def mask8(mk):
            return self.mask(mk)[0:64, :].unsqueeze(1).broadcast_to([64, 8, 128])

        def pv(bt):
            return ps8(bt).rearrange("p (n c) -> p n c", c=128)

        vals = {}

        def step(mmfn, evfn):
            for bt in (0, 1):
                ph.wait("tensor", self.sems["ve"], vals.get(("ev", bt)))
                ph.inc("tensor", te, mmfn(bt))
                vals[("mm", bt)] = self.sems["te"].v
                ph.wait("vector", self.sems["te"], vals[("mm", bt)])
                ph.inc("vector", ve, evfn(bt))
                vals[("ev", bt)] = self.sems["ve"].v

        def mk_mm0(bt):
            def mm0(t, bt=bt):
                ins = None
                for i8 in range(8):
                    inst = i0s[bt] + i8
                    at = self._slice_hc(0, inst)
                    bh = self._slice_hc(2, inst)
                    t.matmul(ps(bt, i8, 0), at, bh, start=True, stop=True)
                    ins = t.matmul(ps(bt, i8, 1), bh, at, start=True, stop=True)
                return ins
            return mm0

        def mk_ev0(bt):
            def ev0(v, bt=bt):
                v.tensor_tensor(self.v3(q_all(bt, 0)), pv(bt), mask8(MK_QAB_S),
                                ALU.mult)
                v.tensor_tensor(self.v3(q_all(bt, 1)), pv(bt), mask8(MK_QAB_D16),
                                ALU.mult)
                return v.tensor_tensor(self.v3(q_all(bt, 2)), self.v3(q_all(bt, 1)),
                                       mask8(MK_QI), ALU.add)
            return ev0
        step(mk_mm0, mk_ev0)

        def prod_step(qP, qQ, qdst, add_from=None):
            def mk_mm(bt):
                def mm(t, bt=bt):
                    ins = None
                    for i8 in range(8):
                        t.matmul(ps(bt, i8, 0), q_ap(bt, qP, i8, 1),
                                 q_ap(bt, qQ, i8, 0), start=True, stop=True)
                        ins = t.matmul(ps(bt, i8, 1), q_ap(bt, qQ, i8, 0),
                                       q_ap(bt, qP, i8, 1), start=True, stop=True)
                    return ins
                return mm

            def mk_ev(bt):
                def ev(v, bt=bt):
                    if add_from is None:
                        return v.tensor_copy(q_all(bt, qdst), ps8(bt))
                    return v.tensor_tensor(q_all(bt, qdst), ps8(bt),
                                           q_all(bt, add_from), ALU.add)
                return ev
            step(mk_mm, mk_ev)

        prod_step(1, 1, 3)
        prod_step(2, 3, 4, add_from=2)
        prod_step(3, 3, 5)
        prod_step(4, 5, 6, add_from=4)
        prod_step(5, 5, 7)
        prod_step(6, 7, 8, add_from=6)

        for bt in (0, 1):
            ph.wait("vector", self.sems["te"], vals[("mm", bt)])

            def lo1(v, bt=bt):
                return v.tensor_sub(q_all(bt, 9), q_all(bt, 0), q_all(bt, 1))
            ph.inc("vector", ve, lo1)
            vals[("ev", bt)] = self.sems["ve"].v

        def mk_mmB(bt):
            def mmB(t, bt=bt):
                ins = None
                for i8 in range(8):
                    t.matmul(ps(bt, i8, 0), q_ap(bt, 8, i8, 1), q_ap(bt, 9, i8, 0),
                             start=True, stop=True)
                    ins = t.matmul(ps(bt, i8, 1), q_ap(bt, 9, i8, 0),
                                   q_ap(bt, 8, i8, 1), start=True, stop=True)
                return ins
            return mmB

        def mk_evB(bt):
            def evB(v, bt=bt):
                v.tensor_copy(q_all(bt, 10), ps8(bt))
                return v.tensor_tensor(self.v3(q_all(bt, 11)),
                                       self.v3(q_all(bt, 10)),
                                       mask8(MK_QI), ALU.add)
            return evB
        step(mk_mmB, mk_evB)
        prod_step(10, 10, 12)
        prod_step(11, 12, 13, add_from=11)

        def mk_mmT(bt):
            def mmT(t, bt=bt):
                ins = None
                for i8 in range(8):
                    ins = t.matmul(self.psF[0:64, bt * 1024 + i8 * 64:
                                            bt * 1024 + (i8 + 1) * 64],
                                   q_ap(bt, 8, i8, 0), q_ap(bt, 13, i8, 1),
                                   start=True, stop=True)
                return ins
            return mmT

        def mk_evT(bt):
            def evT(v, bt=bt):
                return v.tensor_copy(
                    self.xslot(2)[:, i0s[bt] * 64:(i0s[bt] + 8) * 64],
                    ps8(bt, 512))
            return evT
        step(mk_mmT, mk_evT)

    def _tinv_chain(self, ph, b8):
        ve, te, ac = self.sems["ve"], self.sems["te"], self.sems["ac"]
        i0 = b8 * 8

        PHYS = {0: 0, 1: 1, 2: 2, 3: 3, 4: 4, 5: 2, 6: 3, 7: 4,
                8: 5, 9: 2, 10: 3, 11: 4, 12: 0, 13: 1}

        def q_ap(q, i8, b=None):
            s = PHYS[q]
            buf = self.bslot(7) if s < 4 else self.xslot(6)
            base = (s % 4) * 1024 + i8 * 128
            if b is None:
                return buf[:, base:base + 128]
            return buf[:, base + b * 64: base + b * 64 + 64]

        def q_all(q):
            s = PHYS[q]
            buf = self.bslot(7) if s < 4 else self.xslot(6)
            return buf[:, (s % 4) * 1024:(s % 4) * 1024 + 1024]

        def mask8(mk):
            return self.mask(mk)[0:64, :].unsqueeze(1).broadcast_to([64, 8, 128])

        def pv(bank0_pair):
            return self.psF[0:64, 0:1024].rearrange("p (n c) -> p n c", c=128)

        ph.wait("tensor", ve)

        def mm0(t, i0=i0):
            ins = None
            for i8 in range(8):
                inst = i0 + i8
                at, bh = self._slice_hc(0, inst), self._slice_hc(2, inst)
                pa = self.psF[0:64, i8 * 128: i8 * 128 + 64]
                pb = self.psF[0:64, i8 * 128 + 64: i8 * 128 + 128]
                t.matmul(pa, at, bh, start=True, stop=True)
                ins = t.matmul(pb, bh, at, start=True, stop=True)
            return ins
        ph.inc("tensor", te, mm0)
        ph.wait("vector", te)

        def ev0(v):
            v.tensor_tensor(self.v3(q_all(0)), pv(0), mask8(MK_QAB_S), ALU.mult)
            v.tensor_tensor(self.v3(q_all(1)), pv(0), mask8(MK_QAB_D16), ALU.mult)
            return v.tensor_tensor(self.v3(q_all(2)), self.v3(q_all(1)),
                                   mask8(MK_QI), ALU.add)
        ph.inc("vector", ve, ev0)

        def prod(qP, qQ, qdst, add_from=None):
            ph.wait("tensor", ve)

            def mm(t, qP=qP, qQ=qQ):
                ins = None
                for i8 in range(8):
                    pa = self.psF[0:64, i8 * 128: i8 * 128 + 64]
                    pb = self.psF[0:64, i8 * 128 + 64: i8 * 128 + 128]
                    t.matmul(pa, q_ap(qP, i8, 1), q_ap(qQ, i8, 0),
                             start=True, stop=True)
                    ins = t.matmul(pb, q_ap(qQ, i8, 0), q_ap(qP, i8, 1),
                                   start=True, stop=True)
                return ins
            ph.inc("tensor", te, mm)
            ph.wait("vector", te)

            def ev(v, qdst=qdst, add_from=add_from):
                if add_from is None:
                    return v.tensor_copy(q_all(qdst), self.psF[0:64, 0:1024])
                return v.tensor_tensor(q_all(qdst), self.psF[0:64, 0:1024],
                                       q_all(add_from), ALU.add)
            ph.inc("vector", ve, ev)

        prod(1, 1, 3)
        prod(2, 3, 4, add_from=2)
        prod(3, 3, 5)
        prod(4, 5, 6, add_from=4)
        prod(5, 5, 7)
        prod(6, 7, 8, add_from=6)
        ph.wait("vector", te)

        def lo1(v):
            return v.tensor_sub(q_all(9), q_all(0), q_all(1))
        ph.inc("vector", ve, lo1)

        ph.wait("tensor", ve)

        def mmB(t):
            ins = None
            for i8 in range(8):
                pa = self.psF[0:64, i8 * 128: i8 * 128 + 64]
                pb = self.psF[0:64, i8 * 128 + 64: i8 * 128 + 128]
                t.matmul(pa, q_ap(8, i8, 1), q_ap(9, i8, 0), start=True, stop=True)
                ins = t.matmul(pb, q_ap(9, i8, 0), q_ap(8, i8, 1),
                               start=True, stop=True)
            return ins
        ph.inc("tensor", te, mmB)
        ph.wait("vector", te)

        def evB(v):
            v.tensor_copy(q_all(10), self.psF[0:64, 0:1024])
            return v.tensor_tensor(self.v3(q_all(11)), self.v3(q_all(10)),
                                   mask8(MK_QI), ALU.add)
        ph.inc("vector", ve, evB)

        prod(10, 10, 12)
        prod(11, 12, 13, add_from=11)

        ph.wait("tensor", ve)

        def mmT(t):
            ins = None
            for i8 in range(8):
                ins = t.matmul(self.psF[0:64, i8 * 64:(i8 + 1) * 64],
                               q_ap(8, i8, 0), q_ap(13, i8, 1),
                               start=True, stop=True)
            return ins
        ph.inc("tensor", te, mmT)
        ph.wait("vector", te)

        def evT(v, i0=i0):
            return v.tensor_copy(self.xslot(2)[:, i0 * 64:(i0 + 8) * 64],
                                 self.psF[0:64, 0:512])
        ph.inc("vector", ve, evT)

    def _scan_builds(self, ph, prefix):
        ve, te = self.sems["ve"], self.sems["te"]
        sub = self.phb_level

        def mask16(mk):
            return self.mask(mk)[0:64, 0:64].unsqueeze(1).broadcast_to([64, 16, 64])

        for b16 in range(4):
            i0 = b16 * 16
            ph.wait("tensor", ve)

            def mmk(t, i0=i0):
                ins = None
                for k in range(16):
                    inst = i0 + k
                    ins = t.matmul(self.psF[0:64, k * 64:(k + 1) * 64],
                                   self._slice_hc(3, inst),
                                   self._slice_hc(0, inst),
                                   start=True, stop=True)
                return ins
            ph.inc("tensor", te, mmk)
            ph.wait("vector", te)

            def evk(v):
                return v.tensor_tensor(
                    self.v3(self.stg2[0:64, 0:1024], c=64),
                    self.psF[0:64, 0:1024].rearrange("p (n c) -> p n c", c=64),
                    mask16(MK_STRICT_T22), ALU.mult)
            ph.inc("vector", ve, evk)
            if sub == 31:
                continue
            ph.wait("tensor", ve)

            def mmu(t, i0=i0):
                ins = None
                for k in range(16):
                    inst = i0 + k
                    cc, h = inst // 2, inst % 2
                    rh = self.v3(self.bslot(5))[:, cc, h * 64:(h + 1) * 64]
                    ins = t.matmul(self.psF[0:64, 1024 + k * 64:1024 + (k + 1) * 64],
                                   self.stg2[0:64, k * 64:(k + 1) * 64], rh,
                                   start=True, stop=True)
                return ins
            ph.inc("tensor", te, mmu)
            ph.wait("vector", te)

            def evu(v, i0=i0):
                return v.tensor_copy(self.xslot(3)[:, i0 * 64:(i0 + 16) * 64],
                                     self.psF[0:64, 1024:2048])
            ph.inc("vector", ve, evu)
            if sub == 32:
                continue

            if not prefix:
                for which in range(2):
                    srcslot = 2 if which == 0 else 3
                    dstx = 4 if which == 0 else 5
                    ph.wait("tensor", ve)

                    def mmn(t, i0=i0, srcslot=srcslot):
                        ins = None
                        for k in range(16):
                            inst = i0 + k
                            ins = t.matmul(self.psF[0:64, k * 64:(k + 1) * 64],
                                           self._slice_hc(srcslot, inst),
                                           self._slice_hc(4, inst),
                                           start=True, stop=True)
                        return ins
                    ph.inc("tensor", te, mmn)
                    ph.wait("vector", te)

                    def evn(v, i0=i0, dstx=dstx):
                        return v.tensor_tensor(
                            self.v3(self.xslot(dstx)[:, i0 * 64:(i0 + 16) * 64], c=64),
                            self.psF[0:64, 0:1024].rearrange("p (n c) -> p n c", c=64),
                            mask16(MK_INCL_T22), ALU.mult)
                    ph.inc("vector", ve, evn)

    def _sweep(self, ph, prefix, hp):
        ve, te, ac = self.sems["ve"], self.sems["te"], self.sems["ac"]
        st = self.state[0:64, hp * 128:(hp + 1) * 128]
        stb = self.statebf[0:64, hp * 128:(hp + 1) * 128]
        yb = self.bslot(7)
        sub = self.phb_level

        for cc in range(NCH):
            ph.wait("tensor", ve)
            ph.wait("tensor", ac)

            def mmp(t, cc=cc):
                ins = None
                for h in range(2):
                    ins = t.matmul(
                        self.psF[0:64, 4 * 512 + h * 64:4 * 512 + (h + 1) * 64],
                        self._slice_hc(0, cc * 2 + h),
                        stb[:, h * 64:(h + 1) * 64],
                        start=True, stop=True)
                return ins
            ph.inc("tensor", te, mmp)
            ph.wait("vector", te)

            def t0add(v, cc=cc):
                return v.tensor_tensor(self.stg[0:64, 0:128],
                                       self.psF[0:64, 4 * 512:4 * 512 + 128],
                                       self.xslot(3)[:, cc * 128:(cc + 1) * 128],
                                       ALU.add)
            ph.inc("vector", ve, t0add)
            if sub == 51:
                continue
            ph.wait("tensor", ve)

            def mmsa(t, cc=cc):
                ins = None
                for h in range(2):
                    ins = t.matmul(
                        self.psF[0:64, 5 * 512 + h * 64:5 * 512 + (h + 1) * 64],
                        self.xslot(2)[:, (cc * 2 + h) * 64:(cc * 2 + h + 1) * 64],
                        self.stg[0:64, h * 64:(h + 1) * 64],
                        start=True, stop=True)
                return ins
            ph.inc("tensor", te, mmsa)
            ph.wait("vector", te)

            def saev(v):
                return v.tensor_copy(self.stg[0:64, 128:256],
                                     self.psF[0:64, 5 * 512:5 * 512 + 128])
            ph.inc("vector", ve, saev)
            if sub == 52:
                continue
            ph.wait("tensor", ve)

            if not prefix and sub != 53:
                def mmy(t, cc=cc):
                    ins = None
                    for h in range(2):
                        inst = cc * 2 + h
                        po = self.psF[0:64, 6 * 512 + h * 64:6 * 512 + (h + 1) * 64]
                        pb = self.psF[0:64, 6 * 512 + 128 + h * 64:
                                      6 * 512 + 128 + (h + 1) * 64]
                        t.matmul(po, stb[:, h * 64:(h + 1) * 64],
                                 self._slice_hc(4, inst), start=True, stop=False)
                        t.matmul(po, self.stg[0:64, 128 + h * 64:128 + (h + 1) * 64],
                                 self.xslot(4)[:, inst * 64:(inst + 1) * 64],
                                 start=False, stop=False)
                        t.matmul(po, self.v3(self.bslot(5))[:, cc, h * 64:(h + 1) * 64],
                                 self.xslot(5)[:, inst * 64:(inst + 1) * 64],
                                 start=False, stop=True)
                        ins = t.matmul(pb,
                                       self.v3(self.bslot(6))[:, cc, h * 64:(h + 1) * 64],
                                       self.idbf[0:64, 0:64],
                                       start=True, stop=True)
                    return ins
                ph.inc("tensor", te, mmy)

            def mmst(t, cc=cc):
                ins = None
                for h in range(2):
                    inst = cc * 2 + h
                    po = self.psF[0:64, h * 64:(h + 1) * 64]
                    t.matmul(po, self.idbf[0:64, 0:64], stb[:, h * 64:(h + 1) * 64],
                             start=True, stop=False)
                    t.matmul(po, self.xslot(0)[:, inst * 64:(inst + 1) * 64],
                             self.stg[0:64, 128 + h * 64:128 + (h + 1) * 64],
                             start=False, stop=False)
                    ins = t.matmul(po, self.xslot(1)[:, inst * 64:(inst + 1) * 64],
                                   self.v3(self.bslot(5))[:, cc, h * 64:(h + 1) * 64],
                                   start=False, stop=True)
                return ins
            ph.inc("tensor", te, mmst)
            ph.wait("vector", te)
            if not prefix and sub != 53:
                ph.wait("scalar", te)

                def gnaff(s, cc=cc, hp=hp):
                    ins = None
                    for h in range(2):
                        hd = 2 * hp + h
                        ins = s.activation(
                            self.stg[0:64, 256 + h * 64:256 + (h + 1) * 64],
                            self.psF[0:64, 6 * 512 + h * 64:6 * 512 + (h + 1) * 64],
                            AF.Identity,
                            scale=self.gnv[0:64, hd:hd + 1],
                            bias=self.gnv[0:64, 32 + hd:33 + hd])
                    return ins
                ph.inc("scalar", ac, gnaff)
                ph.wait("vector", ac)

                def yev(v, cc=cc):
                    return v.tensor_tensor(
                        yb[:].rearrange("p (h c) -> p h c", c=T)[:, :, cc * L:(cc + 1) * L],
                        self.stg[0:64, 256:384].rearrange("p (h c) -> p h c", c=64),
                        self.psF[0:64, 6 * 512 + 128:6 * 512 + 256].rearrange(
                            "p (h c) -> p h c", c=64),
                        ALU.add)
                ph.inc("vector", ve, yev)
            ph.wait("scalar", ve)

            ph.wait("scalar", te)

            def fold2(s, cc=cc, hp=hp):
                ins = None
                for h in range(2):
                    ins = s.activation(st[:, h * 64:(h + 1) * 64],
                                       self.psF[0:64, h * 64:(h + 1) * 64],
                                       AF.Identity,
                                       scale=self.clb[0:64, h * NCH + cc:
                                                      h * NCH + cc + 1])
                return ins
            ph.inc("scalar", ac, fold2)
            ph.wait("vector", ac)

            def fold2b(v):
                return v.tensor_copy(stb[:, :], st[:, :])
            ph.inc("vector", ve, fold2b)

    def _chain_w(self, ph, spill):
        dlw, dlo, te, ve, ac = (self.sems["dlw"], self.sems["dlo"], self.sems["te"],
                                self.sems["ve"], self.sems["ac"])
        self._mix_full(ph, "w")
        for kt in range(NPT):
            ph.dma("gpsimd", dlw, self.wq[:, kt * 128: kt * 128 + 64],
                   self.w1[kt * 128:(kt + 1) * 128, :])
        ph.wait("tensor", dlw)
        ph.wait("tensor", ve)

        def mm1(t):
            ins = None
            for kt in range(NPT):
                for ci in range(4):
                    ins = t.matmul(self.psF[0:64, ci * 512:(ci + 1) * 512],
                                   self.wq[:, kt * 128: kt * 128 + 64],
                                   self.mx_t(kt, ci * 512, 512),
                                   start=(kt == 0), stop=(kt == NPT - 1))
            return ins
        ph.inc("tensor", te, mm1)
        ph.wait("scalar", te)

        def tanh(s):
            ins = None
            for ci in range(4):
                ins = s.activation(self.smallb[:, ci * 512:(ci + 1) * 512],
                                   self.psF[0:64, ci * 512:(ci + 1) * 512], AF.Tanh)
            return ins
        ph.inc("scalar", ac, tanh)
        ph.dma("gpsimd", dlw, self.wsm[0:64, 0:D], self.w2[:, :])
        ph.wait("tensor", dlw)
        ph.wait("tensor", ac)
        for mt in range(NPT):
            ph.wait("tensor", ve)
            ph.wait("tensor", ac)

            def mm2(t, mt=mt):
                ins = None
                for ci in range(4):
                    ins = t.matmul(self.bank(ci),
                                   self.wsm[0:64, mt * 128:(mt + 1) * 128],
                                   self.smallb[:, ci * 512:(ci + 1) * 512],
                                   start=True, stop=True)
                return ins
            ph.inc("tensor", te, mm2)
            ph.wait("scalar", te)
            ph.wait("scalar", ve)

            def chain(s, mt=mt):
                au = self.f32a[:, 0:D]
                for ci in range(4):
                    s.activation(au[:, ci * 512:(ci + 1) * 512], self.bank(ci),
                                 AF.Abs, bias=self.vcol(mt, VC.w0))
                eu = self.f32b[:, 0:D]
                s.activation(eu, au, AF.Exp, scale=self.cst[:, 1:2])
                lp = self.f32a[:, 0:D]
                s.activation(lp, eu, AF.Ln, bias=1.0)
                ru = self.f32b[:, 0:D]
                ins = None
                for ci in range(4):
                    ins = s.activation(ru[:, ci * 512:(ci + 1) * 512], self.bank(ci),
                                       AF.Relu, scale=self.cst[:, 1:2],
                                       bias=self.vcol(mt, VC.negw0))
                return ins
            ph.inc("scalar", ac, chain)
            ph.wait("vector", ac)

            def chain2(v, mt=mt):
                sp = self.f32a[:, 0:D]
                return v.tensor_add(sp, self.f32a[:, 0:D], self.f32b[:, 0:D])
            ph.inc("vector", ve, chain2)
            ph.wait("scalar", ve)

            def chain3(s, mt=mt):
                return s.activation(self.f32b[:, 0:D], self.f32a[:, 0:D], AF.Exp,
                                    scale=self.cst[:, 1:2], bias=self.cst[:, 0:1])
            ph.inc("scalar", ac, chain3)
            ph.wait("vector", ac)
            ph.wait("vector", dlo, self._spill_guard)

            def chain4(v, mt=mt):
                return v.tensor_scalar(self.stg3[:], self.f32b[:, 0:D], -1.0, None,
                                       ALU.mult)
            ph.inc("vector", ve, chain4)
            ph.wait("sync", ve)
            ph.dma("sync", dlo, spill[mt * 128:(mt + 1) * 128, :], self.stg3[:])
            self._spill_guard = dlo.v

    def _chain_a(self, ph, spill):
        dlw, dlo, te, ve, ac = (self.sems["dlw"], self.sems["dlo"], self.sems["te"],
                                self.sems["ve"], self.sems["ac"])
        self._mix_full(ph, "a")
        for kt in range(NPT):
            ph.dma("gpsimd", dlw, self.wq[:, kt * 128: kt * 128 + 64],
                   self.a1[kt * 128:(kt + 1) * 128, :])
        ph.wait("tensor", dlw)
        ph.wait("tensor", ve)

        def mm1(t):
            ins = None
            for kt in range(NPT):
                for ci in range(4):
                    ins = t.matmul(self.psF[0:64, ci * 512:(ci + 1) * 512],
                                   self.wq[:, kt * 128: kt * 128 + 64],
                                   self.mx_t(kt, ci * 512, 512),
                                   start=(kt == 0), stop=(kt == NPT - 1))
            return ins
        ph.inc("tensor", te, mm1)
        ph.wait("vector", te)

        def cp(v):
            ins = None
            for ci in range(4):
                ins = v.tensor_copy(self.smallb[:, ci * 512:(ci + 1) * 512],
                                    self.psF[0:64, ci * 512:(ci + 1) * 512])
            return ins
        ph.inc("vector", ve, cp)
        ph.dma("gpsimd", dlw, self.wsm[0:64, 0:D], self.a2[:, :])
        ph.wait("tensor", dlw)
        ph.wait("tensor", ve)
        for mt in range(NPT):
            ph.wait("tensor", ac)

            def mm2(t, mt=mt):
                ins = None
                for ci in range(4):
                    ins = t.matmul(self.bank(ci),
                                   self.wsm[0:64, mt * 128:(mt + 1) * 128],
                                   self.smallb[:, ci * 512:(ci + 1) * 512],
                                   start=True, stop=True)
                return ins
            ph.inc("tensor", te, mm2)
            ph.wait("scalar", te)
            ph.wait("scalar", dlo, self._spill_guard)

            def sig(s, mt=mt):
                ins = None
                for ci in range(4):
                    ins = s.activation(self.stg3[:, ci * 512:(ci + 1) * 512],
                                       self.bank(ci), AF.Sigmoid,
                                       bias=self.vcol(mt, VC.a0))
                return ins
            ph.inc("scalar", ac, sig)
            ph.wait("sync", ac)
            ph.dma("sync", dlo, spill[mt * 128:(mt + 1) * 128, :], self.stg3[:])
            self._spill_guard = dlo.v

    def _kpost(self, ph, p):
        dli, dlo, te, ve, ac = (self.sems["dli"], self.sems["dlo"], self.sems["te"],
                                self.sems["ve"], self.sems["ac"])
        sc = self.sc
        ph.wait("sync", dlo)
        for mt in range(NPT):
            ph.wait("sync", ve)
            ph.dma("sync", dli, self.stg[:], sc[p + "kraw"][mt * 128:(mt + 1) * 128, :])
            ph.wait("vector", dli)
            ph.wait("vector", te)

            def kk_sq(v, mt=mt):
                v.tensor_scalar(self.stg2[:], self.stg[:], self.vcol(mt, VC.k_k),
                                None, ALU.mult)
                return v.tensor_tensor(self.stg3[:], self.stg2[:], self.stg2[:],
                                       ALU.mult)
            ph.inc("vector", ve, kk_sq)
            ph.wait("tensor", ve)

            def summ(t, mt=mt):
                ins = None
                for ci in range(4):
                    ins = t.matmul(self.psF[0:32, ci * 512:(ci + 1) * 512],
                                   self.hind[:, mt * H:(mt + 1) * H],
                                   self.stg3[:, ci * 512:(ci + 1) * 512],
                                   start=(mt == 0), stop=(mt == NPT - 1))
                return ins
            ph.inc("tensor", te, summ)
        ph.wait("scalar", te)

        def inv(s):
            s.activation(self.f32a[0:32, 0:D], self.psF[0:32, 0:D], AF.Ln)
            return s.activation(self.f32b[0:32, 0:D], self.f32a[0:32, 0:D], AF.Exp,
                                scale=self.cst[0:32, 0:1])
        ph.inc("scalar", ac, inv)
        ph.wait("vector", ac)

        def invc(v):
            v.tensor_scalar(self.f32b[0:32, 0:D], self.f32b[0:32, 0:D], 1e12,
                            None, ALU.min)
            return v.tensor_copy(self.invnb[0:32, 0:D], self.f32b[0:32, 0:D])
        ph.inc("vector", ve, invc)
        for mt in range(NPT):
            ph.wait("sync", ve)
            ph.dma("sync", dli, self.stg[:], sc[p + "kraw"][mt * 128:(mt + 1) * 128, :])
            ph.dma("sync", dli, self.stg2[:], sc[p + "ag"][mt * 128:(mt + 1) * 128, :])
            ph.wait("tensor", dli)
            ph.wait("tensor", ve)

            def expd(t, mt=mt):
                ins = None
                for ci in range(4):
                    ins = t.matmul(self.bank(ci),
                                   self.eind[:, mt * 128:(mt + 1) * 128],
                                   self.invnb[0:32, ci * 512:(ci + 1) * 512],
                                   start=True, stop=True)
                return ins
            ph.inc("tensor", te, expd)
            ph.wait("vector", te)
            ph.wait("vector", dlo, self._spill_guard)

            def mkstreams(v, mt=mt):
                for ci in range(4):
                    pb = self.bank(ci)
                    sl = slice(ci * 512, (ci + 1) * 512)
                    kkv = self.f32a[:, 0:512]
                    v.tensor_scalar(kkv, self.stg[:, sl], self.vcol(mt, VC.k_k),
                                    None, ALU.mult)
                    t1 = self.f32b[:, 0:512]
                    v.tensor_tensor(t1, kkv, pb, ALU.mult)
                    v.tensor_scalar(self.stg3[:, sl], t1, -1.0, None, ALU.mult)
                    v.tensor_tensor(self.mx[:, sl], t1, self.stg2[:, sl], ALU.mult)
                    t2 = self.f32a[:, 512:1024]
                    v.tensor_scalar(t2, self.stg2[:, sl], self.vcol(mt, VC.k_a),
                                    self.dcol(mt, DV.one_m_ka), ALU.mult, ALU.add)
                    ins = v.tensor_tensor(self.mx[:, D + ci * 512: D + (ci + 1) * 512],
                                          self.stg[:, sl], t2, ALU.mult)
                return ins
            ph.inc("vector", ve, mkstreams)
            ph.wait("sync", ve)
            ph.dma("sync", dlo, sc[p + "a"][mt * 128:(mt + 1) * 128, :], self.stg3[:])
            ph.dma("sync", dlo, sc[p + "b"][mt * 128:(mt + 1) * 128, :], self.mx[:, 0:D])
            ph.dma("sync", dlo, sc[p + "k2"][mt * 128:(mt + 1) * 128, :],
                   self.mx[:, D:2 * D])
            self._spill_guard = dlo.v

    def _proj_v(self, ph, spill):
        dlw, dlo, te, ve = (self.sems["dlw"], self.sems["dlo"], self.sems["te"],
                            self.sems["ve"])
        for kt in range(NPT):
            ph.dma("gpsimd", dlw, self.wbig[:, kt * D:(kt + 1) * D],
                   self.wv[kt * 128:(kt + 1) * 128, :])
        vc_col, dv_col = MIX["v"]
        for tt in range(16):
            ph.wait("vector", te)

            def mixtt(v, tt=tt):
                for kt in range(NPT):
                    x0 = self.xln_t(kt, tt * 128, 128)
                    v.tensor_scalar(self.stg3[:, kt * 128:(kt + 1) * 128], x0,
                                    self.vcol(kt, vc_col), None, ALU.mult)
                ins = None
                for kt in range(NPT):
                    x1 = self.xln_t(kt, 1 + tt * 128, 128)
                    ins = v.scalar_tensor_tensor(self.stg[:, kt * 128:(kt + 1) * 128],
                                                 x1, self.dcol(kt, dv_col),
                                                 self.stg3[:, kt * 128:(kt + 1) * 128],
                                                 ALU.mult, ALU.add)
                return ins
            ph.inc("vector", ve, mixtt)
            if tt == 0 and "vmix0" in self.taps:
                ph.wait("sync", ve)
                ph.dma("sync", dlo, self.taps["vmix0"], self.stg[:])
                ph.wait("sync", dlo)
            ph.wait("tensor", ve)
            ph.wait("tensor", dlw)

            def mmv(t, tt=tt):
                ins = None
                for fq in range(4):
                    for kt in range(NPT):
                        ins = t.matmul(self.bank(fq),
                                       self.stg[:, kt * 128:(kt + 1) * 128],
                                       self.wbig[:, kt * D + fq * 512:
                                                  kt * D + (fq + 1) * 512],
                                       start=(kt == 0), stop=(kt == NPT - 1))
                return ins
            ph.inc("tensor", te, mmv)
            ph.wait("vector", te)
            ph.wait("vector", dlo, self._spill_guard)

            def evv(v, tt=tt):
                ins = None
                for fq in range(4):
                    ins = v.tensor_copy(self.stg2[:, fq * 512:(fq + 1) * 512],
                                        self.bank(fq))
                return ins
            ph.inc("vector", ve, evv)
            ph.wait("sync", ve)
            for hp in range(NHP):
                ph.dma("sync", dlo, spill[hp, tt * 128:(tt + 1) * 128, :],
                       self.stg2[:, hp * 128:(hp + 1) * 128])
            self._spill_guard = dlo.v

    def _beta(self, ph):
        dli, dlo, te, ve = (self.sems["dli"], self.sems["dlo"], self.sems["te"],
                            self.sems["ve"])
        ph.wait("sync", dlo)
        for mt in range(NPT):
            ph.wait("sync", ve)
            ph.dma("sync", dli, self.stg[:], self.sc["r"][mt * 128:(mt + 1) * 128, :])
            ph.dma("sync", dli, self.stg2[:], self.sc["k2"][mt * 128:(mt + 1) * 128, :])
            ph.wait("vector", dli)
            ph.wait("vector", te)

            def rr(v, mt=mt):
                v.tensor_tensor(self.stg3[:], self.stg[:], self.stg2[:], ALU.mult)
                return v.tensor_scalar(self.stg3[:], self.stg3[:],
                                       self.vcol(mt, VC.r_k), None, ALU.mult)
            ph.inc("vector", ve, rr)
            ph.wait("tensor", ve)

            def summ(t, mt=mt):
                ins = None
                for ci in range(4):
                    ins = t.matmul(self.psF[0:32, ci * 512:(ci + 1) * 512],
                                   self.hind[:, mt * H:(mt + 1) * H],
                                   self.stg3[:, ci * 512:(ci + 1) * 512],
                                   start=(mt == 0), stop=(mt == NPT - 1))
                return ins
            ph.inc("tensor", te, summ)
        ph.wait("vector", te)

        def bev(v):
            return v.tensor_copy(self.f32a[0:32, 0:D], self.psF[0:32, 0:D])
        ph.inc("vector", ve, bev)
        ph.wait("sync", ve)
        ph.dma("sync", dlo, self.sc["beta"][:, :], self.f32a[0:32, 0:D])

    def _chain_g(self, ph):
        dlw, dlo, te, ve, ac = (self.sems["dlw"], self.sems["dlo"], self.sems["te"],
                                self.sems["ve"], self.sems["ac"])
        self._mix_full(ph, "g")
        for kt in range(NPT):
            ph.dma("gpsimd", dlw, self.wq[:, kt * 128:(kt + 1) * 128],
                   self.g1[kt * 128:(kt + 1) * 128, :])
        ph.wait("tensor", dlw)
        ph.wait("tensor", ve)

        def mm1(t):
            ins = None
            for kt in range(NPT):
                for ci in range(4):
                    ins = t.matmul(self.bank(ci),
                                   self.wq[:, kt * 128:(kt + 1) * 128],
                                   self.mx_t(kt, ci * 512, 512),
                                   start=(kt == 0), stop=(kt == NPT - 1))
            return ins
        ph.inc("tensor", te, mm1)
        ph.wait("scalar", te)

        def sig(s):
            ins = None
            for ci in range(4):
                ins = s.activation(self.stg[:, ci * 512:(ci + 1) * 512],
                                   self.bank(ci), AF.Sigmoid)
            return ins
        ph.inc("scalar", ac, sig)
        ph.dma("gpsimd", dlw, self.wsm[:, 0:D], self.g2[:, :])
        ph.wait("tensor", dlw)
        for mt in range(NPT):
            ph.wait("tensor", ac)
            ph.wait("tensor", ve)

            def mm2(t, mt=mt):
                ins = None
                for ci in range(4):
                    ins = t.matmul(self.bank(ci),
                                   self.wsm[:, mt * 128:(mt + 1) * 128],
                                   self.stg[:, ci * 512:(ci + 1) * 512],
                                   start=True, stop=True)
                return ins
            ph.inc("tensor", te, mm2)
            ph.wait("vector", te)
            ph.wait("vector", dlo, self._spill_guard)

            def ev(v, mt=mt):
                ins = None
                for ci in range(4):
                    ins = v.tensor_copy(self.stg3[:, ci * 512:(ci + 1) * 512],
                                        self.bank(ci))
                return ins
            ph.inc("vector", ve, ev)
            ph.wait("sync", ve)
            ph.dma("sync", dlo, self.sc["g"][mt * 128:(mt + 1) * 128, :], self.stg3[:])
            self._spill_guard = dlo.v



def _invm_i32(inv):
    arr = np.ascontiguousarray(np.asarray(inv)).reshape(-1).view(np.int32)
    out = np.zeros((1, 2 * E), np.int32)
    if arr.size == E:
        out[0, 0::2] = arr
    else:
        out[0, :] = arr
    return out


def host_inputs(inputs):
    import ml_dtypes
    x = np.asarray(inputs["L0_attn_x"])
    zrow = np.zeros((1, D), dtype=np.float32)

    def tokslab(b, t0):
        if t0 == 0:
            return np.concatenate([zrow, np.asarray(x[b, 0:T])], axis=0)
        return np.asarray(x[b, t0 - 1:t0 + T])

    common = {
        "wr": np.ascontiguousarray(inputs["L0_attn_W_r"], np.float32),
        "wk": np.ascontiguousarray(inputs["L0_attn_W_k"], np.float32),
        "wv": np.ascontiguousarray(inputs["L0_attn_W_v"], np.float32),
        "wo": np.ascontiguousarray(inputs["L0_attn_W_o"], np.float32),
        "w1": np.ascontiguousarray(inputs["L0_attn_w1"], np.float32),
        "w2": np.ascontiguousarray(inputs["L0_attn_w2"], np.float32),
        "a1": np.ascontiguousarray(inputs["L0_attn_a1"], np.float32),
        "a2": np.ascontiguousarray(inputs["L0_attn_a2"], np.float32),
        "g1": np.ascontiguousarray(inputs["L0_attn_g1"], np.float32),
        "g2": np.ascontiguousarray(inputs["L0_attn_g2"], np.float32),
        "gwT": np.ascontiguousarray(np.asarray(inputs["L0_ffn_gate_w"]).T, np.float32),
        "muskT": np.ascontiguousarray(np.transpose(np.asarray(inputs["L0_ffn_mask_up_w"]), (0, 2, 1)), np.float32),
        "gpw": np.ascontiguousarray(np.transpose(np.asarray(inputs["L0_ffn_gate_proj_w"]), (0, 2, 1)), np.float32),
        "upw": np.ascontiguousarray(np.transpose(np.asarray(inputs["L0_ffn_up_proj_w"]), (0, 2, 1)), np.float32),
        "dpw": np.ascontiguousarray(np.transpose(np.asarray(inputs["L0_ffn_down_proj_w"]), (0, 2, 1)), np.float32),
        "invm": _invm_i32(inputs["L0_ffn_inv_map"]),
    }
    vecs = np.zeros((D, NVEC), dtype=np.float32)
    for col, key in [(VC.x_r, "L0_attn_x_r"), (VC.x_w, "L0_attn_x_w"),
                     (VC.x_k, "L0_attn_x_k"), (VC.x_v, "L0_attn_x_v"),
                     (VC.x_a, "L0_attn_x_a"), (VC.x_g, "L0_attn_x_g"),
                     (VC.w0, "L0_attn_w0"), (VC.a0, "L0_attn_a0"),
                     (VC.k_k, "L0_attn_k_k"), (VC.k_a, "L0_attn_k_a")]:
        vecs[:, col] = np.asarray(inputs[key]).reshape(D)
    vecs[:, VC.r_k] = np.asarray(inputs["L0_attn_r_k"]).reshape(D)
    vecs[:, VC.gn_s] = np.asarray(inputs["L0_attn_gn_scale"]).reshape(D)
    vecs[:, VC.gn_o] = np.asarray(inputs["L0_attn_gn_offset"]).reshape(D)
    vecs[:, VC.negw0] = -vecs[:, VC.w0]
    vecs[:, VC.neghalf] = -0.5
    vecs[:, VC.one] = 1.0
    common["vecs"] = vecs

    eye = np.eye(128, dtype=np.float32)
    common["ident32"] = eye
    common["identbf"] = eye.astype(ml_dtypes.bfloat16)

    def dual(m):
        z = np.zeros((128, 128), np.float32)
        z[:N, :N] = m
        z[N:, N:] = m
        return z
    tril_s = np.tril(np.ones((N, N), np.float32), -1)
    tril_i = np.tril(np.ones((N, N), np.float32))
    d16 = np.zeros((N, N), np.float32)
    for i_ in range(0, N, 16):
        d16[i_:i_ + 16, i_:i_ + 16] = 1.0
    d16s = d16 * tril_s
    eye = np.eye(N, dtype=np.float32)
    qab = lambda m: np.concatenate([np.tile(m, (2, 1)), np.tile(m.T, (2, 1))], axis=1)
    masks = np.concatenate(
        [qab(tril_s), qab(d16s), np.tile(tril_s.T, (2, 2)), np.tile(tril_i.T, (2, 2)),
         np.concatenate([np.tile(eye, (2, 1)), np.tile(eye, (2, 1))], axis=1),
         dual(eye), np.ones((128, 128), np.float32)],
        axis=1)
    common["dmasks"] = masks
    common["dmasksbf"] = masks.astype(ml_dtypes.bfloat16)
    seg = np.ones((128, T), np.float32)
    seg[:, 0::L] = 0.0
    common["segm"] = seg
    hind = np.zeros((NPT, 128, H), np.float32)
    eind = np.zeros((NPT, H, 128), np.float32)
    for pt in range(NPT):
        for j in range(128):
            h = (pt * 128 + j) // N
            hind[pt, j, h] = 1.0
            eind[pt, h, j] = 1.0
    common["hind"] = hind.astype(ml_dtypes.bfloat16)
    common["eind"] = eind.astype(ml_dtypes.bfloat16)
    gi = np.zeros((E, G), np.float32)
    ge = np.zeros((G, E), np.float32)
    for x_ in range(E):
        gi[x_, x_ // GS] = 1.0
        ge[x_ // GS, x_] = 1.0
    common["gind"] = gi
    common["gexp"] = ge
    common["ones1"] = np.ones((1, 128), np.float32)
    common["iota"] = np.arange(128, dtype=np.float32).reshape(128, 1)

    in_maps = []
    for c in range(8):
        b_, p = c // 2, c % 2
        m = dict(common)
        m["xm"] = np.ascontiguousarray(tokslab(b_, p * T), dtype=np.float32)
        m["xp"] = (np.ascontiguousarray(tokslab(b_, 0), dtype=np.float32)
                   if p == 1 else np.zeros((TL, D), np.float32))
        in_maps.append(m)
    return in_maps


def kernel(**inputs):
    in_maps = host_inputs(inputs)
    bld = Builder(DEBUG_TAPS)
    bld.build()
    res = run_bass_kernel_spmd(bld.nc, in_maps, core_ids=list(range(8)))
    outs = [res.results[c]["out"] for c in range(8)]
    full = np.stack([np.concatenate([outs[2 * b], outs[2 * b + 1]], axis=0)
                     for b in range(4)], axis=0)
    return full
